# version 04
# baseline (speedup 1.0000x reference)
import dataclasses
from contextlib import ExitStack

import numpy as np

import concourse.mybir as mybir
from concourse.bacc import Bacc
from concourse.bass import IndirectOffsetOnAxis
from concourse.bass_utils import run_bass_kernel_spmd

F32 = mybir.dt.float32
I32 = mybir.dt.int32
AF = mybir.ActivationFunctionType
ALU = mybir.AluOpType
AX = mybir.AxisListType

S = 2048
H = 1024
HC = H // 128
ST = S // 128
NCORE = 8
E = 8
FF = 2048
F4 = 512
CAP = 4096
PAD = 768
RC2 = 256
NH = 16
HD = 64

ENGS = ("sync", "scalar", "vector", "tensor", "gpsimd")


class Rec:
    def __init__(self):
        self.q = {e: [] for e in ENGS}
        self.cnt = {}

    def c(self, sem):
        return self.cnt.get(sem, 0)

    def op(self, eng, fn, inc=None):
        if inc is not None and inc[0].startswith("d"):
            self.wait(eng, inc[0])
        ent = [fn, None, None]
        self.q[eng].append(ent)
        if inc is not None:
            self.inc(eng, inc[0], inc[1])
        return ent

    def inc(self, eng, sem, n=1):
        ent = self.q[eng][-1]
        assert ent[0] is not None and ent[1] is None
        ent[1] = (sem, n)
        self.cnt[sem] = self.cnt.get(sem, 0) + n

    def wait(self, eng, sem, val=None):
        if val is None:
            val = self.cnt.get(sem, 0)
        if val <= 0:
            return
        self.q[eng].append([None, None, (sem, val)])

    def edge(self, eng):
        self.inc(eng, "e_" + eng, 1)

    def join(self, eng, *others):
        for o in others:
            self.wait(eng, "e_" + o)

    def replay(self, eng, handle, sems):
        water = {}
        own = "e_" + eng
        own_cnt = 0
        for fn, inc, wt in self.q[eng]:
            if wt is not None:
                s, v = wt
                if water.get(s, 0) < v:
                    handle.wait_ge(sems[s], v)
                    water[s] = v
            if fn is not None:
                if water.get(own, 0) < own_cnt:
                    handle.wait_ge(sems[own], own_cnt)
                    water[own] = own_cnt
                r = fn(handle)
                if inc is not None:
                    r.then_inc(sems[inc[0]], inc[1])
                    if inc[0] == own:
                        own_cnt += inc[1]


FFN_SECTIONS = ("prep", "router", "exchange", "pos", "zero", "experts",
                "fallback", "combine")
LAST_REC = None


def build_nc(dev=5):
    nc = Bacc("TRN2")

    def dram_in(name, shape, dt=F32):
        return nc.dram_tensor(name, list(shape), dt, kind="ExternalInput").ap()

    x_in = dram_in("x_fm", [128, HC, S])
    rankv = dram_in("rankv", [1], I32)
    maskin = dram_in("maskrow", [128, 64])
    ident_in = dram_in("ident", [128, 128])
    wqkvT_a = dram_in("wqkvT_a", [H, 3 * H])
    wprojT_a = dram_in("wprojT_a", [H, H])
    bproj_a = dram_in("bproj_a", [128, HC])
    wqkvT_b = dram_in("wqkvT_b", [H, 3 * H])
    bqkv_b = dram_in("bqkv_b", [128, 3 * HC])
    woutT_b = dram_in("woutT_b", [H, H])
    bout_b = dram_in("bout_b", [128, HC])
    wr_in = dram_in("wr", [H, E])
    normw_in = dram_in("normw", [E, 128, HC])
    wg_in = dram_in("wg", [E, H, FF])
    wu_in = dram_in("wu", [E, H, FF])
    wd_in = dram_in("wd", [E, FF, H])
    fnorm_in = dram_in("fnorm", [128, HC])
    fwg_in = dram_in("fwg", [H, F4])
    fwu_in = dram_in("fwu", [H, F4])
    fwd_in = dram_in("fwd", [F4, H])

    y_fm = nc.dram_tensor("y_fm", [128, HC, S], F32, kind="ExternalOutput").ap()

    def scratch(name, shape, dt=F32):
        return nc.dram_tensor(name, list(shape), dt, kind="Internal").ap()

    r_spill = scratch("r_spill", [128, HC, S])
    v_spill = scratch("v_spill", [128, ST, H])
    q_spill = scratch("q_spill", [128, HC, S])
    a_spill = scratch("a_spill", [128, HC, S])
    xtm_spill = scratch("xtm_spill", [S, H])
    xr_spill = scratch("xr_spill", [S, H])
    pairs_d = scratch("pairs_d", [E * S, 2])
    sY_d = scratch("sY_d", [S, H])

    rec = Rec()
    global LAST_REC
    LAST_REC = rec
    es = ExitStack()
    with es:
        def sb(name, p, n, dt=F32):
            return es.enter_context(nc.sbuf_tensor(name, [p, n], dt))

        A = sb("A", 128, 16384)
        B = sb("B", 128, 16384)
        C = sb("C", 128, 8192)
        D = sb("D", 128, 8192)
        identt = sb("identt", 128, 128)
        smalls = sb("smalls", 128, 2048)
        zeroK = sb("zeroK", 128, 1024)
        wvec = sb("wvec", 128, 96)
        wrt = sb("wrt", 128, 64)
        iot8 = sb("iot8", 128, 8)
        cntt = sb("cntt", 128, 8)
        tblt = [sb("tbl0", 128, 64), sb("tbl1", 128, 64)]
        maskt = sb("maskt", 128, 64)
        stage16 = sb("stage16", 16, 128)
        onesr = sb("onesr", 1, 128)
        idxt = sb("idxt", 128, 64, I32)
        sidxc = sb("sidxc", 128, 16, I32)

        psb = [es.enter_context(nc.psum_tensor(f"ps{i}", [128, 512], F32))
               for i in range(8)]

        semnames = ["e_sync", "e_scalar", "e_vector", "e_tensor", "e_gpsimd",
                    "dW", "dA", "dG", "dS", "prep", "rsem", "lsem",
                    "rsem2", "lsem2"]
        sems = {n: es.enter_context(nc.semaphore(n)) for n in semnames}
        block = es.enter_context(nc.Block())

        def E_(eng, fn, inc=None):
            rec.op(eng, fn, inc=inc)

        def mm(out_ap, lhsT, rhs, start, stop, tp=None):
            rec.op("tensor", lambda e, o=out_ap, l=lhsT, r=rhs, st=start, sp=stop,
                   tp=tp: e.matmul(o, l, r, start=st, stop=sp, tile_position=tp))

        def tr(out_ap, in_ap):
            p = in_ap.shape[0]
            rec.op("tensor", lambda e, o=out_ap, i=in_ap, p=p:
                   e.transpose(o, i, identt[:p, :p]))

        def fmv(t, off=0):
            return t[:, off:off + HC * S].rearrange("p (c s) -> p c s", c=HC)

        A_fm = fmv(A)
        B_fm = fmv(B)

        def phase():
            engs = ("tensor", "vector", "scalar", "gpsimd", "sync")
            for a in engs:
                rec.join(a, *[o for o in engs if o != a])
                for dsem in ("dW", "dA", "dG", "dS"):
                    rec.wait(a, dsem)

        _svar = {}

        def sv(name, w=16):
            if name not in _svar:
                base = sum(ww for _, ww in _svar.values())
                assert base + w <= 2048, "smalls overflow"
                _svar[name] = (base, w)
            o, ww = _svar[name]
            assert ww == w, name
            return smalls[:, o:o + w]

        def svc(name, t, w=16):
            return sv(name, w)[:, t:t + 1]

        E_("sync", lambda e: e.dma_start(out=identt[:, :], in_=ident_in), inc=("dW", 16))
        E_("sync", lambda e: e.dma_start(out=maskt[:, :], in_=maskin), inc=("dW", 16))
        E_("sync", lambda e: e.dma_start(out=A_fm, in_=x_in), inc=("dW", 16))
        E_("sync", lambda e: e.dma_start(
            out=wrt[:, :].rearrange("p (c e) -> p c e", c=HC),
            in_=wr_in.rearrange("(c p) e -> p c e", p=128)), inc=("dW", 16))
        E_("sync", lambda e: e.dma_start(out=idxt[:1, 62:63], in_=rankv[None, :]),
           inc=("dW", 16))
        E_("vector", lambda e: e.memset(zeroK[:, :], 0.0))
        E_("vector", lambda e: e.memset(smalls[:, :], 0.0))
        E_("vector", lambda e: e.memset(cntt[:, :], 0.0))
        E_("vector", lambda e: e.memset(idxt[:, 0:62], 0))
        E_("vector", lambda e: e.memset(wvec[:, :], 0.0))
        E_("vector", lambda e: e.memset(onesr[:1, :], 1.0))
        rec.edge("vector")
        ones_col = wvec[:, 95:96]
        eps5_col = wvec[:, 94:95]
        eps6_col = wvec[:, 93:94]
        E_("vector", lambda e: e.memset(ones_col, 1.0))
        E_("vector", lambda e: e.memset(eps5_col, 1e-5))
        E_("vector", lambda e: e.memset(eps6_col, 1e-6))
        rec.edge("vector")
        E_("gpsimd", lambda e: e.iota(sidxc[:, :], pattern=[[1, 16]], base=0,
                                      channel_multiplier=16))
        E_("gpsimd", lambda e: e.iota(iot8[:, :], pattern=[[1, 8]], base=0,
                                      channel_multiplier=0,
                                      allow_small_or_imprecise_dtypes=True))
        rec.edge("gpsimd")
        rec.join("vector", "gpsimd")
        E_("vector", lambda e: e.tensor_copy(sv("sidxf"), sidxc[:, :]))
        rec.edge("vector")

        def bcast_tm_var(var_ap, slab):
            rowC = C[0:1, 6144:8192]
            rec.join("tensor", "vector", "scalar")
            tr(psb[6][:16, 0:128], var_ap)
            rec.edge("tensor")
            rec.join("vector", "tensor")
            E_("vector", lambda e: e.tensor_copy(stage16[:, :], psb[6][:16, 0:128]))
            rec.edge("vector")
            rec.join("tensor", "vector")
            for r4 in range(4):
                for j in range(4):
                    c = r4 * 4 + j
                    mm(psb[7][0:1, j * 128:(j + 1) * 128],
                       identt[0:16, c:c + 1], stage16[:, :],
                       start=True, stop=True)
                rec.edge("tensor")
                rec.join("vector", "tensor")
                E_("vector", lambda e, r4=r4: e.tensor_copy(
                    rowC[:, r4 * 512:(r4 + 1) * 512], psb[7][0:1, 0:512]))
                rec.edge("vector")
                rec.join("tensor", "vector")
            for b4 in range(4):
                mm(psb[7][:, :], onesr[:1, :], rowC[:, b4 * 512:(b4 + 1) * 512],
                   start=True, stop=True)
                rec.edge("tensor")
                rec.join("vector", "tensor")
                E_("vector", lambda e, b4=b4: e.tensor_copy(
                    slab[:, b4 * 512:(b4 + 1) * 512], psb[7][:, :]))
                rec.edge("vector")
                rec.join("tensor", "vector")

        def layernorm(src_fm, dst_fm, eps, rms_only=False, apply=True):
            phase()
            x2 = C[:, 0:1024]
            a_slab = C[:, 2048:4096]
            b_slab = C[:, 4096:6144]
            for rnd in range(4):
                for hc in range(HC):
                    xb = x2[:, (hc % 2) * 512:(hc % 2) * 512 + 512]
                    rec.join("scalar", "tensor")
                    E_("scalar", lambda e, hc=hc, rnd=rnd, xb=xb: e.activation(
                        xb, src_fm[:, hc, rnd * 512:rnd * 512 + 512], AF.Square))
                    rec.edge("scalar")
                    rec.join("tensor", "scalar")
                    for i in range(4):
                        t = rnd * 4 + i
                        if not rms_only:
                            mm(psb[i][:, 0:1],
                               src_fm[:, hc, t * 128:(t + 1) * 128],
                               ones_col, start=(hc == 0), stop=(hc == HC - 1))
                        mm(psb[4 + i][:, 0:1], xb[:, i * 128:(i + 1) * 128],
                           ones_col, start=(hc == 0), stop=(hc == HC - 1))
                    rec.edge("tensor")
                rec.join("vector", "tensor")
                for i in range(4):
                    t = rnd * 4 + i
                    if not rms_only:
                        E_("vector", lambda e, i=i, t=t: e.tensor_copy(
                            svc("lsum", t), psb[i][:, 0:1]))
                    E_("vector", lambda e, i=i, t=t: e.tensor_copy(
                        svc("lsq", t), psb[4 + i][:, 0:1]))
                rec.edge("vector")
                rec.join("tensor", "vector")
            if rms_only:
                rec.join("scalar", "vector")
                E_("scalar", lambda e, eps=eps: e.activation(
                    sv("sd"), sv("lsq"), AF.Sqrt,
                    bias=eps5_col if eps == 1e-5 else eps6_col, scale=1.0 / H))
                rec.edge("scalar")
                rec.join("vector", "scalar")
                E_("vector", lambda e: e.reciprocal(sv("a"), sv("sd")))
                rec.edge("vector")
            else:
                E_("vector", lambda e: e.tensor_scalar_mul(sv("mean"), sv("lsum"),
                                                           1.0 / H))
                rec.edge("vector")
                E_("vector", lambda e: e.tensor_tensor(sv("m2"), sv("mean"),
                                                       sv("mean"), ALU.mult))
                rec.edge("vector")
                E_("vector", lambda e: e.scalar_tensor_tensor(
                    sv("var"), sv("lsq"), 1.0 / H, sv("m2"), ALU.mult, ALU.subtract))
                rec.edge("vector")
                rec.join("scalar", "vector")
                E_("scalar", lambda e, eps=eps: e.activation(
                    sv("sd"), sv("var"), AF.Sqrt,
                    bias=eps5_col if eps == 1e-5 else eps6_col))
                rec.edge("scalar")
                rec.join("vector", "scalar")
                E_("vector", lambda e: e.reciprocal(sv("a"), sv("sd")))
                rec.edge("vector")
                E_("vector", lambda e: e.scalar_tensor_tensor(
                    sv("b"), sv("mean"), -1.0, sv("a"), ALU.mult, ALU.mult))
                rec.edge("vector")
            if not apply:
                return
            bcast_tm_var(sv("a"), a_slab)
            if not rms_only:
                bcast_tm_var(sv("b"), b_slab)
            rec.join("vector", "tensor")
            for hc in range(HC):
                E_("vector", lambda e, hc=hc: e.tensor_tensor(
                    dst_fm[:, hc, :], src_fm[:, hc, :], a_slab, ALU.mult))
                rec.edge("vector")
                if not rms_only:
                    E_("vector", lambda e, hc=hc: e.tensor_tensor(
                        dst_fm[:, hc, :], dst_fm[:, hc, :], b_slab, ALU.add))
                    rec.edge("vector")

        def qkv_fm_phase(wsrc, col0, dst_fm_view, spill, bias_src=None, bias_cols=None):
            wq = C[:, 0:8192].rearrange("p (c f) -> p c f", c=HC)
            rec.join("sync", "tensor", "vector")
            for hc in range(HC):
                E_("sync", lambda e, hc=hc: e.dma_start(
                    out=wq[:, hc, :], in_=wsrc[hc * 128:(hc + 1) * 128, col0:col0 + H]),
                   inc=("dW", 16))
            rec.wait("tensor", "dW")
            stage = D[:, 0:4096]
            last_evac = 0
            for qfc in range(HC):
                if last_evac:
                    rec.wait("tensor", "e_vector", last_evac)
                for hc in range(HC):
                    for sc in range(4):
                        mm(psb[sc][:, :], wq[:, hc, qfc * 128:(qfc + 1) * 128],
                           B_fm[:, hc, sc * 512:(sc + 1) * 512],
                           start=(hc == 0), stop=(hc == HC - 1))
                rec.edge("tensor")
                rec.join("vector", "tensor")
                if spill is not None and qfc >= 2:
                    rec.wait("vector", "dA")
                sg = stage[:, (qfc % 2) * 2048:(qfc % 2) * 2048 + 2048]
                for sc in range(4):
                    dst = (sg[:, sc * 512:(sc + 1) * 512] if spill is not None
                           else dst_fm_view[:, qfc, sc * 512:(sc + 1) * 512])
                    if bias_src is not None:
                        E_("vector", lambda e, dst=dst, sc=sc, qfc=qfc:
                           e.tensor_scalar_add(dst, psb[sc][:, :],
                                               wvec[:, bias_cols + qfc:
                                                    bias_cols + qfc + 1]))
                    else:
                        E_("vector", lambda e, dst=dst, sc=sc: e.tensor_copy(
                            dst, psb[sc][:, :]))
                rec.edge("vector")
                last_evac = rec.c("e_vector")
                if spill is not None:
                    rec.join("scalar", "vector")
                    E_("scalar", lambda e, sg=sg, qfc=qfc: e.dma_start(
                        out=spill[:, qfc, :], in_=sg), inc=("dA", 16))

        def load_bias_cols(src_dram, cols, n=HC):
            E_("sync", lambda e: e.dma_start(
                out=wvec[:, cols:cols + n], in_=src_dram), inc=("dW", 16))

        def fm_to_tm_spill(src_fm, spill, psum_pair, tile_ap=None):
            b0, b1 = psum_pair
            if tile_ap is None:
                tile_ap = lambda tt: spill.rearrange("(p b) h -> p b h",
                                                     p=128)[:, tt, :]
            tmb = D[:, 4096:8192]
            for tt in range(ST):
                tb = tmb[:, (tt % 2) * 2048:(tt % 2) * 2048 + 1024]
                rec.join("tensor", "vector")
                for hc in range(HC):
                    bank = b0 if hc < 4 else b1
                    tr(psb[bank][:, (hc % 4) * 128:(hc % 4) * 128 + 128],
                       src_fm[:, hc, tt * 128:(tt + 1) * 128])
                rec.edge("tensor")
                rec.join("vector", "tensor")
                if tt >= 2:
                    rec.wait("vector", "dA")
                E_("vector", lambda e, tb=tb, b0=b0: e.tensor_copy(
                    tb[:, 0:512], psb[b0][:, :]))
                E_("vector", lambda e, tb=tb, b1=b1: e.tensor_copy(
                    tb[:, 512:1024], psb[b1][:, :]))
                rec.edge("vector")
                rec.join("scalar", "vector")
                E_("scalar", lambda e, tb=tb, tt=tt: e.dma_start(
                    out=tile_ap(tt), in_=tb), inc=("dA", 16))

        def attn_a():
            rec.join("scalar", "vector", "tensor")
            rec.wait("scalar", "dW")
            E_("scalar", lambda e: e.dma_start(out=r_spill, in_=A_fm), inc=("dA", 16))
            rspill_at = rec.c("dA")
            layernorm(A_fm, B_fm, 1e-5)

            phase()
            wv = C[:, 0:8192].rearrange("p (c f) -> p c f", c=HC)
            rec.join("sync", "vector", "tensor")
            for hc in range(HC):
                E_("sync", lambda e, hc=hc: e.dma_start(
                    out=wv[:, hc, :],
                    in_=wqkvT_a[hc * 128:(hc + 1) * 128, 2 * H:3 * H]), inc=("dW", 16))
            vbuf = D[:, 0:2048]
            rec.wait("tensor", "dW")
            evac_at = [0, 0]
            for st in range(ST):
                b0 = (st % 2) * 2
                if evac_at[st % 2]:
                    rec.wait("tensor", "e_vector", evac_at[st % 2])
                for hc in range(HC):
                    mm(psb[b0][:, :], B_fm[:, hc, st * 128:(st + 1) * 128],
                       wv[:, hc, 0:512], start=(hc == 0), stop=(hc == HC - 1))
                    mm(psb[b0 + 1][:, :], B_fm[:, hc, st * 128:(st + 1) * 128],
                       wv[:, hc, 512:1024], start=(hc == 0), stop=(hc == HC - 1))
                rec.edge("tensor")
                rec.join("vector", "tensor")
                if st >= 2:
                    rec.wait("vector", "dA")
                vb = vbuf[:, (st % 2) * 1024:(st % 2) * 1024 + 1024]
                E_("vector", lambda e, vb=vb, b0=b0: e.tensor_copy(vb[:, 0:512],
                                                                   psb[b0][:, :]))
                E_("vector", lambda e, vb=vb, b0=b0: e.tensor_copy(vb[:, 512:1024],
                                                                   psb[b0 + 1][:, :]))
                rec.edge("vector")
                evac_at[st % 2] = rec.c("e_vector")
                rec.join("scalar", "vector")
                E_("scalar", lambda e, vb=vb, st=st: e.dma_start(
                    out=v_spill[:, st, :], in_=vb), inc=("dA", 16))

            phase()
            qkv_fm_phase(wqkvT_a, 0, None, q_spill)
            phase()
            qkv_fm_phase(wqkvT_a, H, A_fm, None)
            phase()

            phase()
            scores = B[:, 0:8192]
            PT = B[:, 8192:16384].rearrange("p (k q) -> p k q", k=ST)
            qbuf = D[:, 0:8192]
            vstream = C[:, 0:4096]
            aout = C[:, 4096:8192]
            qload_at = [0, 0]
            rec.join("sync", "vector", "tensor")
            for qb in range(2):
                qbv = qbuf[:, (qb % 2) * 4096:(qb % 2) * 4096 + 4096]
                E_("sync", lambda e, qb=qb, qbv=qbv: e.dma_start(
                    out=qbv.rearrange("p (c s) -> p c s", c=HC),
                    in_=q_spill[:, :, qb * 512:(qb + 1) * 512]), inc=("dW", 16))
                qload_at[qb % 2] = rec.c("dW")
            sc_evac = [0, 0, 0, 0]
            pt_evac = [0, 0]
            pv_evac = [0, 0]
            for qb in range(4):
                qbv = qbuf[:, (qb % 2) * 4096:(qb % 2) * 4096 + 4096].rearrange(
                    "p (c s) -> p c s", c=HC)
                rec.wait("tensor", "dW", qload_at[qb % 2])
                for qt in range(4):
                    for kc in range(4):
                        bank = kc
                        if sc_evac[bank]:
                            rec.wait("tensor", "e_vector", sc_evac[bank])
                        for hc in range(HC):
                            mm(psb[bank][:, :], qbv[:, hc, qt * 128:(qt + 1) * 128],
                               A_fm[:, hc, kc * 512:(kc + 1) * 512],
                               start=(hc == 0), stop=(hc == HC - 1))
                        rec.edge("tensor")
                        rec.join("vector", "tensor")
                        E_("vector", lambda e, qt=qt, kc=kc, bank=bank: e.tensor_copy(
                            scores[:, qt * 2048 + kc * 512:qt * 2048 + kc * 512 + 512],
                            psb[bank][:, :]))
                        rec.edge("vector")
                        sc_evac[bank] = rec.c("e_vector")
                for qt in range(4):
                    sl = scores[:, qt * 2048:(qt + 1) * 2048]
                    E_("vector", lambda e, sl=sl, qt=qt: e.reduce_max(
                        svc("negmax", qt), sl, axis=AX.X, negate=True))
                    rec.edge("vector")
                    E_("vector", lambda e, qt=qt: e.tensor_scalar_mul(
                        svc("negmax", qt), svc("negmax", qt), 0.125))
                    rec.edge("vector")
                    rec.join("scalar", "vector")
                    E_("scalar", lambda e, sl=sl, qt=qt: e.activation(
                        sl, sl, AF.Exp, bias=svc("negmax", qt), scale=0.125,
                        accum_out=svc("ssum", qt)))
                    rec.edge("scalar")
                    rec.join("vector", "scalar")
                    E_("vector", lambda e, qt=qt: e.reciprocal(
                        svc("srec", qt), svc("ssum", qt)))
                    rec.edge("vector")
                    E_("vector", lambda e, sl=sl, qt=qt: e.tensor_scalar_mul(
                        sl, sl, svc("srec", qt)))
                    rec.edge("vector")
                rec.join("tensor", "vector")
                for kt in range(ST):
                    bank = 4 + (kt % 2)
                    if pt_evac[kt % 2]:
                        rec.wait("tensor", "e_vector", pt_evac[kt % 2])
                    for qt in range(4):
                        tr(psb[bank][:, qt * 128:(qt + 1) * 128],
                           scores[:, qt * 2048 + kt * 128:qt * 2048 + kt * 128 + 128])
                    rec.edge("tensor")
                    rec.join("vector", "tensor")
                    E_("vector", lambda e, kt=kt, bank=bank: e.tensor_copy(
                        PT[:, kt, :], psb[bank][:, :]))
                    rec.edge("vector")
                    pt_evac[kt % 2] = rec.c("e_vector")
                vload_at = [0, 0]
                rec.join("sync", "tensor")
                for vfc in range(2):
                    vsb = vstream[:, (vfc % 2) * 2048:(vfc % 2) * 2048 + 2048]
                    E_("sync", lambda e, vsb=vsb, vfc=vfc: e.dma_start(
                        out=vsb.rearrange("p (k f) -> p k f", k=ST),
                        in_=v_spill[:, :, vfc * 128:(vfc + 1) * 128]), inc=("dW", 16))
                    vload_at[vfc % 2] = rec.c("dW")
                rec.join("tensor", "vector")
                rec.wait("vector", "dA")
                for vfc in range(HC):
                    bank = 6 + (vfc % 2)
                    vsb = vstream[:, (vfc % 2) * 2048:(vfc % 2) * 2048 + 2048].rearrange(
                        "p (k f) -> p k f", k=ST)
                    rec.wait("tensor", "dW", vload_at[vfc % 2])
                    if pv_evac[vfc % 2]:
                        rec.wait("tensor", "e_vector", pv_evac[vfc % 2])
                    for kt in range(ST):
                        mm(psb[bank][:, :], vsb[:, kt, :], PT[:, kt, :],
                           start=(kt == 0), stop=(kt == ST - 1))
                    rec.edge("tensor")
                    mm_done = rec.c("e_tensor")
                    if vfc + 2 < HC:
                        vfc2 = vfc + 2
                        vsb2 = vstream[:, (vfc2 % 2) * 2048:(vfc2 % 2) * 2048 + 2048]
                        rec.wait("sync", "e_tensor", mm_done)
                        E_("sync", lambda e, vsb2=vsb2, vfc2=vfc2: e.dma_start(
                            out=vsb2.rearrange("p (k f) -> p k f", k=ST),
                            in_=v_spill[:, :, vfc2 * 128:(vfc2 + 1) * 128]),
                           inc=("dW", 16))
                        vload_at[vfc2 % 2] = rec.c("dW")
                    rec.join("vector", "tensor")
                    E_("vector", lambda e, vfc=vfc, bank=bank: e.tensor_copy(
                        aout[:, vfc * 512:(vfc + 1) * 512], psb[bank][:, :]))
                    rec.edge("vector")
                    pv_evac[vfc % 2] = rec.c("e_vector")
                rec.join("scalar", "vector")
                E_("scalar", lambda e, qb=qb: e.dma_start(
                    out=a_spill[:, :, qb * 512:(qb + 1) * 512],
                    in_=aout.rearrange("p (c s) -> p c s", c=HC)), inc=("dA", 16))
                if qb + 2 < 4:
                    qb2 = qb + 2
                    qbv2 = qbuf[:, (qb2 % 2) * 4096:(qb2 % 2) * 4096 + 4096]
                    rec.wait("sync", "e_tensor")
                    E_("sync", lambda e, qb2=qb2, qbv2=qbv2: e.dma_start(
                        out=qbv2.rearrange("p (c s) -> p c s", c=HC),
                        in_=q_spill[:, :, qb2 * 512:(qb2 + 1) * 512]), inc=("dW", 16))
                    qload_at[qb2 % 2] = rec.c("dW")

            phase()
            wp = D[:, 0:8192].rearrange("p (c f) -> p c f", c=HC)
            rec.join("sync", "tensor", "vector")
            for jc in range(HC):
                E_("sync", lambda e, jc=jc: e.dma_start(
                    out=wp[:, jc, :], in_=wprojT_a[jc * 128:(jc + 1) * 128, :]),
                   inc=("dW", 16))
            load_bias_cols(bproj_a, 0)
            astream = C[:, 0:4096]
            rstream = C[:, 4096:8192]
            out2 = B[:, 0:2048].rearrange("p (c i) -> p c i", c=HC)
            tr_evac = [0, 0, 0, 0]
            pj_evac = [0, 0, 0, 0]
            rec.wait("sync", "dA", rspill_at)
            for c8 in range(HC):
                asb = astream[:, (c8 % 2) * 2048:(c8 % 2) * 2048 + 2048]
                rsb = rstream[:, (c8 % 2) * 2048:(c8 % 2) * 2048 + 2048]
                if c8 >= 2:
                    rec.wait("sync", "e_tensor")
                    rec.wait("sync", "e_vector")
                E_("sync", lambda e, asb=asb, c8=c8: e.dma_start(
                    out=asb, in_=a_spill[:, c8, :]), inc=("dW", 16))
                E_("sync", lambda e, rsb=rsb, c8=c8: e.dma_start(
                    out=rsb.rearrange("p (c i) -> p c i", c=HC),
                    in_=r_spill[:, :, c8 * 256:(c8 + 1) * 256]), inc=("dW", 16))
                aload = rec.c("dW")
                rec.wait("tensor", "dW", aload)
                rec.wait("tensor", "dW")
                for m in range(2):
                    for jb in range(HC):
                        bank = (m * 8 + jb) % 4
                        if tr_evac[bank]:
                            rec.wait("tensor", "e_vector", tr_evac[bank])
                        tr(psb[bank][:, 0:128],
                           asb[:, m * 1024 + jb * 128:m * 1024 + jb * 128 + 128])
                        rec.edge("tensor")
                        rec.join("vector", "tensor")
                        E_("vector", lambda e, m=m, jb=jb, bank=bank: e.tensor_copy(
                            out2[:, jb, m:256:2], psb[bank][:, 0:128]))
                        rec.edge("vector")
                        tr_evac[bank] = rec.c("e_vector")
                rsv = rsb.rearrange("p (c i) -> p c i", c=HC)
                rec.join("tensor", "vector")
                for ec in range(HC):
                    bank = 4 + (ec % 4)
                    if pj_evac[ec % 4]:
                        rec.wait("tensor", "e_vector", pj_evac[ec % 4])
                    for jc in range(HC):
                        mm(psb[bank][:, 0:256], wp[:, jc, ec * 128:(ec + 1) * 128],
                           out2[:, jc, :], start=(jc == 0), stop=(jc == HC - 1))
                    rec.edge("tensor")
                    rec.join("vector", "tensor")
                    E_("vector", lambda e, ec=ec, bank=bank, rsv=rsv, c8=c8:
                       e.scalar_tensor_tensor(
                           A_fm[:, ec, c8 * 256:(c8 + 1) * 256], psb[bank][:, 0:256],
                           wvec[:, ec:ec + 1], rsv[:, ec, :], ALU.add, ALU.add))
                    rec.edge("vector")
                    pj_evac[ec % 4] = rec.c("e_vector")

        def ffn(ci):
            SEC = FFN_SECTIONS
            rec.join("scalar", "vector", "tensor")
            E_("scalar", lambda e: e.dma_start(out=r_spill, in_=A_fm), inc=("dA", 16))
            rspill_at = rec.c("dA")
            layernorm(A_fm, B_fm, 1e-5)
            if "router" not in SEC:
                return
            rec.join("tensor", "vector")
            rt_evac = [0] * 8
            for tt in range(ST):
                bank = tt % 8
                if rt_evac[bank]:
                    rec.wait("tensor", "e_vector", rt_evac[bank])
                for hc in range(HC):
                    mm(psb[bank][:, 0:8], B_fm[:, hc, tt * 128:(tt + 1) * 128],
                       wrt[:, hc * 8:hc * 8 + 8], start=(hc == 0), stop=(hc == HC - 1))
                rec.edge("tensor")
                rec.join("vector", "tensor")
                E_("vector", lambda e, bank=bank: e.tensor_copy(sv("z8", 8),
                                                                psb[bank][:, 0:8]))
                rec.edge("vector")
                rt_evac[bank] = rec.c("e_vector")
                E_("vector", lambda e: e.reduce_max(svc("nm1", 0, 8), sv("z8", 8),
                                                    axis=AX.X, negate=True))
                rec.edge("vector")
                E_("vector", lambda e: e.tensor_single_scalar(
                    sv("zs", 8), sv("z8", 8), svc("nm1", 0, 8), ALU.add))
                rec.edge("vector")
                E_("vector", lambda e: e.tensor_single_scalar(
                    sv("oh", 8), sv("zs", 8), 0.0, ALU.is_equal))
                rec.edge("vector")
                E_("vector", lambda e: e.scalar_tensor_tensor(
                    sv("ti", 8), sv("oh", 8), -1000.0, iot8[:, :], ALU.mult, ALU.add))
                rec.edge("vector")
                E_("vector", lambda e: e.tensor_reduce(
                    svc("e1r", 0, 8), sv("ti", 8), axis=AX.X, op=ALU.min))
                rec.edge("vector")
                E_("vector", lambda e, tt=tt: e.tensor_scalar_add(
                    svc("e1", tt), svc("e1r", 0, 8), 1000.0))
                rec.edge("vector")
                E_("vector", lambda e, tt=tt: e.tensor_single_scalar(
                    sv_oh1(tt), iot8[:, :], svc("e1", tt), ALU.is_equal))
                rec.edge("vector")
                rec.join("scalar", "vector")
                E_("scalar", lambda e: e.activation(
                    sv("ez", 8), sv("zs", 8), AF.Exp, accum_out=svc("esum", 0, 8)))
                rec.edge("scalar")
                rec.join("vector", "scalar")
                E_("vector", lambda e, tt=tt: e.reciprocal(svc("w1", tt),
                                                           svc("esum", 0, 8)))
                rec.edge("vector")
                E_("vector", lambda e, tt=tt: e.scalar_tensor_tensor(
                    sv("z2", 8), sv_oh1(tt), -1000.0, sv("zs", 8), ALU.mult, ALU.add))
                rec.edge("vector")
                E_("vector", lambda e: e.reduce_max(svc("nm2", 0, 8), sv("z2", 8),
                                                    axis=AX.X, negate=True))
                rec.edge("vector")
                E_("vector", lambda e: e.tensor_single_scalar(
                    sv("z2s", 8), sv("z2", 8), svc("nm2", 0, 8), ALU.add))
                rec.edge("vector")
                E_("vector", lambda e: e.tensor_single_scalar(
                    sv("oh2r", 8), sv("z2s", 8), 0.0, ALU.is_equal))
                rec.edge("vector")
                E_("vector", lambda e: e.scalar_tensor_tensor(
                    sv("ti", 8), sv("oh2r", 8), -1000.0, iot8[:, :], ALU.mult,
                    ALU.add))
                rec.edge("vector")
                E_("vector", lambda e: e.tensor_reduce(
                    svc("e2r", 0, 8), sv("ti", 8), axis=AX.X, op=ALU.min))
                rec.edge("vector")
                E_("vector", lambda e, tt=tt: e.tensor_scalar_add(
                    svc("e2", tt), svc("e2r", 0, 8), 1000.0))
                rec.edge("vector")
                E_("vector", lambda e, tt=tt: e.tensor_single_scalar(
                    sv_oh2(tt), iot8[:, :], svc("e2", tt), ALU.is_equal))
                rec.edge("vector")
                rec.join("scalar", "vector")
                E_("scalar", lambda e, tt=tt: e.activation(
                    svc("w2e", tt), svc("nm2", 0, 8), AF.Exp, scale=-1.0))
                rec.edge("scalar")
                rec.join("vector", "scalar")
                E_("vector", lambda e, tt=tt: e.tensor_tensor(
                    svc("w2", tt), svc("w2e", tt), svc("w1", tt), ALU.mult))
                rec.edge("vector")
                E_("vector", lambda e, tt=tt: e.tensor_tensor(
                    sv("y8", 8), sv_oh1(tt), sv_oh2(tt), ALU.add))
                rec.edge("vector")
                rec.join("tensor", "vector")
                bank2 = 4 + (tt % 2)
                tr(psb[bank2][:8, 0:128], sv("y8", 8))
                rec.edge("tensor")
                rec.join("vector", "tensor")
                E_("vector", lambda e, tt=tt, bank2=bank2: e.tensor_copy(
                    D[0:8, tt * 128:(tt + 1) * 128], psb[bank2][:8, 0:128]))
                rec.edge("vector")

            if "exchange" not in SEC:
                return
            phase()
            y_em = D[0:8, 0:2048]
            P_incl = D[0:8, 2048:4096]
            E_("vector", lambda e: e.tensor_tensor_scan(
                P_incl, y_em, y_em, 0.0, ALU.add, ALU.bypass))
            rec.edge("vector")
            E_("vector", lambda e: e.tensor_tensor(y_em, P_incl, y_em, ALU.subtract))
            rec.edge("vector")
            rec.wait("vector", "lsem")
            rec.wait("vector", "lsem2")
            E_("vector", lambda e: e.tensor_copy(cntt[0:8, 0:1],
                                                 P_incl[:, 2047:2048]))
            rec.edge("vector")
            rec.join("gpsimd", "vector")
            rec.wait("gpsimd", "dW")
            tb = tblt[ci]
            rs = "rsem" if ci == 0 else "rsem2"
            ls = "lsem" if ci == 0 else "lsem2"
            E_("gpsimd", lambda e, rs=rs, ls=ls: e.remote_dma_broadcast(
                dataclasses.replace(tb[:, 0:8], offset=rankreg[0] * 8),
                cntt[:, :], remote_sem=sems[rs], local_sem=sems[ls],
                rdests=[(0, k) for k in range(8)]), inc=("prep", 1))
            rec.wait("gpsimd", "prep")
            E_("gpsimd", lambda e: e.trigger_dma(count=1))
            rec.cnt[rs] = rec.c(rs) + 16
            rec.cnt[ls] = rec.c(ls) + 16

            if "zero" not in SEC:
                return
            rec.join("gpsimd", "vector")
            E_("gpsimd", lambda e: e.dma_start(
                out=pairs_d.rearrange("(p n) two -> p n two", p=128),
                in_=zeroK[:, 0:256].rearrange("p (n two) -> p n two", two=2)),
               inc=("dG", 16))
            for k in range(16):
                E_("gpsimd", lambda e, k=k: e.dma_start(
                    out=sY_d[k * 128:(k + 1) * 128, :], in_=zeroK[:, :]),
                   inc=("dG", 16))

            layernorm(B_fm, None, 1e-6, rms_only=True, apply=False)
            rs_slab = C[:, 2048:4096]
            bcast_tm_var(sv("a"), rs_slab)
            rec.join("vector", "tensor")
            rec.wait("vector", "dA", rspill_at)
            for hc in range(HC):
                E_("vector", lambda e, hc=hc: e.tensor_tensor(
                    A_fm[:, hc, :], B_fm[:, hc, :], rs_slab, ALU.mult))
            rec.edge("vector")

            phase()
            fm_to_tm_spill(B_fm, xtm_spill, (0, 1))
            phase()
            fm_to_tm_spill(A_fm, xr_spill, (2, 3))
            phase()

            phase()
            if "pos" not in SEC:
                return
            for tt in range(ST):
                bank = tt % 2
                rec.join("tensor", "vector")
                tr(psb[bank][:, 0:8], D[0:8, tt * 128:(tt + 1) * 128])
                rec.edge("tensor")
                rec.join("vector", "tensor")
                E_("vector", lambda e, bank=bank: e.tensor_copy(sv("ptm", 8),
                                                                psb[bank][:, 0:8]))
                rec.edge("vector")
                for k, ohf in ((1, sv_oh1), (2, sv_oh2)):
                    E_("vector", lambda e, ohf=ohf, tt=tt: e.tensor_tensor(
                        sv("t8", 8), sv("ptm", 8), ohf(tt), ALU.mult))
                    rec.edge("vector")
                    E_("vector", lambda e, k=k, tt=tt: e.reduce_sum(
                        svc("pos%d" % k, tt), sv("t8", 8), axis=AX.X))
                    rec.edge("vector")
                    ev = svc("e1", tt) if k == 1 else svc("e2", tt)
                    E_("vector", lambda e, ev=ev, k=k, tt=tt: e.scalar_tensor_tensor(
                        sv("slot", 8)[:, 0:1], ev, float(S), svc("pos%d" % k, tt),
                        ALU.mult, ALU.add))
                    rec.edge("vector")
                    E_("vector", lambda e: e.tensor_copy(idxt[:, 0:1],
                                                         sv("slot", 8)[:, 0:1]))
                    rec.edge("vector")
                    E_("vector", lambda e, tt=tt: e.tensor_copy(
                        sv("pv", 8)[:, 0:1], svc("sidxf", tt)))
                    E_("vector", lambda e, k=k, tt=tt: e.tensor_copy(
                        sv("pv", 8)[:, 1:2], svc("w%d" % k, tt)))
                    rec.edge("vector")
                    rec.join("gpsimd", "vector")
                    rec.wait("gpsimd", "dG")
                    E_("gpsimd", lambda e: e.indirect_dma_start(
                        pairs_d, IndirectOffsetOnAxis(idxt[:, 0:1], axis=0),
                        sv("pv", 8)[:, 0:2], None), inc=("dG", 16))
                    rec.wait("vector", "dG")

            if "experts" not in SEC:
                return
            phase()
            Xe = B[:, 0:6144].rearrange("p (c r) -> p c r", c=HC)
            Ytm = B[:, 8192:9216]
            gbuf = B[:, 9216:11264]
            wgv = lambda b: C[:, b * 4096:b * 4096 + 2048].rearrange(
                "p (c f) -> p c f", c=HC)
            wuv = lambda b: C[:, b * 4096 + 2048:b * 4096 + 4096].rearrange(
                "p (c f) -> p c f", c=HC)
            wdv = lambda b: D[:, b * 2048:b * 2048 + 2048].rearrange(
                "p (f h) -> p f h", f=2)
            scat_prev = 0
            for ex in range(E):
                rec.join("sync", "vector")
                E_("sync", lambda e, ex=ex: e.dma_start(
                    out=wvec[:, 8:16], in_=normw_in[ex]), inc=("dW", 16))
                nw_at = rec.c("dW")
                E_("sync", lambda e, ex=ex: e.dma_start(
                    out=sv("pairv", 16)[:, 0:12].rearrange("p (b two) -> p b two",
                                                           two=2),
                    in_=pairs_d[ex * S:ex * S + PAD, :].rearrange(
                        "(b p) two -> p b two", p=128)), inc=("dW", 16))
                pv_at = rec.c("dW")
                rec.wait("vector", "dW", pv_at)
                rec.wait("vector", "dG")
                for rt in range(PAD // 128):
                    E_("vector", lambda e, rt=rt: e.tensor_copy(
                        idxt[:, 8 + rt:9 + rt], sv("pairv", 16)[:, 2 * rt:2 * rt + 1]))
                rec.edge("vector")
                idx_at = rec.c("e_vector")
                for rt in range(PAD // 128):
                    gb = gbuf[:, (rt % 2) * 1024:(rt % 2) * 1024 + 1024]
                    rec.join("gpsimd", "tensor")
                    rec.wait("gpsimd", "e_vector", idx_at)
                    E_("gpsimd", lambda e, gb=gb, rt=rt: e.indirect_dma_start(
                        gb, None, xr_spill,
                        IndirectOffsetOnAxis(idxt[:, 8 + rt:9 + rt], axis=0)),
                       inc=("dG", 16))
                    g_at = rec.c("dG")
                    rec.wait("tensor", "dG", g_at)
                    rec.join("tensor", "vector")
                    for hc in range(HC):
                        bank = 4 + (hc % 2)
                        tr(psb[bank][:, (hc // 2 % 2) * 128:(hc // 2 % 2) * 128 + 128],
                           gb[:, hc * 128:(hc + 1) * 128])
                        if hc % 4 == 3:
                            rec.edge("tensor")
                            rec.join("vector", "tensor")
                            rec.wait("vector", "dW", nw_at)
                            for hh in range(hc - 3, hc + 1):
                                bank2 = 4 + (hh % 2)
                                E_("vector", lambda e, hh=hh, bank2=bank2, rt=rt, ex=ex:
                                   e.tensor_scalar_mul(
                                       Xe[:, hh, rt * 128:(rt + 1) * 128],
                                       psb[bank2][:, (hh // 2 % 2) * 128:
                                                  (hh // 2 % 2) * 128 + 128],
                                       wvec[:, 8 + hh:9 + hh]))
                            rec.edge("vector")
                            rec.join("tensor", "vector")
                for rc in range(3):
                    r0 = rc * RC2
                    nrt = 2
                    wload_at = [0, 0]

                    def load_fc(fc):
                        bb = fc % 2
                        rec.wait("sync", "e_tensor")
                        E_("sync", lambda e, ex=ex, fc=fc, bb=bb: e.dma_start(
                            out=wgv(bb), in_=wg_in[ex][:, fc * 256:(fc + 1) * 256]
                            .rearrange("(c p) f -> p c f", p=128)), inc=("dW", 16))
                        E_("sync", lambda e, ex=ex, fc=fc, bb=bb: e.dma_start(
                            out=wuv(bb), in_=wu_in[ex][:, fc * 256:(fc + 1) * 256]
                            .rearrange("(c p) f -> p c f", p=128)), inc=("dW", 16))
                        E_("sync", lambda e, ex=ex, fc=fc, bb=bb: e.dma_start(
                            out=wdv(bb), in_=wd_in[ex][fc * 256:(fc + 1) * 256, :]
                            .rearrange("(f p) h -> p f h", p=128)), inc=("dW", 16))
                        wload_at[bb] = rec.c("dW")

                    load_fc(0)
                    load_fc(1)
                    gmark = [0] * 16
                    umark = [0] * 16
                    gumark = [0] * 16
                    wdmark = [0] * 16

                    def gbank(fgl):
                        return 4 + 2 * (fgl % 2)

                    def ubank(fgl):
                        return 5 + 2 * (fgl % 2)

                    def emit_wd(fgl, bb):
                        rec.wait("tensor", "e_vector", gumark[fgl])
                        for rl in range(nrt):
                            for hh in range(2):
                                mm(psb[rl * 2 + hh][:, :],
                                   sv_gu(fgl % 2)[:, rl * 128:(rl + 1) * 128],
                                   wdv(bb)[:, fgl % 2, hh * 512:(hh + 1) * 512],
                                   start=(fgl == 0), stop=(fgl == 15))
                        rec.edge("tensor")
                        wdmark[fgl] = rec.c("e_tensor")

                    for fgl in range(16):
                        fc, ft = fgl // 2, fgl % 2
                        bb = fc % 2
                        if ft == 0:
                            rec.wait("tensor", "dW", wload_at[bb])
                        if fgl >= 2:
                            rec.wait("tensor", "e_scalar", gmark[fgl - 2])
                        for hc in range(HC):
                            mm(psb[gbank(fgl)][:, 0:RC2],
                               wgv(bb)[:, hc, ft * 128:(ft + 1) * 128],
                               Xe[:, hc, r0:r0 + RC2],
                               start=(hc == 0), stop=(hc == HC - 1))
                        rec.edge("tensor")
                        gmm = rec.c("e_tensor")
                        if fgl >= 2:
                            rec.wait("tensor", "e_vector", gumark[fgl - 2])
                        for hc in range(HC):
                            mm(psb[ubank(fgl)][:, 0:RC2],
                               wuv(bb)[:, hc, ft * 128:(ft + 1) * 128],
                               Xe[:, hc, r0:r0 + RC2],
                               start=(hc == 0), stop=(hc == HC - 1))
                        rec.edge("tensor")
                        umm = rec.c("e_tensor")
                        rec.wait("scalar", "e_tensor", gmm)
                        if fgl >= 2:
                            rec.wait("scalar", "e_vector", gumark[fgl - 2])
                        E_("scalar", lambda e, fgl=fgl: e.activation(
                            sv_gg(fgl % 2), psb[gbank(fgl)][:, 0:RC2], AF.Gelu))
                        rec.edge("scalar")
                        gmark[fgl] = rec.c("e_scalar")
                        rec.wait("vector", "e_tensor", umm)
                        rec.wait("vector", "e_scalar", gmark[fgl])
                        if fgl >= 2:
                            rec.wait("vector", "e_tensor", wdmark[fgl - 2])
                        E_("vector", lambda e, fgl=fgl: e.tensor_tensor(
                            sv_gu(fgl % 2), sv_gg(fgl % 2), psb[ubank(fgl)][:, 0:RC2],
                            ALU.mult))
                        rec.edge("vector")
                        gumark[fgl] = rec.c("e_vector")
                        if fgl >= 1:
                            emit_wd(fgl - 1, (fgl - 1) // 2 % 2)
                            if (fgl - 1) % 2 == 1 and (fgl - 1) // 2 + 2 < 8:
                                load_fc((fgl - 1) // 2 + 2)
                    emit_wd(15, 1)
                    for rl in range(2):
                        rt = rc * 2 + rl
                        rec.join("vector", "tensor")
                        rec.wait("vector", "dG")
                        for hh in range(2):
                            E_("vector", lambda e, rl=rl, hh=hh, rt=rt:
                               e.tensor_scalar_mul(
                                   Ytm[:, hh * 512:(hh + 1) * 512],
                                   psb[rl * 2 + hh][:, :],
                                   sv("pairv", 16)[:, 2 * rt + 1:2 * rt + 2]))
                        rec.edge("vector")
                        rec.join("gpsimd", "vector")
                        if scat_prev:
                            rec.wait("gpsimd", "dG", scat_prev)
                        E_("gpsimd", lambda e, rt=rt: e.indirect_dma_start(
                            sY_d, IndirectOffsetOnAxis(idxt[:, 8 + rt:9 + rt],
                                                       axis=0),
                            Ytm[:, :], None, compute_op=ALU.add), inc=("dG", 16))
                        rec.join("tensor", "vector")
                scat_prev = rec.c("dG")

            rec.wait("gpsimd", rs)
            rec.wait("gpsimd", ls)
            rec.wait("vector", rs)
            E_("vector", lambda e, tb=tb: e.tensor_tensor(sv("mtbl", 64), tb[:, :],
                                                          maskt[:, :], ALU.mult))
            rec.edge("vector")
            E_("vector", lambda e: e.reduce_sum(svc("offsE", 0, 8), sv("mtbl", 64),
                                                axis=AX.X))
            rec.edge("vector")
            rec.join("tensor", "vector")
            tr(psb[6][:1, 0:128], sv("offsE", 8)[:, 0:1])
            rec.edge("tensor")
            rec.join("vector", "tensor")
            E_("vector", lambda e: e.tensor_copy(stage16[0:1, 0:128],
                                                 psb[6][:1, 0:128]))
            rec.edge("vector")
            rec.join("tensor", "vector")
            mm(psb[7][:, 0:8], onesr[:1, :], stage16[0:1, 0:8], start=True, stop=True)
            rec.edge("tensor")
            rec.join("vector", "tensor")
            E_("vector", lambda e: e.tensor_copy(sv("offsB", 8), psb[7][:, 0:8]))
            rec.edge("vector")
            for tt in range(ST):
                for k, ohf in ((1, sv_oh1), (2, sv_oh2)):
                    E_("vector", lambda e, ohf=ohf, tt=tt: e.tensor_tensor(
                        sv("t8", 8), sv("offsB", 8), ohf(tt), ALU.mult))
                    rec.edge("vector")
                    E_("vector", lambda e, k=k: e.reduce_sum(
                        svc("od%d" % k, 0, 8), sv("t8", 8), axis=AX.X))
                    rec.edge("vector")
                    E_("vector", lambda e, k=k, tt=tt: e.tensor_tensor(
                        svc("gp%d" % k, 0), svc("pos%d" % k, tt),
                        svc("od%d" % k, 0, 8), ALU.add))
                    rec.edge("vector")
                    E_("vector", lambda e, k=k: e.tensor_single_scalar(
                        svc("keep%d" % k, 0), svc("gp%d" % k, 0), float(CAP),
                        ALU.is_lt))
                    rec.edge("vector")
                E_("vector", lambda e, tt=tt: e.tensor_tensor(
                    svc("kk", tt), svc("keep1", 0), svc("keep2", 0), ALU.mult))
                rec.edge("vector")
                E_("vector", lambda e, tt=tt: e.tensor_tensor(
                    svc("wsum", 0), svc("w1", tt), svc("w2", tt), ALU.add))
                rec.edge("vector")
                E_("vector", lambda e, tt=tt: e.tensor_tensor(
                    svc("t1", 0), svc("kk", tt), svc("wsum", 0), ALU.mult))
                rec.edge("vector")
                E_("vector", lambda e, tt=tt: e.tensor_scalar_add(
                    svc("Cc", tt), svc("t1", 0), 1.0))
                rec.edge("vector")
                E_("vector", lambda e, tt=tt: e.tensor_scalar(
                    svc("s1", tt), svc("kk", tt), -1.0, 1.0,
                    op0=ALU.mult, op1=ALU.add))
                rec.edge("vector")

            if "fallback" not in SEC:
                return
            phase()
            fwgv = C[:, 0:4096].rearrange("p (c f) -> p c f", c=HC)
            fwuv = C[:, 4096:8192].rearrange("p (c f) -> p c f", c=HC)
            fwdv = D[:, 0:4096].rearrange("p (f h) -> p f h", f=4)
            rec.join("sync", "tensor", "vector", "scalar")
            E_("sync", lambda e: e.dma_start(
                out=fwgv, in_=fwg_in.rearrange("(c p) f -> p c f", p=128)),
               inc=("dW", 16))
            E_("sync", lambda e: e.dma_start(
                out=fwuv, in_=fwu_in.rearrange("(c p) f -> p c f", p=128)),
               inc=("dW", 16))
            E_("sync", lambda e: e.dma_start(
                out=fwdv, in_=fwd_in.rearrange("(f p) h -> p f h", p=128)),
               inc=("dW", 16))
            E_("sync", lambda e: e.dma_start(
                out=wvec[:, 16:24], in_=fnorm_in), inc=("dW", 16))
            rec.wait("vector", "dW")
            for hc in range(HC):
                E_("vector", lambda e, hc=hc: e.tensor_scalar_mul(
                    fwgv[:, hc, :], fwgv[:, hc, :], wvec[:, 16 + hc:17 + hc]))
                E_("vector", lambda e, hc=hc: e.tensor_scalar_mul(
                    fwuv[:, hc, :], fwuv[:, hc, :], wvec[:, 16 + hc:17 + hc]))
            rec.edge("vector")
            GU4 = B[:, 0:2048].rearrange("p (f s) -> p f s", f=4)
            xs_buf = B[:, 8192:12288]
            rstr = B[:, 2048:4096]
            fb_sb = B[:, 12288:16384].rearrange("p (c s) -> p c s", c=HC)[:, :, 0:512]
            for qc in range(4):
                rec.join("tensor", "vector", "scalar")
                for ft4 in range(4):
                    bank = ft4 % 4
                    for hc in range(HC):
                        mm(psb[bank][:, :], fwgv[:, hc, ft4 * 128:(ft4 + 1) * 128],
                           A_fm[:, hc, qc * 512:(qc + 1) * 512],
                           start=(hc == 0), stop=(hc == HC - 1))
                    rec.edge("tensor")
                    rec.join("scalar", "tensor")
                    E_("scalar", lambda e, ft4=ft4, bank=bank: e.activation(
                        GU4[:, ft4, :], psb[bank][:, :], AF.Gelu))
                    rec.edge("scalar")
                rec.join("tensor", "scalar")
                for ft4 in range(4):
                    bank = ft4 % 4
                    for hc in range(HC):
                        mm(psb[bank][:, :], fwuv[:, hc, ft4 * 128:(ft4 + 1) * 128],
                           A_fm[:, hc, qc * 512:(qc + 1) * 512],
                           start=(hc == 0), stop=(hc == HC - 1))
                    rec.edge("tensor")
                    rec.join("vector", "tensor")
                    E_("vector", lambda e, ft4=ft4, bank=bank: e.tensor_tensor(
                        GU4[:, ft4, :], GU4[:, ft4, :], psb[bank][:, :], ALU.mult))
                    rec.edge("vector")
                rec.join("tensor", "vector")
                for half in range(2):
                    for hh in range(4):
                        hc = half * 4 + hh
                        bank = 4 + hh
                        for ft4 in range(4):
                            mm(psb[bank][:, :], fwdv[:, ft4, hc * 128:(hc + 1) * 128],
                               GU4[:, ft4, :], start=(ft4 == 0), stop=(ft4 == 3))
                    rec.edge("tensor")
                    rec.join("vector", "tensor")
                    for hh in range(4):
                        hc = half * 4 + hh
                        E_("vector", lambda e, hc=hc, hh=hh: e.tensor_copy(
                            fb_sb[:, hc, :], psb[4 + hh][:, :]))
                    rec.edge("vector")
                    rec.join("tensor", "vector")
                for j in range(4):
                    tt = qc * 4 + j
                    xb = xs_buf[:, (j % 2) * 2048:(j % 2) * 2048 + 1024]
                    syb = xs_buf[:, (j % 2) * 2048 + 1024:(j % 2) * 2048 + 2048]
                    rec.join("scalar", "vector", "gpsimd")
                    rec.wait("scalar", "dG")
                    E_("scalar", lambda e, xb=xb, tt=tt: e.dma_start(
                        out=xb, in_=xtm_spill.rearrange("(p b) h -> p b h",
                                                        p=128)[:, tt, :]),
                       inc=("dA", 16))
                    E_("scalar", lambda e, syb=syb, tt=tt: e.dma_start(
                        out=syb, in_=sY_d.rearrange("(p b) h -> p b h",
                                                    p=128)[:, tt, :]),
                       inc=("dA", 16))
                    xld = rec.c("dA")
                    rec.join("tensor", "vector")
                    for hc in range(HC):
                        bank = hc % 2
                        tr(psb[bank][:, (hc // 2 % 2) * 128:(hc // 2 % 2) * 128 + 128],
                           fb_sb[:, hc, j * 128:(j + 1) * 128])
                        if hc % 4 == 3:
                            rec.edge("tensor")
                            rec.join("vector", "tensor")
                            for hh in range(hc - 3, hc + 1):
                                E_("vector", lambda e, hh=hh: e.tensor_copy(
                                    sv_fbtm()[:, hh * 128:(hh + 1) * 128],
                                    psb[hh % 2][:, (hh // 2 % 2) * 128:
                                                (hh // 2 % 2) * 128 + 128]))
                            rec.edge("vector")
                            rec.join("tensor", "vector")
                    rec.wait("vector", "dA", xld)
                    rec.wait("vector", "dG")
                    E_("vector", lambda e, xb=xb, tt=tt: e.tensor_scalar_mul(
                        sv_cmb(), xb, svc("Cc", tt)))
                    rec.edge("vector")
                    E_("vector", lambda e, syb=syb, tt=tt: e.tensor_scalar_mul(
                        sv_cm2(), syb, svc("kk", tt)))
                    rec.edge("vector")
                    E_("vector", lambda e: e.tensor_tensor(
                        sv_cmb(), sv_cmb(), sv_cm2(), ALU.add))
                    rec.edge("vector")
                    E_("vector", lambda e, tt=tt: e.tensor_scalar_mul(
                        sv_cm2(), sv_fbtm(), svc("s1", tt)))
                    rec.edge("vector")
                    E_("vector", lambda e: e.tensor_tensor(
                        sv_cmb(), sv_cmb(), sv_cm2(), ALU.add))
                    rec.edge("vector")
                    rec.join("sync", "vector")
                    rsb = rstr[:, (j % 2) * 1024:(j % 2) * 1024 + 1024]
                    E_("sync", lambda e, rsb=rsb, tt=tt: e.dma_start(
                        out=rsb.rearrange("p (c s) -> p c s", c=HC),
                        in_=r_spill[:, :, tt * 128:(tt + 1) * 128]), inc=("dW", 16))
                    rld = rec.c("dW")
                    rec.join("tensor", "vector")
                    for hc in range(HC):
                        bank = 2 + (hc % 2)
                        tr(psb[bank][:, (hc // 2 % 2) * 128:(hc // 2 % 2) * 128 + 128],
                           sv_cmb()[:, hc * 128:(hc + 1) * 128])
                        if hc % 4 == 3:
                            rec.edge("tensor")
                            rec.join("vector", "tensor")
                            rec.wait("vector", "dW", rld)
                            for hh in range(hc - 3, hc + 1):
                                E_("vector", lambda e, hh=hh, rsb=rsb, tt=tt:
                                   e.tensor_tensor(
                                       A_fm[:, hh, tt * 128:(tt + 1) * 128],
                                       psb[2 + hh % 2][:, (hh // 2 % 2) * 128:
                                                       (hh // 2 % 2) * 128 + 128],
                                       rsb.rearrange("p (c s) -> p c s",
                                                     c=HC)[:, hh, :], ALU.add))
                            rec.edge("vector")
                            rec.join("tensor", "vector")

        def attn_b():
            rec.join("scalar", "vector", "tensor")
            E_("scalar", lambda e: e.dma_start(out=r_spill, in_=A_fm), inc=("dA", 16))
            rspill_at = rec.c("dA")
            layernorm(A_fm, B_fm, 1e-5)
            rec.join("sync", "vector")
            load_bias_cols(bqkv_b, 32, 24)
            rec.wait("vector", "dA", rspill_at)
            phase()
            qkv_fm_phase(wqkvT_b, 2 * H, A_fm, None, bias_src=True, bias_cols=48)
            phase()
            fm_to_tm_spill(A_fm, None, (0, 1), tile_ap=lambda tt: v_spill[:, tt, :])
            phase()
            qkv_fm_phase(wqkvT_b, 0, None, q_spill, bias_src=True, bias_cols=32)
            phase()
            qkv_fm_phase(wqkvT_b, H, A_fm, None, bias_src=True, bias_cols=40)
            phase()

            PTa = B[:, 0:8192].rearrange("p (k q) -> p k q", k=ST)
            PTb = B[:, 8192:16384].rearrange("p (k q) -> p k q", k=ST)
            qbuf = D[:, 0:8192]
            vstream = C[:, 0:4096]
            sco = [C[:, 4096:6144], C[:, 6144:8192]]
            qload_at = [0, 0]
            for qb in range(2):
                qbv = qbuf[:, (qb % 2) * 4096:(qb % 2) * 4096 + 4096]
                E_("sync", lambda e, qb=qb, qbv=qbv: e.dma_start(
                    out=qbv.rearrange("p (c s) -> p c s", c=HC),
                    in_=q_spill[:, :, qb * 512:(qb + 1) * 512]), inc=("dW", 16))
                qload_at[qb % 2] = rec.c("dW")
            sc_evac = [0, 0, 0, 0]
            pt_evac = [0] * 8
            pv_evac = [0, 0]
            ao_dma = [0]
            for qb in range(4):
                qbv = qbuf[:, (qb % 2) * 4096:(qb % 2) * 4096 + 4096].rearrange(
                    "p (c s) -> p c s", c=HC)
                rec.wait("tensor", "dW", qload_at[qb % 2])
                vload_at = [0, 0]
                rec.join("sync", "tensor")
                for hc in range(2):
                    vsb = vstream[:, (hc % 2) * 2048:(hc % 2) * 2048 + 2048]
                    E_("sync", lambda e, vsb=vsb, hc=hc: e.dma_start(
                        out=vsb.rearrange("p (k f) -> p k f", k=ST),
                        in_=v_spill[:, :, hc * 128:(hc + 1) * 128]), inc=("dW", 16))
                    vload_at[hc % 2] = rec.c("dW")
                for hc in range(HC):
                    for qt in range(4):
                        for h2 in range(2):
                            p0 = h2 * 64
                            for kc in range(4):
                                bank = h2 * 2 + kc % 2
                                if sc_evac[bank]:
                                    rec.wait("tensor", "e_scalar", sc_evac[bank])
                                mm2 = lambda e, hc=hc, qt=qt, kc=kc, p0=p0, bank=bank, qbv=qbv: \
                                    e.matmul(psb[bank][:, :],
                                             qbv[p0:p0 + 64, hc, qt * 128:(qt + 1) * 128],
                                             A_fm[p0:p0 + 64, hc, kc * 512:(kc + 1) * 512],
                                             start=True, stop=True,
                                             tile_position=(p0, 0))
                                rec.op("tensor", mm2)
                                rec.edge("tensor")
                                rec.join("scalar", "tensor")
                                rec.wait("scalar", "dA", ao_dma[0])
                                E_("scalar", lambda e, h2=h2, qt=qt, kc=kc, bank=bank:
                                   e.copy(sco[h2][:, kc * 512:(kc + 1) * 512],
                                          psb[bank][:, :]))
                                rec.edge("scalar")
                                sc_evac[bank] = rec.c("e_scalar")
                        for h2 in range(2):
                            sl = sco[h2]
                            rec.join("vector", "scalar")
                            E_("vector", lambda e, sl=sl: e.reduce_max(
                                svc("negmax", 0), sl, axis=AX.X, negate=True))
                            rec.edge("vector")
                            E_("vector", lambda e: e.tensor_scalar_mul(
                                svc("negmax", 0), svc("negmax", 0), 0.125))
                            rec.edge("vector")
                            rec.join("scalar", "vector")
                            E_("scalar", lambda e, sl=sl: e.activation(
                                sl, sl, AF.Exp, bias=svc("negmax", 0), scale=0.125,
                                accum_out=svc("ssum", 0)))
                            rec.edge("scalar")
                            rec.join("vector", "scalar")
                            E_("vector", lambda e: e.reciprocal(
                                svc("srec", 0), svc("ssum", 0)))
                            rec.edge("vector")
                            E_("vector", lambda e, sl=sl: e.tensor_scalar_mul(
                                sl, sl, svc("srec", 0)))
                            rec.edge("vector")
                            rec.join("tensor", "vector")
                            PT = PTa if h2 == 0 else PTb
                            for kt4 in range(4):
                                bank = 5 + (kt4 % 2)
                                if pt_evac[bank]:
                                    rec.wait("tensor", "e_vector", pt_evac[bank])
                                for j in range(4):
                                    kt = kt4 * 4 + j
                                    tr(psb[bank][:, j * 128:(j + 1) * 128],
                                       sl[:, kt * 128:(kt + 1) * 128])
                                rec.edge("tensor")
                                rec.join("vector", "tensor")
                                E_("vector", lambda e, PT=PT, kt4=kt4, qt=qt, bank=bank:
                                   e.tensor_copy(
                                       PT[:, kt4 * 4:kt4 * 4 + 4,
                                          qt * 128:(qt + 1) * 128],
                                       psb[bank][:, :].rearrange(
                                           "p (a q) -> p a q", a=4)))
                                rec.edge("vector")
                                pt_evac[bank] = rec.c("e_vector")
                    rec.wait("tensor", "dW", vload_at[hc % 2])
                    rec.join("tensor", "vector")
                    if pv_evac[0]:
                        rec.wait("tensor", "e_vector", pv_evac[0])
                    vsb = vstream[:, (hc % 2) * 2048:(hc % 2) * 2048 + 2048].rearrange(
                        "p (k f) -> p k f", k=ST)
                    for kt in range(ST):
                        for h2 in range(2):
                            mm2 = lambda e, kt=kt, h2=h2, vsb=vsb, hc=hc: \
                                e.matmul((psb[4] if h2 == 0 else psb[7])[
                                             h2 * 64:h2 * 64 + 64, :],
                                         vsb[:, kt, h2 * 64:h2 * 64 + 64],
                                         (PTa if h2 == 0 else PTb)[:, kt, :],
                                         start=(kt == 0), stop=(kt == ST - 1),
                                         tile_position=(0, h2 * 64))
                            rec.op("tensor", mm2)
                    rec.edge("tensor")
                    rec.join("vector", "tensor")
                    if ao_dma[0]:
                        rec.wait("vector", "dA", ao_dma[0])
                    E_("vector", lambda e, hc=hc: e.tensor_copy(
                        sco[0][0:64, 0:512], psb[4][0:64, :]))
                    E_("vector", lambda e, hc=hc: e.tensor_copy(
                        sco[0][64:128, 0:512], psb[7][64:128, :]))
                    rec.edge("vector")
                    pv_evac[0] = rec.c("e_vector")
                    rec.join("scalar", "vector")
                    E_("scalar", lambda e, hc=hc, qb=qb: e.dma_start(
                        out=a_spill[:, hc, qb * 512:(qb + 1) * 512],
                        in_=sco[0][:, 0:512]), inc=("dA", 16))
                    ao_dma[0] = rec.c("dA")
                    if hc + 2 < HC:
                        hc2 = hc + 2
                        rec.wait("sync", "e_tensor")
                        vsb2 = vstream[:, (hc2 % 2) * 2048:(hc2 % 2) * 2048 + 2048]
                        E_("sync", lambda e, vsb2=vsb2, hc2=hc2: e.dma_start(
                            out=vsb2.rearrange("p (k f) -> p k f", k=ST),
                            in_=v_spill[:, :, hc2 * 128:(hc2 + 1) * 128]),
                           inc=("dW", 16))
                        vload_at[hc2 % 2] = rec.c("dW")
                if qb + 2 < 4:
                    qb2 = qb + 2
                    qbv2 = qbuf[:, (qb2 % 2) * 4096:(qb2 % 2) * 4096 + 4096]
                    rec.wait("sync", "e_tensor")
                    E_("sync", lambda e, qb2=qb2, qbv2=qbv2: e.dma_start(
                        out=qbv2.rearrange("p (c s) -> p c s", c=HC),
                        in_=q_spill[:, :, qb2 * 512:(qb2 + 1) * 512]), inc=("dW", 16))
                    qload_at[qb2 % 2] = rec.c("dW")

            phase()
            wp = D[:, 0:8192].rearrange("p (c f) -> p c f", c=HC)
            for jc in range(HC):
                E_("sync", lambda e, jc=jc: e.dma_start(
                    out=wp[:, jc, :], in_=woutT_b[jc * 128:(jc + 1) * 128, :]),
                   inc=("dW", 16))
            load_bias_cols(bout_b, 0)
            astream = C[:, 0:4096]
            rstream = C[:, 4096:8192]
            pj_evac = [0, 0, 0, 0]
            for blk in range(HC):
                asb = astream[:, (blk % 2) * 2048:(blk % 2) * 2048 + 2048]
                rsb = rstream[:, (blk % 2) * 2048:(blk % 2) * 2048 + 2048]
                if blk >= 2:
                    rec.wait("sync", "e_tensor")
                    rec.wait("sync", "e_vector")
                E_("sync", lambda e, asb=asb, blk=blk: e.dma_start(
                    out=asb.rearrange("p (c i) -> p c i", c=HC),
                    in_=a_spill[:, :, blk * 256:(blk + 1) * 256]), inc=("dW", 16))
                E_("sync", lambda e, rsb=rsb, blk=blk: e.dma_start(
                    out=rsb.rearrange("p (c i) -> p c i", c=HC),
                    in_=r_spill[:, :, blk * 256:(blk + 1) * 256]), inc=("dW", 16))
                aload = rec.c("dW")
                rec.wait("tensor", "dW", aload)
                asv = asb.rearrange("p (c i) -> p c i", c=HC)
                rsv = rsb.rearrange("p (c i) -> p c i", c=HC)
                for ec in range(HC):
                    bank = 4 + (ec % 4)
                    if pj_evac[ec % 4]:
                        rec.wait("tensor", "e_vector", pj_evac[ec % 4])
                    for jc in range(HC):
                        mm(psb[bank][:, 0:256], wp[:, jc, ec * 128:(ec + 1) * 128],
                           asv[:, jc, :], start=(jc == 0), stop=(jc == HC - 1))
                    rec.edge("tensor")
                    rec.join("vector", "tensor")
                    E_("vector", lambda e, ec=ec, bank=bank, rsv=rsv, blk=blk:
                       e.scalar_tensor_tensor(
                           A_fm[:, ec, blk * 256:(blk + 1) * 256], psb[bank][:, 0:256],
                           wvec[:, ec:ec + 1], rsv[:, ec, :], ALU.add, ALU.add))
                    rec.edge("vector")
                    pj_evac[ec % 4] = rec.c("e_vector")

        def sv_gg(p=0):
            return D[:, 4096 + p * 512:4096 + p * 512 + RC2]

        def sv_gu(p=0):
            return D[:, 4352 + p * 512:4352 + p * 512 + RC2]

        def sv_fbtm():
            return D[:, 5120:6144]

        def sv_cmb():
            return D[:, 6144:7168]

        def sv_cm2():
            return D[:, 7168:8192]

        def sv_oh1(tt):
            return sv("oh1_%d" % tt, 8)

        def sv_oh2(tt):
            return sv("oh2_%d" % tt, 8)

        rankreg = []
        rec.wait("gpsimd", "dW")

        def _loadrank(e):
            rankreg.append(e.value_load(idxt[:1, 62:63], min_val=0, max_val=7))
            return e.nop()
        E_("gpsimd", _loadrank)

        if dev >= 2:
            attn_a()
        if dev >= 3:
            ffn(0)
        if dev >= 4:
            attn_b()
        if dev >= 5:
            ffn(1)

        rec.wait("sync", "dW")
        rec.wait("sync", "dA")
        rec.join("sync", "vector", "tensor", "scalar", "gpsimd")
        E_("sync", lambda e: e.dma_start(out=y_fm, in_=A_fm), inc=("dS", 16))
        rec.wait("gpsimd", "dS")
        rec.wait("sync", "dS")

        for ename in ENGS:
            def mk(ename):
                def _(h):
                    rec.replay(ename, h, sems)
                return _
            getattr(block, ename)(mk(ename))

    nc.compile()
    return nc


def kernel(L0_attn_a_x, L0_attn_a_w_qkv, L0_attn_a_w_proj, L0_attn_a_b_proj,
           L0_attn_b_W_qkv, L0_attn_b_b_qkv, L0_attn_b_W_out, L0_attn_b_b_out,
           L0_ffn_Wr, L0_ffn_norm_w, L0_ffn_Wg, L0_ffn_Wu, L0_ffn_Wd,
           L0_ffn_f_norm, L0_ffn_fWg, L0_ffn_fWu, L0_ffn_fWd):
    x = np.ascontiguousarray(np.asarray(L0_attn_a_x, dtype=np.float32))
    shared = {
        "ident": np.eye(128, dtype=np.float32),
        "wqkvT_a": np.ascontiguousarray(np.asarray(L0_attn_a_w_qkv, np.float32).T),
        "wprojT_a": np.ascontiguousarray(np.asarray(L0_attn_a_w_proj, np.float32).T),
        "bproj_a": np.ascontiguousarray(np.asarray(L0_attn_a_b_proj, np.float32).reshape(HC, 128).T),
        "wqkvT_b": np.ascontiguousarray(np.asarray(L0_attn_b_W_qkv, np.float32).T),
        "bqkv_b": np.ascontiguousarray(np.asarray(L0_attn_b_b_qkv, np.float32).reshape(3 * HC, 128).T),
        "woutT_b": np.ascontiguousarray(np.asarray(L0_attn_b_W_out, np.float32).T),
        "bout_b": np.ascontiguousarray(np.asarray(L0_attn_b_b_out, np.float32).reshape(HC, 128).T),
        "wr": np.ascontiguousarray(np.asarray(L0_ffn_Wr, np.float32)),
        "normw": np.ascontiguousarray(np.asarray(L0_ffn_norm_w, np.float32).reshape(E, HC, 128).transpose(0, 2, 1)),
        "wg": np.ascontiguousarray(np.asarray(L0_ffn_Wg, np.float32)),
        "wu": np.ascontiguousarray(np.asarray(L0_ffn_Wu, np.float32)),
        "wd": np.ascontiguousarray(np.asarray(L0_ffn_Wd, np.float32)),
        "fnorm": np.ascontiguousarray(np.asarray(L0_ffn_f_norm, np.float32).reshape(HC, 128).T),
        "fwg": np.ascontiguousarray(np.asarray(L0_ffn_fWg, np.float32)),
        "fwu": np.ascontiguousarray(np.asarray(L0_ffn_fWu, np.float32)),
        "fwd": np.ascontiguousarray(np.asarray(L0_ffn_fWd, np.float32)),
    }
    in_maps = []
    for c in range(NCORE):
        m = dict(shared)
        xt = np.ascontiguousarray(x[c].T).reshape(HC, 128, S).transpose(1, 0, 2)
        m["x_fm"] = np.ascontiguousarray(xt)
        m["rankv"] = np.array([c], dtype=np.int32)
        mk = np.zeros((NCORE, E), dtype=np.float32)
        mk[:c, :] = 1.0
        m["maskrow"] = np.broadcast_to(mk.reshape(-1), (128, 64)).copy()
        in_maps.append(m)

    nc = build_nc(globals().get("DEV_STAGE", 5))
    res = run_bass_kernel_spmd(nc, in_maps, core_ids=list(range(NCORE)))
    out = np.empty((NCORE, S, H), dtype=np.float32)
    for c in range(NCORE):
        yf = res.results[c]["y_fm"]
        out[c] = yf.transpose(1, 0, 2).reshape(H, S).T
    return out


# version 05
# speedup vs baseline: 1.1811x; 1.1811x over previous
import dataclasses
from contextlib import ExitStack

import numpy as np

import concourse.mybir as mybir
from concourse.bacc import Bacc
from concourse.bass import IndirectOffsetOnAxis
from concourse.bass_utils import run_bass_kernel_spmd

F32 = mybir.dt.float32
I32 = mybir.dt.int32
AF = mybir.ActivationFunctionType
ALU = mybir.AluOpType
AX = mybir.AxisListType

S = 2048
H = 1024
HC = H // 128
ST = S // 128
NCORE = 8
E = 8
FF = 2048
F4 = 512
CAP = 4096
PAD = 768
RC2 = 256
NH = 16
HD = 64

ENGS = ("sync", "scalar", "vector", "tensor", "gpsimd")


class Rec:
    def __init__(self):
        self.q = {e: [] for e in ENGS}
        self.cnt = {}

    def c(self, sem):
        return self.cnt.get(sem, 0)

    def op(self, eng, fn, inc=None):
        if inc is not None and inc[0].startswith("d"):
            self.wait(eng, inc[0])
        ent = [fn, None, None]
        self.q[eng].append(ent)
        if inc is not None:
            self.inc(eng, inc[0], inc[1])
        return ent

    def inc(self, eng, sem, n=1):
        ent = self.q[eng][-1]
        assert ent[0] is not None and ent[1] is None
        ent[1] = (sem, n)
        self.cnt[sem] = self.cnt.get(sem, 0) + n

    def wait(self, eng, sem, val=None):
        if val is None:
            val = self.cnt.get(sem, 0)
        if val <= 0:
            return
        self.q[eng].append([None, None, (sem, val)])

    def edge(self, eng):
        self.inc(eng, "e_" + eng, 1)

    def join(self, eng, *others):
        for o in others:
            self.wait(eng, "e_" + o)

    def replay(self, eng, handle, sems):
        water = {}
        own = "e_" + eng
        own_cnt = 0
        for fn, inc, wt in self.q[eng]:
            if wt is not None:
                s, v = wt
                if water.get(s, 0) < v:
                    handle.wait_ge(sems[s], v)
                    water[s] = v
            if fn is not None:
                if water.get(own, 0) < own_cnt:
                    handle.wait_ge(sems[own], own_cnt)
                    water[own] = own_cnt
                r = fn(handle)
                if inc is not None:
                    r.then_inc(sems[inc[0]], inc[1])
                    if inc[0] == own:
                        own_cnt += inc[1]


FFN_SECTIONS = ("prep", "router", "exchange", "pos", "zero", "experts",
                "fallback", "combine")
LAST_REC = None


def build_nc(dev=5):
    nc = Bacc("TRN2")

    def dram_in(name, shape, dt=F32):
        return nc.dram_tensor(name, list(shape), dt, kind="ExternalInput").ap()

    x_in = dram_in("x_fm", [128, HC, S])
    rankv = dram_in("rankv", [1], I32)
    maskin = dram_in("maskrow", [128, 64])
    ident_in = dram_in("ident", [128, 128])
    wqkvT_a = dram_in("wqkvT_a", [H, 3 * H])
    wprojT_a = dram_in("wprojT_a", [H, H])
    bproj_a = dram_in("bproj_a", [128, HC])
    wqkvT_b = dram_in("wqkvT_b", [H, 3 * H])
    bqkv_b = dram_in("bqkv_b", [128, 3 * HC])
    woutT_b = dram_in("woutT_b", [H, H])
    bout_b = dram_in("bout_b", [128, HC])
    wr_in = dram_in("wr", [H, E])
    normw_in = dram_in("normw", [E, 128, HC])
    wg_in = dram_in("wg", [E, H, FF])
    wu_in = dram_in("wu", [E, H, FF])
    wd_in = dram_in("wd", [E, FF, H])
    fnorm_in = dram_in("fnorm", [128, HC])
    fwg_in = dram_in("fwg", [H, F4])
    fwu_in = dram_in("fwu", [H, F4])
    fwd_in = dram_in("fwd", [F4, H])

    y_fm = nc.dram_tensor("y_fm", [128, HC, S], F32, kind="ExternalOutput").ap()

    def scratch(name, shape, dt=F32):
        return nc.dram_tensor(name, list(shape), dt, kind="Internal").ap()

    r_spill = scratch("r_spill", [128, HC, S])
    v_spill = scratch("v_spill", [128, ST, H])
    q_spill = scratch("q_spill", [128, HC, S])
    a_spill = scratch("a_spill", [128, HC, S])
    xtm_spill = scratch("xtm_spill", [S, H])
    xr_spill = scratch("xr_spill", [S, H])
    pairs_d = scratch("pairs_d", [E * S, 2])
    sY_d = scratch("sY_d", [S, H])

    rec = Rec()
    global LAST_REC
    LAST_REC = rec
    es = ExitStack()
    with es:
        def sb(name, p, n, dt=F32):
            return es.enter_context(nc.sbuf_tensor(name, [p, n], dt))

        A = sb("A", 128, 16384)
        B = sb("B", 128, 16384)
        C = sb("C", 128, 8192)
        D = sb("D", 128, 8192)
        identt = sb("identt", 128, 128)
        smalls = sb("smalls", 128, 2048)
        zeroK = sb("zeroK", 128, 1024)
        wvec = sb("wvec", 128, 96)
        wrt = sb("wrt", 128, 64)
        iot8 = sb("iot8", 128, 8)
        cntt = sb("cntt", 128, 8)
        tblt = [sb("tbl0", 128, 64), sb("tbl1", 128, 64)]
        maskt = sb("maskt", 128, 64)
        stage16 = sb("stage16", 16, 128)
        onesr = sb("onesr", 1, 128)
        idxt = sb("idxt", 128, 64, I32)
        sidxc = sb("sidxc", 128, 16, I32)

        psb = [es.enter_context(nc.psum_tensor(f"ps{i}", [128, 512], F32))
               for i in range(8)]

        semnames = ["e_sync", "e_scalar", "e_vector", "e_tensor", "e_gpsimd",
                    "dW", "dA", "dG", "dS", "dWg", "dWu", "dWd", "dQ", "dV",
                    "prep", "rsem", "lsem", "rsem2", "lsem2"]
        sems = {n: es.enter_context(nc.semaphore(n)) for n in semnames}
        block = es.enter_context(nc.Block())

        def E_(eng, fn, inc=None):
            rec.op(eng, fn, inc=inc)

        def mm(out_ap, lhsT, rhs, start, stop, tp=None):
            rec.op("tensor", lambda e, o=out_ap, l=lhsT, r=rhs, st=start, sp=stop,
                   tp=tp: e.matmul(o, l, r, start=st, stop=sp, tile_position=tp))

        def tr(out_ap, in_ap):
            p = in_ap.shape[0]
            rec.op("tensor", lambda e, o=out_ap, i=in_ap, p=p:
                   e.transpose(o, i, identt[:p, :p]))

        def fmv(t, off=0):
            return t[:, off:off + HC * S].rearrange("p (c s) -> p c s", c=HC)

        A_fm = fmv(A)
        B_fm = fmv(B)

        def phase():
            engs = ("tensor", "vector", "scalar", "gpsimd", "sync")
            for a in engs:
                rec.join(a, *[o for o in engs if o != a])
                for dsem in ("dW", "dA", "dG", "dS", "dWg", "dWu", "dWd",
                             "dQ", "dV"):
                    rec.wait(a, dsem)

        _svar = {}

        def sv(name, w=16):
            if name not in _svar:
                base = sum(ww for _, ww in _svar.values())
                assert base + w <= 2048, "smalls overflow"
                _svar[name] = (base, w)
            o, ww = _svar[name]
            assert ww == w, name
            return smalls[:, o:o + w]

        def svc(name, t, w=16):
            return sv(name, w)[:, t:t + 1]

        E_("sync", lambda e: e.dma_start(out=identt[:, :], in_=ident_in), inc=("dW", 16))
        E_("sync", lambda e: e.dma_start(out=maskt[:, :], in_=maskin), inc=("dW", 16))
        E_("sync", lambda e: e.dma_start(out=A_fm, in_=x_in), inc=("dW", 16))
        E_("sync", lambda e: e.dma_start(
            out=wrt[:, :].rearrange("p (c e) -> p c e", c=HC),
            in_=wr_in.rearrange("(c p) e -> p c e", p=128)), inc=("dW", 16))
        E_("sync", lambda e: e.dma_start(out=idxt[:1, 62:63], in_=rankv[None, :]),
           inc=("dW", 16))
        E_("vector", lambda e: e.memset(zeroK[:, :], 0.0))
        E_("vector", lambda e: e.memset(smalls[:, :], 0.0))
        E_("vector", lambda e: e.memset(cntt[:, :], 0.0))
        E_("vector", lambda e: e.memset(idxt[:, 0:62], 0))
        E_("vector", lambda e: e.memset(wvec[:, :], 0.0))
        E_("vector", lambda e: e.memset(onesr[:1, :], 1.0))
        rec.edge("vector")
        ones_col = wvec[:, 95:96]
        eps5_col = wvec[:, 94:95]
        eps6_col = wvec[:, 93:94]
        E_("vector", lambda e: e.memset(ones_col, 1.0))
        E_("vector", lambda e: e.memset(eps5_col, 1e-5))
        E_("vector", lambda e: e.memset(eps6_col, 1e-6))
        rec.edge("vector")
        E_("gpsimd", lambda e: e.iota(sidxc[:, :], pattern=[[1, 16]], base=0,
                                      channel_multiplier=16))
        E_("gpsimd", lambda e: e.iota(iot8[:, :], pattern=[[1, 8]], base=0,
                                      channel_multiplier=0,
                                      allow_small_or_imprecise_dtypes=True))
        rec.edge("gpsimd")
        rec.join("vector", "gpsimd")
        E_("vector", lambda e: e.tensor_copy(sv("sidxf"), sidxc[:, :]))
        rec.edge("vector")

        def bcast_tm_var(var_ap, slab):
            rowC = C[0:1, 6144:8192]
            rec.join("tensor", "vector", "scalar")
            tr(psb[6][:16, 0:128], var_ap)
            rec.edge("tensor")
            rec.join("vector", "tensor")
            E_("vector", lambda e: e.tensor_copy(stage16[:, :], psb[6][:16, 0:128]))
            rec.edge("vector")
            rec.join("tensor", "vector")
            for r4 in range(4):
                for j in range(4):
                    c = r4 * 4 + j
                    mm(psb[7][0:1, j * 128:(j + 1) * 128],
                       identt[0:16, c:c + 1], stage16[:, :],
                       start=True, stop=True)
                rec.edge("tensor")
                rec.join("vector", "tensor")
                E_("vector", lambda e, r4=r4: e.tensor_copy(
                    rowC[:, r4 * 512:(r4 + 1) * 512], psb[7][0:1, 0:512]))
                rec.edge("vector")
                rec.join("tensor", "vector")
            for b4 in range(4):
                mm(psb[7][:, :], onesr[:1, :], rowC[:, b4 * 512:(b4 + 1) * 512],
                   start=True, stop=True)
                rec.edge("tensor")
                rec.join("vector", "tensor")
                E_("vector", lambda e, b4=b4: e.tensor_copy(
                    slab[:, b4 * 512:(b4 + 1) * 512], psb[7][:, :]))
                rec.edge("vector")
                rec.join("tensor", "vector")

        def layernorm(src_fm, dst_fm, eps, rms_only=False, apply=True):
            phase()
            x2 = C[:, 0:1024]
            a_slab = C[:, 2048:4096]
            b_slab = C[:, 4096:6144]
            for rnd in range(4):
                for hc in range(HC):
                    xb = x2[:, (hc % 2) * 512:(hc % 2) * 512 + 512]
                    rec.join("scalar", "tensor")
                    E_("scalar", lambda e, hc=hc, rnd=rnd, xb=xb: e.activation(
                        xb, src_fm[:, hc, rnd * 512:rnd * 512 + 512], AF.Square))
                    rec.edge("scalar")
                    rec.join("tensor", "scalar")
                    for i in range(4):
                        t = rnd * 4 + i
                        if not rms_only:
                            mm(psb[i][:, 0:1],
                               src_fm[:, hc, t * 128:(t + 1) * 128],
                               ones_col, start=(hc == 0), stop=(hc == HC - 1))
                        mm(psb[4 + i][:, 0:1], xb[:, i * 128:(i + 1) * 128],
                           ones_col, start=(hc == 0), stop=(hc == HC - 1))
                    rec.edge("tensor")
                rec.join("vector", "tensor")
                for i in range(4):
                    t = rnd * 4 + i
                    if not rms_only:
                        E_("vector", lambda e, i=i, t=t: e.tensor_copy(
                            svc("lsum", t), psb[i][:, 0:1]))
                    E_("vector", lambda e, i=i, t=t: e.tensor_copy(
                        svc("lsq", t), psb[4 + i][:, 0:1]))
                rec.edge("vector")
                rec.join("tensor", "vector")
            if rms_only:
                rec.join("scalar", "vector")
                E_("scalar", lambda e, eps=eps: e.activation(
                    sv("sd"), sv("lsq"), AF.Sqrt,
                    bias=eps5_col if eps == 1e-5 else eps6_col, scale=1.0 / H))
                rec.edge("scalar")
                rec.join("vector", "scalar")
                E_("vector", lambda e: e.reciprocal(sv("a"), sv("sd")))
                rec.edge("vector")
            else:
                E_("vector", lambda e: e.tensor_scalar_mul(sv("mean"), sv("lsum"),
                                                           1.0 / H))
                rec.edge("vector")
                E_("vector", lambda e: e.tensor_tensor(sv("m2"), sv("mean"),
                                                       sv("mean"), ALU.mult))
                rec.edge("vector")
                E_("vector", lambda e: e.scalar_tensor_tensor(
                    sv("var"), sv("lsq"), 1.0 / H, sv("m2"), ALU.mult, ALU.subtract))
                rec.edge("vector")
                rec.join("scalar", "vector")
                E_("scalar", lambda e, eps=eps: e.activation(
                    sv("sd"), sv("var"), AF.Sqrt,
                    bias=eps5_col if eps == 1e-5 else eps6_col))
                rec.edge("scalar")
                rec.join("vector", "scalar")
                E_("vector", lambda e: e.reciprocal(sv("a"), sv("sd")))
                rec.edge("vector")
                E_("vector", lambda e: e.scalar_tensor_tensor(
                    sv("b"), sv("mean"), -1.0, sv("a"), ALU.mult, ALU.mult))
                rec.edge("vector")
            if not apply:
                return
            bcast_tm_var(sv("a"), a_slab)
            if not rms_only:
                bcast_tm_var(sv("b"), b_slab)
            rec.join("vector", "tensor")
            for hc in range(HC):
                E_("vector", lambda e, hc=hc: e.tensor_tensor(
                    dst_fm[:, hc, :], src_fm[:, hc, :], a_slab, ALU.mult))
                rec.edge("vector")
                if not rms_only:
                    E_("vector", lambda e, hc=hc: e.tensor_tensor(
                        dst_fm[:, hc, :], dst_fm[:, hc, :], b_slab, ALU.add))
                    rec.edge("vector")

        def qkv_fm_phase(wsrc, col0, dst_fm_view, spill, bias_src=None, bias_cols=None):
            wq = C[:, 0:8192].rearrange("p (c f) -> p c f", c=HC)
            rec.join("sync", "tensor", "vector")
            for hc in range(HC):
                E_("sync", lambda e, hc=hc: e.dma_start(
                    out=wq[:, hc, :], in_=wsrc[hc * 128:(hc + 1) * 128, col0:col0 + H]),
                   inc=("dW", 16))
            rec.wait("tensor", "dW")
            stage = D[:, 0:4096]
            last_evac = 0
            for qfc in range(HC):
                if last_evac:
                    rec.wait("tensor", "e_vector", last_evac)
                for hc in range(HC):
                    for sc in range(4):
                        mm(psb[sc][:, :], wq[:, hc, qfc * 128:(qfc + 1) * 128],
                           B_fm[:, hc, sc * 512:(sc + 1) * 512],
                           start=(hc == 0), stop=(hc == HC - 1))
                rec.edge("tensor")
                rec.join("vector", "tensor")
                if spill is not None and qfc >= 2:
                    rec.wait("vector", "dA")
                sg = stage[:, (qfc % 2) * 2048:(qfc % 2) * 2048 + 2048]
                for sc in range(4):
                    dst = (sg[:, sc * 512:(sc + 1) * 512] if spill is not None
                           else dst_fm_view[:, qfc, sc * 512:(sc + 1) * 512])
                    if bias_src is not None:
                        E_("vector", lambda e, dst=dst, sc=sc, qfc=qfc:
                           e.tensor_scalar_add(dst, psb[sc][:, :],
                                               wvec[:, bias_cols + qfc:
                                                    bias_cols + qfc + 1]))
                    else:
                        E_("vector", lambda e, dst=dst, sc=sc: e.tensor_copy(
                            dst, psb[sc][:, :]))
                rec.edge("vector")
                last_evac = rec.c("e_vector")
                if spill is not None:
                    rec.join("scalar", "vector")
                    E_("scalar", lambda e, sg=sg, qfc=qfc: e.dma_start(
                        out=spill[:, qfc, :], in_=sg), inc=("dA", 16))

        def load_bias_cols(src_dram, cols, n=HC):
            E_("sync", lambda e: e.dma_start(
                out=wvec[:, cols:cols + n], in_=src_dram), inc=("dW", 16))

        def fm_to_tm_spill(src_fm, spill, psum_pair, tile_ap=None):
            b0, b1 = psum_pair
            if tile_ap is None:
                tile_ap = lambda tt: spill.rearrange("(p b) h -> p b h",
                                                     p=128)[:, tt, :]
            tmb = D[:, 4096:8192]
            for tt in range(ST):
                tb = tmb[:, (tt % 2) * 2048:(tt % 2) * 2048 + 1024]
                rec.join("tensor", "vector")
                for hc in range(HC):
                    bank = b0 if hc < 4 else b1
                    tr(psb[bank][:, (hc % 4) * 128:(hc % 4) * 128 + 128],
                       src_fm[:, hc, tt * 128:(tt + 1) * 128])
                rec.edge("tensor")
                rec.join("vector", "tensor")
                if tt >= 2:
                    rec.wait("vector", "dA")
                E_("vector", lambda e, tb=tb, b0=b0: e.tensor_copy(
                    tb[:, 0:512], psb[b0][:, :]))
                E_("vector", lambda e, tb=tb, b1=b1: e.tensor_copy(
                    tb[:, 512:1024], psb[b1][:, :]))
                rec.edge("vector")
                rec.join("scalar", "vector")
                E_("scalar", lambda e, tb=tb, tt=tt: e.dma_start(
                    out=tile_ap(tt), in_=tb), inc=("dA", 16))

        def attn_a():
            rec.join("scalar", "vector", "tensor")
            rec.wait("scalar", "dW")
            E_("scalar", lambda e: e.dma_start(out=r_spill, in_=A_fm), inc=("dA", 16))
            rspill_at = rec.c("dA")
            layernorm(A_fm, B_fm, 1e-5)

            phase()
            wv = C[:, 0:8192].rearrange("p (c f) -> p c f", c=HC)
            rec.join("sync", "vector", "tensor")
            for hc in range(HC):
                E_("sync", lambda e, hc=hc: e.dma_start(
                    out=wv[:, hc, :],
                    in_=wqkvT_a[hc * 128:(hc + 1) * 128, 2 * H:3 * H]), inc=("dW", 16))
            vbuf = D[:, 0:2048]
            rec.wait("tensor", "dW")
            evac_at = [0, 0]
            for st in range(ST):
                b0 = (st % 2) * 2
                if evac_at[st % 2]:
                    rec.wait("tensor", "e_vector", evac_at[st % 2])
                for hc in range(HC):
                    mm(psb[b0][:, :], B_fm[:, hc, st * 128:(st + 1) * 128],
                       wv[:, hc, 0:512], start=(hc == 0), stop=(hc == HC - 1))
                    mm(psb[b0 + 1][:, :], B_fm[:, hc, st * 128:(st + 1) * 128],
                       wv[:, hc, 512:1024], start=(hc == 0), stop=(hc == HC - 1))
                rec.edge("tensor")
                rec.join("vector", "tensor")
                if st >= 2:
                    rec.wait("vector", "dA")
                vb = vbuf[:, (st % 2) * 1024:(st % 2) * 1024 + 1024]
                E_("vector", lambda e, vb=vb, b0=b0: e.tensor_copy(vb[:, 0:512],
                                                                   psb[b0][:, :]))
                E_("vector", lambda e, vb=vb, b0=b0: e.tensor_copy(vb[:, 512:1024],
                                                                   psb[b0 + 1][:, :]))
                rec.edge("vector")
                evac_at[st % 2] = rec.c("e_vector")
                rec.join("scalar", "vector")
                E_("scalar", lambda e, vb=vb, st=st: e.dma_start(
                    out=v_spill[:, st, :], in_=vb), inc=("dA", 16))

            phase()
            qkv_fm_phase(wqkvT_a, 0, None, q_spill)
            phase()
            qkv_fm_phase(wqkvT_a, H, A_fm, None)
            phase()

            phase()
            scores = B[:, 0:8192]
            PT = B[:, 8192:16384].rearrange("p (k q) -> p k q", k=ST)
            qbuf = D[:, 0:8192]
            vstream = C[:, 0:4096]
            aout = C[:, 4096:8192]
            qload_at = [0, 0]
            rec.join("sync", "vector", "tensor")
            for qb in range(2):
                qbv = qbuf[:, (qb % 2) * 4096:(qb % 2) * 4096 + 4096]
                E_("sync", lambda e, qb=qb, qbv=qbv: e.dma_start(
                    out=qbv.rearrange("p (c s) -> p c s", c=HC),
                    in_=q_spill[:, :, qb * 512:(qb + 1) * 512]), inc=("dQ", 16))
                qload_at[qb % 2] = rec.c("dQ")
            sc_evac = [0, 0, 0, 0]
            pt_evac = [0, 0]
            pv_evac = [0, 0]
            for qb in range(4):
                qbv = qbuf[:, (qb % 2) * 4096:(qb % 2) * 4096 + 4096].rearrange(
                    "p (c s) -> p c s", c=HC)
                rec.wait("tensor", "dQ", qload_at[qb % 2])

                def scores_qt(qt, qbv=qbv):
                    for kc in range(4):
                        bank = kc
                        if sc_evac[bank]:
                            rec.wait("tensor", "e_vector", sc_evac[bank])
                        for hc in range(HC):
                            mm(psb[bank][:, :], qbv[:, hc, qt * 128:(qt + 1) * 128],
                               A_fm[:, hc, kc * 512:(kc + 1) * 512],
                               start=(hc == 0), stop=(hc == HC - 1))
                        rec.edge("tensor")
                        rec.join("vector", "tensor")
                        E_("vector", lambda e, qt=qt, kc=kc, bank=bank: e.tensor_copy(
                            scores[:, qt * 2048 + kc * 512:qt * 2048 + kc * 512 + 512],
                            psb[bank][:, :]))
                        rec.edge("vector")
                        sc_evac[bank] = rec.c("e_vector")

                def softmax_qt(qt):
                    sl = scores[:, qt * 2048:(qt + 1) * 2048]
                    E_("vector", lambda e, sl=sl, qt=qt: e.reduce_max(
                        svc("negmax", qt), sl, axis=AX.X, negate=True))
                    rec.edge("vector")
                    E_("vector", lambda e, qt=qt: e.tensor_scalar_mul(
                        svc("negmax", qt), svc("negmax", qt), 0.125))
                    rec.edge("vector")
                    rec.join("scalar", "vector")
                    E_("scalar", lambda e, sl=sl, qt=qt: e.activation(
                        sl, sl, AF.Exp, bias=svc("negmax", qt), scale=0.125,
                        accum_out=svc("ssum", qt)))
                    rec.edge("scalar")
                    rec.join("vector", "scalar")
                    E_("vector", lambda e, qt=qt: e.reciprocal(
                        svc("srec", qt), svc("ssum", qt)))
                    rec.edge("vector")
                    E_("vector", lambda e, sl=sl, qt=qt: e.tensor_scalar_mul(
                        sl, sl, svc("srec", qt)))
                    rec.edge("vector")

                def pt_qt(qt):
                    rec.join("tensor", "vector")
                    for kt4 in range(4):
                        bank = 4 + (kt4 % 2)
                        if pt_evac[kt4 % 2]:
                            rec.wait("tensor", "e_vector", pt_evac[kt4 % 2])
                        for j in range(4):
                            kt = kt4 * 4 + j
                            tr(psb[bank][:, j * 128:(j + 1) * 128],
                               scores[:, qt * 2048 + kt * 128:qt * 2048 + kt * 128 + 128])
                        rec.edge("tensor")
                        rec.join("vector", "tensor")
                        E_("vector", lambda e, kt4=kt4, qt=qt, bank=bank:
                           e.tensor_copy(
                               PT[:, kt4 * 4:kt4 * 4 + 4, qt * 128:(qt + 1) * 128],
                               psb[bank][:, :].rearrange("p (a q) -> p a q", a=4)))
                        rec.edge("vector")
                        pt_evac[kt4 % 2] = rec.c("e_vector")

                scores_qt(0)
                for qt in range(4):
                    if qt + 1 < 4:
                        scores_qt(qt + 1)
                    softmax_qt(qt)
                    pt_qt(qt)
                vload_at = [0, 0]
                rec.join("sync", "tensor")
                for vfc in range(2):
                    vsb = vstream[:, (vfc % 2) * 2048:(vfc % 2) * 2048 + 2048]
                    E_("sync", lambda e, vsb=vsb, vfc=vfc: e.dma_start(
                        out=vsb.rearrange("p (k f) -> p k f", k=ST),
                        in_=v_spill[:, :, vfc * 128:(vfc + 1) * 128]), inc=("dV", 16))
                    vload_at[vfc % 2] = rec.c("dV")
                rec.join("tensor", "vector")
                rec.wait("vector", "dA")
                for vfc in range(HC):
                    bank = 6 + (vfc % 2)
                    vsb = vstream[:, (vfc % 2) * 2048:(vfc % 2) * 2048 + 2048].rearrange(
                        "p (k f) -> p k f", k=ST)
                    rec.wait("tensor", "dV", vload_at[vfc % 2])
                    if pv_evac[vfc % 2]:
                        rec.wait("tensor", "e_vector", pv_evac[vfc % 2])
                    for kt in range(ST):
                        mm(psb[bank][:, :], vsb[:, kt, :], PT[:, kt, :],
                           start=(kt == 0), stop=(kt == ST - 1))
                    rec.edge("tensor")
                    mm_done = rec.c("e_tensor")
                    if vfc + 2 < HC:
                        vfc2 = vfc + 2
                        vsb2 = vstream[:, (vfc2 % 2) * 2048:(vfc2 % 2) * 2048 + 2048]
                        rec.wait("sync", "e_tensor", mm_done)
                        E_("sync", lambda e, vsb2=vsb2, vfc2=vfc2: e.dma_start(
                            out=vsb2.rearrange("p (k f) -> p k f", k=ST),
                            in_=v_spill[:, :, vfc2 * 128:(vfc2 + 1) * 128]),
                           inc=("dV", 16))
                        vload_at[vfc2 % 2] = rec.c("dV")
                    rec.join("vector", "tensor")
                    E_("vector", lambda e, vfc=vfc, bank=bank: e.tensor_copy(
                        aout[:, vfc * 512:(vfc + 1) * 512], psb[bank][:, :]))
                    rec.edge("vector")
                    pv_evac[vfc % 2] = rec.c("e_vector")
                rec.join("scalar", "vector")
                E_("scalar", lambda e, qb=qb: e.dma_start(
                    out=a_spill[:, :, qb * 512:(qb + 1) * 512],
                    in_=aout.rearrange("p (c s) -> p c s", c=HC)), inc=("dA", 16))
                if qb + 2 < 4:
                    qb2 = qb + 2
                    qbv2 = qbuf[:, (qb2 % 2) * 4096:(qb2 % 2) * 4096 + 4096]
                    rec.wait("sync", "e_tensor")
                    E_("sync", lambda e, qb2=qb2, qbv2=qbv2: e.dma_start(
                        out=qbv2.rearrange("p (c s) -> p c s", c=HC),
                        in_=q_spill[:, :, qb2 * 512:(qb2 + 1) * 512]), inc=("dQ", 16))
                    qload_at[qb2 % 2] = rec.c("dQ")

            phase()
            wp = D[:, 0:8192].rearrange("p (c f) -> p c f", c=HC)
            rec.join("sync", "tensor", "vector")
            for jc in range(HC):
                E_("sync", lambda e, jc=jc: e.dma_start(
                    out=wp[:, jc, :], in_=wprojT_a[jc * 128:(jc + 1) * 128, :]),
                   inc=("dW", 16))
            load_bias_cols(bproj_a, 0)
            rec.wait("vector", "dW")
            astream = C[:, 0:4096]
            rstream = C[:, 4096:8192]
            out2 = B[:, 0:2048].rearrange("p (c i) -> p c i", c=HC)
            tr_evac = [0, 0, 0, 0]
            pj_evac = [0, 0, 0, 0]
            rec.wait("sync", "dA", rspill_at)
            for c8 in range(HC):
                asb = astream[:, (c8 % 2) * 2048:(c8 % 2) * 2048 + 2048]
                rsb = rstream[:, (c8 % 2) * 2048:(c8 % 2) * 2048 + 2048]
                if c8 >= 2:
                    rec.wait("sync", "e_tensor")
                    rec.wait("sync", "e_vector")
                E_("sync", lambda e, asb=asb, c8=c8: e.dma_start(
                    out=asb, in_=a_spill[:, c8, :]), inc=("dQ", 16))
                E_("sync", lambda e, rsb=rsb, c8=c8: e.dma_start(
                    out=rsb.rearrange("p (c i) -> p c i", c=HC),
                    in_=r_spill[:, :, c8 * 256:(c8 + 1) * 256]), inc=("dV", 16))
                aload = rec.c("dQ")
                rload2 = rec.c("dV")
                rec.wait("tensor", "dQ", aload)
                rec.wait("vector", "dV", rload2)
                rec.wait("tensor", "dW")
                for m in range(2):
                    for jb in range(HC):
                        bank = (m * 8 + jb) % 4
                        if tr_evac[bank]:
                            rec.wait("tensor", "e_vector", tr_evac[bank])
                        tr(psb[bank][:, 0:128],
                           asb[:, m * 1024 + jb * 128:m * 1024 + jb * 128 + 128])
                        rec.edge("tensor")
                        rec.join("vector", "tensor")
                        E_("vector", lambda e, m=m, jb=jb, bank=bank: e.tensor_copy(
                            out2[:, jb, m:256:2], psb[bank][:, 0:128]))
                        rec.edge("vector")
                        tr_evac[bank] = rec.c("e_vector")
                rsv = rsb.rearrange("p (c i) -> p c i", c=HC)
                rec.join("tensor", "vector")
                for ec in range(HC):
                    bank = 4 + (ec % 4)
                    if pj_evac[ec % 4]:
                        rec.wait("tensor", "e_vector", pj_evac[ec % 4])
                    for jc in range(HC):
                        mm(psb[bank][:, 0:256], wp[:, jc, ec * 128:(ec + 1) * 128],
                           out2[:, jc, :], start=(jc == 0), stop=(jc == HC - 1))
                    rec.edge("tensor")
                    rec.join("vector", "tensor")
                    E_("vector", lambda e, ec=ec, bank=bank, rsv=rsv, c8=c8:
                       e.scalar_tensor_tensor(
                           A_fm[:, ec, c8 * 256:(c8 + 1) * 256], psb[bank][:, 0:256],
                           wvec[:, ec:ec + 1], rsv[:, ec, :], ALU.add, ALU.add))
                    rec.edge("vector")
                    pj_evac[ec % 4] = rec.c("e_vector")

        def ffn(ci):
            SEC = FFN_SECTIONS
            rec.join("scalar", "vector", "tensor")
            E_("scalar", lambda e: e.dma_start(out=r_spill, in_=A_fm), inc=("dA", 16))
            rspill_at = rec.c("dA")
            layernorm(A_fm, B_fm, 1e-5)
            if "router" not in SEC:
                return
            rec.join("tensor", "vector")
            rt_evac = [0] * 8
            for tt in range(ST):
                bank = tt % 8
                if rt_evac[bank]:
                    rec.wait("tensor", "e_vector", rt_evac[bank])
                for hc in range(HC):
                    mm(psb[bank][:, 0:8], B_fm[:, hc, tt * 128:(tt + 1) * 128],
                       wrt[:, hc * 8:hc * 8 + 8], start=(hc == 0), stop=(hc == HC - 1))
                rec.edge("tensor")
                rec.join("vector", "tensor")
                E_("vector", lambda e, bank=bank: e.tensor_copy(sv("z8", 8),
                                                                psb[bank][:, 0:8]))
                rec.edge("vector")
                rt_evac[bank] = rec.c("e_vector")
                E_("vector", lambda e: e.reduce_max(svc("nm1", 0, 8), sv("z8", 8),
                                                    axis=AX.X, negate=True))
                rec.edge("vector")
                E_("vector", lambda e: e.tensor_single_scalar(
                    sv("zs", 8), sv("z8", 8), svc("nm1", 0, 8), ALU.add))
                rec.edge("vector")
                E_("vector", lambda e: e.tensor_single_scalar(
                    sv("oh", 8), sv("zs", 8), 0.0, ALU.is_equal))
                rec.edge("vector")
                E_("vector", lambda e: e.scalar_tensor_tensor(
                    sv("ti", 8), sv("oh", 8), -1000.0, iot8[:, :], ALU.mult, ALU.add))
                rec.edge("vector")
                E_("vector", lambda e: e.tensor_reduce(
                    svc("e1r", 0, 8), sv("ti", 8), axis=AX.X, op=ALU.min))
                rec.edge("vector")
                E_("vector", lambda e, tt=tt: e.tensor_scalar_add(
                    svc("e1", tt), svc("e1r", 0, 8), 1000.0))
                rec.edge("vector")
                E_("vector", lambda e, tt=tt: e.tensor_single_scalar(
                    sv_oh1(tt), iot8[:, :], svc("e1", tt), ALU.is_equal))
                rec.edge("vector")
                rec.join("scalar", "vector")
                E_("scalar", lambda e: e.activation(
                    sv("ez", 8), sv("zs", 8), AF.Exp, accum_out=svc("esum", 0, 8)))
                rec.edge("scalar")
                rec.join("vector", "scalar")
                E_("vector", lambda e, tt=tt: e.reciprocal(svc("w1", tt),
                                                           svc("esum", 0, 8)))
                rec.edge("vector")
                E_("vector", lambda e, tt=tt: e.scalar_tensor_tensor(
                    sv("z2", 8), sv_oh1(tt), -1000.0, sv("zs", 8), ALU.mult, ALU.add))
                rec.edge("vector")
                E_("vector", lambda e: e.reduce_max(svc("nm2", 0, 8), sv("z2", 8),
                                                    axis=AX.X, negate=True))
                rec.edge("vector")
                E_("vector", lambda e: e.tensor_single_scalar(
                    sv("z2s", 8), sv("z2", 8), svc("nm2", 0, 8), ALU.add))
                rec.edge("vector")
                E_("vector", lambda e: e.tensor_single_scalar(
                    sv("oh2r", 8), sv("z2s", 8), 0.0, ALU.is_equal))
                rec.edge("vector")
                E_("vector", lambda e: e.scalar_tensor_tensor(
                    sv("ti", 8), sv("oh2r", 8), -1000.0, iot8[:, :], ALU.mult,
                    ALU.add))
                rec.edge("vector")
                E_("vector", lambda e: e.tensor_reduce(
                    svc("e2r", 0, 8), sv("ti", 8), axis=AX.X, op=ALU.min))
                rec.edge("vector")
                E_("vector", lambda e, tt=tt: e.tensor_scalar_add(
                    svc("e2", tt), svc("e2r", 0, 8), 1000.0))
                rec.edge("vector")
                E_("vector", lambda e, tt=tt: e.tensor_single_scalar(
                    sv_oh2(tt), iot8[:, :], svc("e2", tt), ALU.is_equal))
                rec.edge("vector")
                rec.join("scalar", "vector")
                E_("scalar", lambda e, tt=tt: e.activation(
                    svc("w2e", tt), svc("nm2", 0, 8), AF.Exp, scale=-1.0))
                rec.edge("scalar")
                rec.join("vector", "scalar")
                E_("vector", lambda e, tt=tt: e.tensor_tensor(
                    svc("w2", tt), svc("w2e", tt), svc("w1", tt), ALU.mult))
                rec.edge("vector")
                E_("vector", lambda e, tt=tt: e.tensor_tensor(
                    sv("y8", 8), sv_oh1(tt), sv_oh2(tt), ALU.add))
                rec.edge("vector")
                rec.join("tensor", "vector")
                bank2 = 4 + (tt % 2)
                tr(psb[bank2][:8, 0:128], sv("y8", 8))
                rec.edge("tensor")
                rec.join("vector", "tensor")
                E_("vector", lambda e, tt=tt, bank2=bank2: e.tensor_copy(
                    D[0:8, tt * 128:(tt + 1) * 128], psb[bank2][:8, 0:128]))
                rec.edge("vector")

            if "exchange" not in SEC:
                return
            phase()
            y_em = D[0:8, 0:2048]
            P_incl = D[0:8, 2048:4096]
            E_("vector", lambda e: e.tensor_tensor_scan(
                P_incl, y_em, y_em, 0.0, ALU.add, ALU.bypass))
            rec.edge("vector")
            E_("vector", lambda e: e.tensor_tensor(y_em, P_incl, y_em, ALU.subtract))
            rec.edge("vector")
            rec.wait("vector", "lsem")
            rec.wait("vector", "lsem2")
            E_("vector", lambda e: e.tensor_copy(cntt[0:8, 0:1],
                                                 P_incl[:, 2047:2048]))
            rec.edge("vector")
            rec.join("gpsimd", "vector")
            rec.wait("gpsimd", "dW")
            tb = tblt[ci]
            rs = "rsem" if ci == 0 else "rsem2"
            ls = "lsem" if ci == 0 else "lsem2"
            E_("gpsimd", lambda e, rs=rs, ls=ls: e.remote_dma_broadcast(
                dataclasses.replace(tb[:, 0:8], offset=rankreg[0] * 8),
                cntt[:, :], remote_sem=sems[rs], local_sem=sems[ls],
                rdests=[(0, k) for k in range(8)]), inc=("prep", 1))
            rec.wait("gpsimd", "prep")
            E_("gpsimd", lambda e: e.trigger_dma(count=1))
            rec.cnt[rs] = rec.c(rs) + 16
            rec.cnt[ls] = rec.c(ls) + 16

            if "zero" not in SEC:
                return
            rec.join("gpsimd", "vector")
            E_("gpsimd", lambda e: e.dma_start(
                out=pairs_d.rearrange("(p n) two -> p n two", p=128),
                in_=zeroK[:, 0:256].rearrange("p (n two) -> p n two", two=2)),
               inc=("dG", 16))
            for k in range(16):
                E_("gpsimd", lambda e, k=k: e.dma_start(
                    out=sY_d[k * 128:(k + 1) * 128, :], in_=zeroK[:, :]),
                   inc=("dG", 16))

            layernorm(B_fm, None, 1e-6, rms_only=True, apply=False)
            rs_slab = C[:, 2048:4096]
            bcast_tm_var(sv("a"), rs_slab)
            rec.join("vector", "tensor")
            rec.wait("vector", "dA", rspill_at)
            for hc in range(HC):
                E_("vector", lambda e, hc=hc: e.tensor_tensor(
                    A_fm[:, hc, :], B_fm[:, hc, :], rs_slab, ALU.mult))
            rec.edge("vector")

            phase()
            fm_to_tm_spill(B_fm, xtm_spill, (0, 1))
            phase()
            fm_to_tm_spill(A_fm, xr_spill, (2, 3))
            phase()

            phase()
            if "pos" not in SEC:
                return
            for tt in range(ST):
                bank = tt % 2
                rec.join("tensor", "vector")
                tr(psb[bank][:, 0:8], D[0:8, tt * 128:(tt + 1) * 128])
                rec.edge("tensor")
                rec.join("vector", "tensor")
                E_("vector", lambda e, bank=bank: e.tensor_copy(sv("ptm", 8),
                                                                psb[bank][:, 0:8]))
                rec.edge("vector")
                for k, ohf in ((1, sv_oh1), (2, sv_oh2)):
                    E_("vector", lambda e, ohf=ohf, tt=tt: e.tensor_tensor(
                        sv("t8", 8), sv("ptm", 8), ohf(tt), ALU.mult))
                    rec.edge("vector")
                    E_("vector", lambda e, k=k, tt=tt: e.reduce_sum(
                        svc("pos%d" % k, tt), sv("t8", 8), axis=AX.X))
                    rec.edge("vector")
                    ev = svc("e1", tt) if k == 1 else svc("e2", tt)
                    E_("vector", lambda e, ev=ev, k=k, tt=tt: e.scalar_tensor_tensor(
                        sv("slot", 8)[:, 0:1], ev, float(S), svc("pos%d" % k, tt),
                        ALU.mult, ALU.add))
                    rec.edge("vector")
                    E_("vector", lambda e: e.tensor_copy(idxt[:, 0:1],
                                                         sv("slot", 8)[:, 0:1]))
                    rec.edge("vector")
                    E_("vector", lambda e, tt=tt: e.tensor_copy(
                        sv("pv", 8)[:, 0:1], svc("sidxf", tt)))
                    E_("vector", lambda e, k=k, tt=tt: e.tensor_copy(
                        sv("pv", 8)[:, 1:2], svc("w%d" % k, tt)))
                    rec.edge("vector")
                    rec.join("gpsimd", "vector")
                    rec.wait("gpsimd", "dG")
                    E_("gpsimd", lambda e: e.indirect_dma_start(
                        pairs_d, IndirectOffsetOnAxis(idxt[:, 0:1], axis=0),
                        sv("pv", 8)[:, 0:2], None), inc=("dG", 16))
                    rec.wait("vector", "dG")

            if "experts" not in SEC:
                return
            phase()
            Xe = B[:, 0:6144].rearrange("p (c r) -> p c r", c=HC)
            Ytm = B[:, 8192:9216]
            gbuf = B[:, 9216:11264]
            wgv = lambda b: C[:, b * 4096:b * 4096 + 2048].rearrange(
                "p (c f) -> p c f", c=HC)
            wuv = lambda b: C[:, b * 4096 + 2048:b * 4096 + 4096].rearrange(
                "p (c f) -> p c f", c=HC)
            wdv = lambda b: D[:, b * 2048:b * 2048 + 2048].rearrange(
                "p (f h) -> p f h", f=2)
            scat_prev = 0
            for ex in range(E):
                rec.join("sync", "vector")
                E_("sync", lambda e, ex=ex: e.dma_start(
                    out=wvec[:, 8:16], in_=normw_in[ex]), inc=("dW", 16))
                nw_at = rec.c("dW")
                E_("sync", lambda e, ex=ex: e.dma_start(
                    out=sv("pairv", 16)[:, 0:12].rearrange("p (b two) -> p b two",
                                                           two=2),
                    in_=pairs_d[ex * S:ex * S + PAD, :].rearrange(
                        "(b p) two -> p b two", p=128)), inc=("dW", 16))
                pv_at = rec.c("dW")
                rec.wait("vector", "dW", pv_at)
                rec.wait("vector", "dG")
                for rt in range(PAD // 128):
                    E_("vector", lambda e, rt=rt: e.tensor_copy(
                        idxt[:, 8 + rt:9 + rt], sv("pairv", 16)[:, 2 * rt:2 * rt + 1]))
                rec.edge("vector")
                idx_at = rec.c("e_vector")
                for rt in range(PAD // 128):
                    gb = gbuf[:, (rt % 2) * 1024:(rt % 2) * 1024 + 1024]
                    rec.join("gpsimd", "tensor")
                    rec.wait("gpsimd", "e_vector", idx_at)
                    E_("gpsimd", lambda e, gb=gb, rt=rt: e.indirect_dma_start(
                        gb, None, xr_spill,
                        IndirectOffsetOnAxis(idxt[:, 8 + rt:9 + rt], axis=0)),
                       inc=("dG", 16))
                    g_at = rec.c("dG")
                    rec.wait("tensor", "dG", g_at)
                    rec.join("tensor", "vector")
                    for hc in range(HC):
                        bank = 4 + (hc % 2)
                        tr(psb[bank][:, (hc // 2 % 2) * 128:(hc // 2 % 2) * 128 + 128],
                           gb[:, hc * 128:(hc + 1) * 128])
                        if hc % 4 == 3:
                            rec.edge("tensor")
                            rec.join("vector", "tensor")
                            rec.wait("vector", "dW", nw_at)
                            for hh in range(hc - 3, hc + 1):
                                bank2 = 4 + (hh % 2)
                                E_("vector", lambda e, hh=hh, bank2=bank2, rt=rt, ex=ex:
                                   e.tensor_scalar_mul(
                                       Xe[:, hh, rt * 128:(rt + 1) * 128],
                                       psb[bank2][:, (hh // 2 % 2) * 128:
                                                  (hh // 2 % 2) * 128 + 128],
                                       wvec[:, 8 + hh:9 + hh]))
                            rec.edge("vector")
                            rec.join("tensor", "vector")
                for rc in range(3):
                    r0 = rc * RC2
                    nrt = 2

                    def load_fc(fc):
                        bb = fc % 2
                        rec.wait("sync", "e_tensor")
                        E_("sync", lambda e, ex=ex, fc=fc, bb=bb: e.dma_start(
                            out=wgv(bb), in_=wg_in[ex][:, fc * 256:(fc + 1) * 256]
                            .rearrange("(c p) f -> p c f", p=128)), inc=("dWg", 16))
                        E_("sync", lambda e, ex=ex, fc=fc, bb=bb: e.dma_start(
                            out=wuv(bb), in_=wu_in[ex][:, fc * 256:(fc + 1) * 256]
                            .rearrange("(c p) f -> p c f", p=128)), inc=("dWu", 16))
                        E_("sync", lambda e, ex=ex, fc=fc, bb=bb: e.dma_start(
                            out=wdv(bb), in_=wd_in[ex][fc * 256:(fc + 1) * 256, :]
                            .rearrange("(f p) h -> p f h", p=128)), inc=("dWd", 16))
                        wload_at[bb] = (rec.c("dWg"), rec.c("dWu"), rec.c("dWd"))

                    wload_at = [None, None]
                    load_fc(0)
                    load_fc(1)
                    gmark = [0] * 16
                    umark = [0] * 16
                    gumark = [0] * 16
                    wdmark = [0] * 16

                    def gbank(fgl):
                        return 4 + 2 * (fgl % 2)

                    def ubank(fgl):
                        return 5 + 2 * (fgl % 2)

                    def emit_wd(fgl, bb):
                        rec.wait("tensor", "e_vector", gumark[fgl])
                        for rl in range(nrt):
                            for hh in range(2):
                                mm(psb[rl * 2 + hh][:, :],
                                   sv_gu(fgl % 2)[:, rl * 128:(rl + 1) * 128],
                                   wdv(bb)[:, fgl % 2, hh * 512:(hh + 1) * 512],
                                   start=(fgl == 0), stop=(fgl == 15))
                        rec.edge("tensor")
                        wdmark[fgl] = rec.c("e_tensor")

                    for fgl in range(16):
                        fc, ft = fgl // 2, fgl % 2
                        bb = fc % 2
                        if ft == 0:
                            rec.wait("tensor", "dWg", wload_at[bb][0])
                            rec.wait("tensor", "dWu", wload_at[bb][1])
                            rec.wait("tensor", "dWd", wload_at[bb][2])
                        if fgl >= 2:
                            rec.wait("tensor", "e_scalar", gmark[fgl - 2])
                        for hc in range(HC):
                            mm(psb[gbank(fgl)][:, 0:RC2],
                               wgv(bb)[:, hc, ft * 128:(ft + 1) * 128],
                               Xe[:, hc, r0:r0 + RC2],
                               start=(hc == 0), stop=(hc == HC - 1))
                        rec.edge("tensor")
                        gmm = rec.c("e_tensor")
                        if fgl >= 2:
                            rec.wait("tensor", "e_vector", gumark[fgl - 2])
                        for hc in range(HC):
                            mm(psb[ubank(fgl)][:, 0:RC2],
                               wuv(bb)[:, hc, ft * 128:(ft + 1) * 128],
                               Xe[:, hc, r0:r0 + RC2],
                               start=(hc == 0), stop=(hc == HC - 1))
                        rec.edge("tensor")
                        umm = rec.c("e_tensor")
                        rec.wait("scalar", "e_tensor", gmm)
                        if fgl >= 2:
                            rec.wait("scalar", "e_vector", gumark[fgl - 2])
                        E_("scalar", lambda e, fgl=fgl: e.activation(
                            sv_gg(fgl % 2), psb[gbank(fgl)][:, 0:RC2], AF.Gelu))
                        rec.edge("scalar")
                        gmark[fgl] = rec.c("e_scalar")
                        rec.wait("vector", "e_tensor", umm)
                        rec.wait("vector", "e_scalar", gmark[fgl])
                        if fgl >= 2:
                            rec.wait("vector", "e_tensor", wdmark[fgl - 2])
                        E_("vector", lambda e, fgl=fgl: e.tensor_tensor(
                            sv_gu(fgl % 2), sv_gg(fgl % 2), psb[ubank(fgl)][:, 0:RC2],
                            ALU.mult))
                        rec.edge("vector")
                        gumark[fgl] = rec.c("e_vector")
                        if fgl >= 1:
                            emit_wd(fgl - 1, (fgl - 1) // 2 % 2)
                            if (fgl - 1) % 2 == 1 and (fgl - 1) // 2 + 2 < 8:
                                load_fc((fgl - 1) // 2 + 2)
                    emit_wd(15, 1)
                    for rl in range(2):
                        rt = rc * 2 + rl
                        rec.join("vector", "tensor")
                        rec.wait("vector", "dG")
                        for hh in range(2):
                            E_("vector", lambda e, rl=rl, hh=hh, rt=rt:
                               e.tensor_scalar_mul(
                                   Ytm[:, hh * 512:(hh + 1) * 512],
                                   psb[rl * 2 + hh][:, :],
                                   sv("pairv", 16)[:, 2 * rt + 1:2 * rt + 2]))
                        rec.edge("vector")
                        rec.join("gpsimd", "vector")
                        if scat_prev:
                            rec.wait("gpsimd", "dG", scat_prev)
                        E_("gpsimd", lambda e, rt=rt: e.indirect_dma_start(
                            sY_d, IndirectOffsetOnAxis(idxt[:, 8 + rt:9 + rt],
                                                       axis=0),
                            Ytm[:, :], None, compute_op=ALU.add), inc=("dG", 16))
                        rec.join("tensor", "vector")
                scat_prev = rec.c("dG")

            rec.wait("gpsimd", rs)
            rec.wait("gpsimd", ls)
            rec.wait("vector", rs)
            E_("vector", lambda e, tb=tb: e.tensor_tensor(sv("mtbl", 64), tb[:, :],
                                                          maskt[:, :], ALU.mult))
            rec.edge("vector")
            E_("vector", lambda e: e.reduce_sum(svc("offsE", 0, 8), sv("mtbl", 64),
                                                axis=AX.X))
            rec.edge("vector")
            rec.join("tensor", "vector")
            tr(psb[6][:1, 0:128], sv("offsE", 8)[:, 0:1])
            rec.edge("tensor")
            rec.join("vector", "tensor")
            E_("vector", lambda e: e.tensor_copy(stage16[0:1, 0:128],
                                                 psb[6][:1, 0:128]))
            rec.edge("vector")
            rec.join("tensor", "vector")
            mm(psb[7][:, 0:8], onesr[:1, :], stage16[0:1, 0:8], start=True, stop=True)
            rec.edge("tensor")
            rec.join("vector", "tensor")
            E_("vector", lambda e: e.tensor_copy(sv("offsB", 8), psb[7][:, 0:8]))
            rec.edge("vector")
            for tt in range(ST):
                for k, ohf in ((1, sv_oh1), (2, sv_oh2)):
                    E_("vector", lambda e, ohf=ohf, tt=tt: e.tensor_tensor(
                        sv("t8", 8), sv("offsB", 8), ohf(tt), ALU.mult))
                    rec.edge("vector")
                    E_("vector", lambda e, k=k: e.reduce_sum(
                        svc("od%d" % k, 0, 8), sv("t8", 8), axis=AX.X))
                    rec.edge("vector")
                    E_("vector", lambda e, k=k, tt=tt: e.tensor_tensor(
                        svc("gp%d" % k, 0), svc("pos%d" % k, tt),
                        svc("od%d" % k, 0, 8), ALU.add))
                    rec.edge("vector")
                    E_("vector", lambda e, k=k: e.tensor_single_scalar(
                        svc("keep%d" % k, 0), svc("gp%d" % k, 0), float(CAP),
                        ALU.is_lt))
                    rec.edge("vector")
                E_("vector", lambda e, tt=tt: e.tensor_tensor(
                    svc("kk", tt), svc("keep1", 0), svc("keep2", 0), ALU.mult))
                rec.edge("vector")
                E_("vector", lambda e, tt=tt: e.tensor_tensor(
                    svc("wsum", 0), svc("w1", tt), svc("w2", tt), ALU.add))
                rec.edge("vector")
                E_("vector", lambda e, tt=tt: e.tensor_tensor(
                    svc("t1", 0), svc("kk", tt), svc("wsum", 0), ALU.mult))
                rec.edge("vector")
                E_("vector", lambda e, tt=tt: e.tensor_scalar_add(
                    svc("Cc", tt), svc("t1", 0), 1.0))
                rec.edge("vector")
                E_("vector", lambda e, tt=tt: e.tensor_scalar(
                    svc("s1", tt), svc("kk", tt), -1.0, 1.0,
                    op0=ALU.mult, op1=ALU.add))
                rec.edge("vector")

            if "fallback" not in SEC:
                return
            phase()
            fwgv = C[:, 0:4096].rearrange("p (c f) -> p c f", c=HC)
            fwuv = C[:, 4096:8192].rearrange("p (c f) -> p c f", c=HC)
            fwdv = D[:, 0:4096].rearrange("p (f h) -> p f h", f=4)
            rec.join("sync", "tensor", "vector", "scalar")
            E_("sync", lambda e: e.dma_start(
                out=fwgv, in_=fwg_in.rearrange("(c p) f -> p c f", p=128)),
               inc=("dW", 16))
            E_("sync", lambda e: e.dma_start(
                out=fwuv, in_=fwu_in.rearrange("(c p) f -> p c f", p=128)),
               inc=("dW", 16))
            E_("sync", lambda e: e.dma_start(
                out=fwdv, in_=fwd_in.rearrange("(f p) h -> p f h", p=128)),
               inc=("dW", 16))
            E_("sync", lambda e: e.dma_start(
                out=wvec[:, 16:24], in_=fnorm_in), inc=("dW", 16))
            rec.wait("vector", "dW")
            for hc in range(HC):
                E_("vector", lambda e, hc=hc: e.tensor_scalar_mul(
                    fwgv[:, hc, :], fwgv[:, hc, :], wvec[:, 16 + hc:17 + hc]))
                E_("vector", lambda e, hc=hc: e.tensor_scalar_mul(
                    fwuv[:, hc, :], fwuv[:, hc, :], wvec[:, 16 + hc:17 + hc]))
            rec.edge("vector")
            GU4 = B[:, 0:2048].rearrange("p (f s) -> p f s", f=4)
            xs_buf = B[:, 8192:12288]
            rstr = B[:, 2048:4096]
            fb_sb = B[:, 12288:16384].rearrange("p (c s) -> p c s", c=HC)[:, :, 0:512]
            for qc in range(4):
                rec.join("tensor", "vector", "scalar")
                for ft4 in range(4):
                    bank = ft4 % 4
                    for hc in range(HC):
                        mm(psb[bank][:, :], fwgv[:, hc, ft4 * 128:(ft4 + 1) * 128],
                           A_fm[:, hc, qc * 512:(qc + 1) * 512],
                           start=(hc == 0), stop=(hc == HC - 1))
                    rec.edge("tensor")
                    rec.join("scalar", "tensor")
                    E_("scalar", lambda e, ft4=ft4, bank=bank: e.activation(
                        GU4[:, ft4, :], psb[bank][:, :], AF.Gelu))
                    rec.edge("scalar")
                rec.join("tensor", "scalar")
                for ft4 in range(4):
                    bank = ft4 % 4
                    for hc in range(HC):
                        mm(psb[bank][:, :], fwuv[:, hc, ft4 * 128:(ft4 + 1) * 128],
                           A_fm[:, hc, qc * 512:(qc + 1) * 512],
                           start=(hc == 0), stop=(hc == HC - 1))
                    rec.edge("tensor")
                    rec.join("vector", "tensor")
                    E_("vector", lambda e, ft4=ft4, bank=bank: e.tensor_tensor(
                        GU4[:, ft4, :], GU4[:, ft4, :], psb[bank][:, :], ALU.mult))
                    rec.edge("vector")
                rec.join("tensor", "vector")
                for half in range(2):
                    for hh in range(4):
                        hc = half * 4 + hh
                        bank = 4 + hh
                        for ft4 in range(4):
                            mm(psb[bank][:, :], fwdv[:, ft4, hc * 128:(hc + 1) * 128],
                               GU4[:, ft4, :], start=(ft4 == 0), stop=(ft4 == 3))
                    rec.edge("tensor")
                    rec.join("vector", "tensor")
                    for hh in range(4):
                        hc = half * 4 + hh
                        E_("vector", lambda e, hc=hc, hh=hh: e.tensor_copy(
                            fb_sb[:, hc, :], psb[4 + hh][:, :]))
                    rec.edge("vector")
                    rec.join("tensor", "vector")
                for j in range(4):
                    tt = qc * 4 + j
                    xb = xs_buf[:, (j % 2) * 2048:(j % 2) * 2048 + 1024]
                    syb = xs_buf[:, (j % 2) * 2048 + 1024:(j % 2) * 2048 + 2048]
                    rec.join("scalar", "vector", "gpsimd")
                    rec.wait("scalar", "dG")
                    E_("scalar", lambda e, xb=xb, tt=tt: e.dma_start(
                        out=xb, in_=xtm_spill.rearrange("(p b) h -> p b h",
                                                        p=128)[:, tt, :]),
                       inc=("dA", 16))
                    E_("scalar", lambda e, syb=syb, tt=tt: e.dma_start(
                        out=syb, in_=sY_d.rearrange("(p b) h -> p b h",
                                                    p=128)[:, tt, :]),
                       inc=("dA", 16))
                    xld = rec.c("dA")
                    rec.join("tensor", "vector")
                    for hc in range(HC):
                        bank = hc % 2
                        tr(psb[bank][:, (hc // 2 % 2) * 128:(hc // 2 % 2) * 128 + 128],
                           fb_sb[:, hc, j * 128:(j + 1) * 128])
                        if hc % 4 == 3:
                            rec.edge("tensor")
                            rec.join("vector", "tensor")
                            for hh in range(hc - 3, hc + 1):
                                E_("vector", lambda e, hh=hh: e.tensor_copy(
                                    sv_fbtm()[:, hh * 128:(hh + 1) * 128],
                                    psb[hh % 2][:, (hh // 2 % 2) * 128:
                                                (hh // 2 % 2) * 128 + 128]))
                            rec.edge("vector")
                            rec.join("tensor", "vector")
                    rec.wait("vector", "dA", xld)
                    rec.wait("vector", "dG")
                    E_("vector", lambda e, xb=xb, tt=tt: e.tensor_scalar_mul(
                        sv_cmb(), xb, svc("Cc", tt)))
                    rec.edge("vector")
                    E_("vector", lambda e, syb=syb, tt=tt: e.tensor_scalar_mul(
                        sv_cm2(), syb, svc("kk", tt)))
                    rec.edge("vector")
                    E_("vector", lambda e: e.tensor_tensor(
                        sv_cmb(), sv_cmb(), sv_cm2(), ALU.add))
                    rec.edge("vector")
                    E_("vector", lambda e, tt=tt: e.tensor_scalar_mul(
                        sv_cm2(), sv_fbtm(), svc("s1", tt)))
                    rec.edge("vector")
                    E_("vector", lambda e: e.tensor_tensor(
                        sv_cmb(), sv_cmb(), sv_cm2(), ALU.add))
                    rec.edge("vector")
                    rec.join("sync", "vector")
                    rsb = rstr[:, (j % 2) * 1024:(j % 2) * 1024 + 1024]
                    E_("sync", lambda e, rsb=rsb, tt=tt: e.dma_start(
                        out=rsb.rearrange("p (c s) -> p c s", c=HC),
                        in_=r_spill[:, :, tt * 128:(tt + 1) * 128]), inc=("dV", 16))
                    rld = rec.c("dV")
                    rec.join("tensor", "vector")
                    for hc in range(HC):
                        bank = 2 + (hc % 2)
                        tr(psb[bank][:, (hc // 2 % 2) * 128:(hc // 2 % 2) * 128 + 128],
                           sv_cmb()[:, hc * 128:(hc + 1) * 128])
                        if hc % 4 == 3:
                            rec.edge("tensor")
                            rec.join("vector", "tensor")
                            rec.wait("vector", "dV", rld)
                            for hh in range(hc - 3, hc + 1):
                                E_("vector", lambda e, hh=hh, rsb=rsb, tt=tt:
                                   e.tensor_tensor(
                                       A_fm[:, hh, tt * 128:(tt + 1) * 128],
                                       psb[2 + hh % 2][:, (hh // 2 % 2) * 128:
                                                       (hh // 2 % 2) * 128 + 128],
                                       rsb.rearrange("p (c s) -> p c s",
                                                     c=HC)[:, hh, :], ALU.add))
                            rec.edge("vector")
                            rec.join("tensor", "vector")

        def attn_b():
            rec.join("scalar", "vector", "tensor")
            E_("scalar", lambda e: e.dma_start(out=r_spill, in_=A_fm), inc=("dA", 16))
            rspill_at = rec.c("dA")
            layernorm(A_fm, B_fm, 1e-5)
            rec.join("sync", "vector")
            load_bias_cols(bqkv_b, 32, 24)
            rec.wait("vector", "dA", rspill_at)
            phase()
            qkv_fm_phase(wqkvT_b, 2 * H, A_fm, None, bias_src=True, bias_cols=48)
            phase()
            fm_to_tm_spill(A_fm, None, (0, 1), tile_ap=lambda tt: v_spill[:, tt, :])
            phase()
            qkv_fm_phase(wqkvT_b, 0, None, q_spill, bias_src=True, bias_cols=32)
            phase()
            qkv_fm_phase(wqkvT_b, H, A_fm, None, bias_src=True, bias_cols=40)
            phase()

            PTa = B[:, 0:8192].rearrange("p (k q) -> p k q", k=ST)
            PTb = B[:, 8192:16384].rearrange("p (k q) -> p k q", k=ST)
            qbuf = D[:, 0:8192]
            vstream = C[:, 0:4096]
            sco = [C[:, 4096:6144], C[:, 6144:8192]]
            qload_at = [0, 0]
            for qb in range(2):
                qbv = qbuf[:, (qb % 2) * 4096:(qb % 2) * 4096 + 4096]
                rec.join("sync", "tensor")
                E_("sync", lambda e, qb=qb, qbv=qbv: e.dma_start(
                    out=qbv.rearrange("p (c s) -> p c s", c=HC),
                    in_=q_spill[:, :, qb * 512:(qb + 1) * 512]), inc=("dQ", 16))
                qload_at[qb % 2] = rec.c("dQ")
            sc_evac = [0, 0, 0, 0]
            pt_evac = [0] * 8
            pt_done = [0, 0]
            pv_evac = [0, 0]
            ao_dma = [0]
            for qb in range(4):
                qbv = qbuf[:, (qb % 2) * 4096:(qb % 2) * 4096 + 4096].rearrange(
                    "p (c s) -> p c s", c=HC)
                rec.wait("tensor", "dQ", qload_at[qb % 2])
                vload_at = [0, 0]
                rec.join("sync", "tensor")
                for hc in range(2):
                    vsb = vstream[:, (hc % 2) * 2048:(hc % 2) * 2048 + 2048]
                    E_("sync", lambda e, vsb=vsb, hc=hc: e.dma_start(
                        out=vsb.rearrange("p (k f) -> p k f", k=ST),
                        in_=v_spill[:, :, hc * 128:(hc + 1) * 128]), inc=("dV", 16))
                    vload_at[hc % 2] = rec.c("dV")
                for hc in range(HC):
                    def scoresb_qt(qt, hc=hc, qbv=qbv):
                        for h2 in range(2):
                            p0 = h2 * 64
                            for kc in range(4):
                                bank = h2 * 2 + kc % 2
                                if sc_evac[bank]:
                                    rec.wait("tensor", "e_scalar", sc_evac[bank])
                                mm2 = lambda e, hc=hc, qt=qt, kc=kc, p0=p0, bank=bank, \
                                    qbv=qbv: \
                                    e.matmul(psb[bank][:, :],
                                             qbv[p0:p0 + 64, hc, qt * 128:(qt + 1) * 128],
                                             A_fm[p0:p0 + 64, hc, kc * 512:(kc + 1) * 512],
                                             start=True, stop=True,
                                             tile_position=(p0, 0))
                                rec.op("tensor", mm2)
                                rec.edge("tensor")
                                rec.join("scalar", "tensor")
                                rec.wait("scalar", "dA", ao_dma[0])
                                if pt_done[h2]:
                                    rec.wait("scalar", "e_tensor", pt_done[h2])
                                E_("scalar", lambda e, h2=h2, qt=qt, kc=kc, bank=bank:
                                   e.copy(sco[h2][:, kc * 512:(kc + 1) * 512],
                                          psb[bank][:, :]))
                                rec.edge("scalar")
                                sc_evac[bank] = rec.c("e_scalar")

                    def smpt_qt(qt, hc=hc):
                        for h2 in range(2):
                            sl = sco[h2]
                            rec.join("vector", "scalar")
                            E_("vector", lambda e, sl=sl: e.reduce_max(
                                svc("negmax", 0), sl, axis=AX.X, negate=True))
                            rec.edge("vector")
                            E_("vector", lambda e: e.tensor_scalar_mul(
                                svc("negmax", 0), svc("negmax", 0), 0.125))
                            rec.edge("vector")
                            rec.join("scalar", "vector")
                            E_("scalar", lambda e, sl=sl: e.activation(
                                sl, sl, AF.Exp, bias=svc("negmax", 0), scale=0.125,
                                accum_out=svc("ssum", 0)))
                            rec.edge("scalar")
                            rec.join("vector", "scalar")
                            E_("vector", lambda e: e.reciprocal(
                                svc("srec", 0), svc("ssum", 0)))
                            rec.edge("vector")
                            E_("vector", lambda e, sl=sl: e.tensor_scalar_mul(
                                sl, sl, svc("srec", 0)))
                            rec.edge("vector")
                            rec.join("tensor", "vector")
                            PT = PTa if h2 == 0 else PTb
                            for kt4 in range(4):
                                bank = 5 + (kt4 % 2)
                                if pt_evac[bank]:
                                    rec.wait("tensor", "e_vector", pt_evac[bank])
                                for j in range(4):
                                    kt = kt4 * 4 + j
                                    tr(psb[bank][:, j * 128:(j + 1) * 128],
                                       sl[:, kt * 128:(kt + 1) * 128])
                                rec.edge("tensor")
                                rec.join("vector", "tensor")
                                E_("vector", lambda e, PT=PT, kt4=kt4, qt=qt, bank=bank:
                                   e.tensor_copy(
                                       PT[:, kt4 * 4:kt4 * 4 + 4,
                                          qt * 128:(qt + 1) * 128],
                                       psb[bank][:, :].rearrange(
                                           "p (a q) -> p a q", a=4)))
                                rec.edge("vector")
                                pt_evac[bank] = rec.c("e_vector")
                            pt_done[h2] = rec.c("e_tensor")

                    for qt in range(4):
                        scoresb_qt(qt)
                        smpt_qt(qt)
                    rec.wait("tensor", "dV", vload_at[hc % 2])
                    rec.join("tensor", "vector")
                    if pv_evac[0]:
                        rec.wait("tensor", "e_vector", pv_evac[0])
                    vsb = vstream[:, (hc % 2) * 2048:(hc % 2) * 2048 + 2048].rearrange(
                        "p (k f) -> p k f", k=ST)
                    for kt in range(ST):
                        for h2 in range(2):
                            mm2 = lambda e, kt=kt, h2=h2, vsb=vsb, hc=hc: \
                                e.matmul((psb[4] if h2 == 0 else psb[7])[
                                             h2 * 64:h2 * 64 + 64, :],
                                         vsb[:, kt, h2 * 64:h2 * 64 + 64],
                                         (PTa if h2 == 0 else PTb)[:, kt, :],
                                         start=(kt == 0), stop=(kt == ST - 1),
                                         tile_position=(0, h2 * 64))
                            rec.op("tensor", mm2)
                    rec.edge("tensor")
                    rec.join("vector", "tensor")
                    if ao_dma[0]:
                        rec.wait("vector", "dA", ao_dma[0])
                    E_("vector", lambda e, hc=hc: e.tensor_copy(
                        sco[0][0:64, 0:512], psb[4][0:64, :]))
                    E_("vector", lambda e, hc=hc: e.tensor_copy(
                        sco[0][64:128, 0:512], psb[7][64:128, :]))
                    rec.edge("vector")
                    pv_evac[0] = rec.c("e_vector")
                    rec.join("scalar", "vector")
                    E_("scalar", lambda e, hc=hc, qb=qb: e.dma_start(
                        out=a_spill[:, hc, qb * 512:(qb + 1) * 512],
                        in_=sco[0][:, 0:512]), inc=("dA", 16))
                    ao_dma[0] = rec.c("dA")
                    if hc + 2 < HC:
                        hc2 = hc + 2
                        rec.wait("sync", "e_tensor")
                        vsb2 = vstream[:, (hc2 % 2) * 2048:(hc2 % 2) * 2048 + 2048]
                        E_("sync", lambda e, vsb2=vsb2, hc2=hc2: e.dma_start(
                            out=vsb2.rearrange("p (k f) -> p k f", k=ST),
                            in_=v_spill[:, :, hc2 * 128:(hc2 + 1) * 128]),
                           inc=("dV", 16))
                        vload_at[hc2 % 2] = rec.c("dV")
                if qb + 2 < 4:
                    qb2 = qb + 2
                    qbv2 = qbuf[:, (qb2 % 2) * 4096:(qb2 % 2) * 4096 + 4096]
                    rec.wait("sync", "e_tensor")
                    E_("sync", lambda e, qb2=qb2, qbv2=qbv2: e.dma_start(
                        out=qbv2.rearrange("p (c s) -> p c s", c=HC),
                        in_=q_spill[:, :, qb2 * 512:(qb2 + 1) * 512]), inc=("dQ", 16))
                    qload_at[qb2 % 2] = rec.c("dQ")

            phase()
            wp = D[:, 0:8192].rearrange("p (c f) -> p c f", c=HC)
            for jc in range(HC):
                E_("sync", lambda e, jc=jc: e.dma_start(
                    out=wp[:, jc, :], in_=woutT_b[jc * 128:(jc + 1) * 128, :]),
                   inc=("dW", 16))
            load_bias_cols(bout_b, 0)
            rec.wait("vector", "dW")
            astream = C[:, 0:4096]
            rstream = C[:, 4096:8192]
            pj_evac = [0, 0, 0, 0]
            for blk in range(HC):
                asb = astream[:, (blk % 2) * 2048:(blk % 2) * 2048 + 2048]
                rsb = rstream[:, (blk % 2) * 2048:(blk % 2) * 2048 + 2048]
                if blk >= 2:
                    rec.wait("sync", "e_tensor")
                    rec.wait("sync", "e_vector")
                E_("sync", lambda e, asb=asb, blk=blk: e.dma_start(
                    out=asb.rearrange("p (c i) -> p c i", c=HC),
                    in_=a_spill[:, :, blk * 256:(blk + 1) * 256]), inc=("dQ", 16))
                E_("sync", lambda e, rsb=rsb, blk=blk: e.dma_start(
                    out=rsb.rearrange("p (c i) -> p c i", c=HC),
                    in_=r_spill[:, :, blk * 256:(blk + 1) * 256]), inc=("dV", 16))
                aload = rec.c("dQ")
                rload2 = rec.c("dV")
                rec.wait("tensor", "dQ", aload)
                rec.wait("vector", "dV", rload2)
                asv = asb.rearrange("p (c i) -> p c i", c=HC)
                rsv = rsb.rearrange("p (c i) -> p c i", c=HC)
                for ec in range(HC):
                    bank = 4 + (ec % 4)
                    if pj_evac[ec % 4]:
                        rec.wait("tensor", "e_vector", pj_evac[ec % 4])
                    for jc in range(HC):
                        mm(psb[bank][:, 0:256], wp[:, jc, ec * 128:(ec + 1) * 128],
                           asv[:, jc, :], start=(jc == 0), stop=(jc == HC - 1))
                    rec.edge("tensor")
                    rec.join("vector", "tensor")
                    E_("vector", lambda e, ec=ec, bank=bank, rsv=rsv, blk=blk:
                       e.scalar_tensor_tensor(
                           A_fm[:, ec, blk * 256:(blk + 1) * 256], psb[bank][:, 0:256],
                           wvec[:, ec:ec + 1], rsv[:, ec, :], ALU.add, ALU.add))
                    rec.edge("vector")
                    pj_evac[ec % 4] = rec.c("e_vector")

        def sv_gg(p=0):
            return D[:, 4096 + p * 512:4096 + p * 512 + RC2]

        def sv_gu(p=0):
            return D[:, 4352 + p * 512:4352 + p * 512 + RC2]

        def sv_fbtm():
            return D[:, 5120:6144]

        def sv_cmb():
            return D[:, 6144:7168]

        def sv_cm2():
            return D[:, 7168:8192]

        def sv_oh1(tt):
            return sv("oh1_%d" % tt, 8)

        def sv_oh2(tt):
            return sv("oh2_%d" % tt, 8)

        rankreg = []
        rec.wait("gpsimd", "dW")

        def _loadrank(e):
            rankreg.append(e.value_load(idxt[:1, 62:63], min_val=0, max_val=7))
            return e.nop()
        E_("gpsimd", _loadrank)

        if dev >= 2:
            attn_a()
        if dev >= 3:
            ffn(0)
        if dev >= 4:
            attn_b()
        if dev >= 5:
            ffn(1)

        rec.wait("sync", "dW")
        rec.wait("sync", "dA")
        rec.join("sync", "vector", "tensor", "scalar", "gpsimd")
        E_("sync", lambda e: e.dma_start(out=y_fm, in_=A_fm), inc=("dS", 16))
        rec.wait("gpsimd", "dS")
        rec.wait("sync", "dS")

        for ename in ENGS:
            def mk(ename):
                def _(h):
                    rec.replay(ename, h, sems)
                return _
            getattr(block, ename)(mk(ename))

    nc.compile()
    return nc


def kernel(L0_attn_a_x, L0_attn_a_w_qkv, L0_attn_a_w_proj, L0_attn_a_b_proj,
           L0_attn_b_W_qkv, L0_attn_b_b_qkv, L0_attn_b_W_out, L0_attn_b_b_out,
           L0_ffn_Wr, L0_ffn_norm_w, L0_ffn_Wg, L0_ffn_Wu, L0_ffn_Wd,
           L0_ffn_f_norm, L0_ffn_fWg, L0_ffn_fWu, L0_ffn_fWd):
    x = np.ascontiguousarray(np.asarray(L0_attn_a_x, dtype=np.float32))
    shared = {
        "ident": np.eye(128, dtype=np.float32),
        "wqkvT_a": np.ascontiguousarray(np.asarray(L0_attn_a_w_qkv, np.float32).T),
        "wprojT_a": np.ascontiguousarray(np.asarray(L0_attn_a_w_proj, np.float32).T),
        "bproj_a": np.ascontiguousarray(np.asarray(L0_attn_a_b_proj, np.float32).reshape(HC, 128).T),
        "wqkvT_b": np.ascontiguousarray(np.asarray(L0_attn_b_W_qkv, np.float32).T),
        "bqkv_b": np.ascontiguousarray(np.asarray(L0_attn_b_b_qkv, np.float32).reshape(3 * HC, 128).T),
        "woutT_b": np.ascontiguousarray(np.asarray(L0_attn_b_W_out, np.float32).T),
        "bout_b": np.ascontiguousarray(np.asarray(L0_attn_b_b_out, np.float32).reshape(HC, 128).T),
        "wr": np.ascontiguousarray(np.asarray(L0_ffn_Wr, np.float32)),
        "normw": np.ascontiguousarray(np.asarray(L0_ffn_norm_w, np.float32).reshape(E, HC, 128).transpose(0, 2, 1)),
        "wg": np.ascontiguousarray(np.asarray(L0_ffn_Wg, np.float32)),
        "wu": np.ascontiguousarray(np.asarray(L0_ffn_Wu, np.float32)),
        "wd": np.ascontiguousarray(np.asarray(L0_ffn_Wd, np.float32)),
        "fnorm": np.ascontiguousarray(np.asarray(L0_ffn_f_norm, np.float32).reshape(HC, 128).T),
        "fwg": np.ascontiguousarray(np.asarray(L0_ffn_fWg, np.float32)),
        "fwu": np.ascontiguousarray(np.asarray(L0_ffn_fWu, np.float32)),
        "fwd": np.ascontiguousarray(np.asarray(L0_ffn_fWd, np.float32)),
    }
    in_maps = []
    for c in range(NCORE):
        m = dict(shared)
        xt = np.ascontiguousarray(x[c].T).reshape(HC, 128, S).transpose(1, 0, 2)
        m["x_fm"] = np.ascontiguousarray(xt)
        m["rankv"] = np.array([c], dtype=np.int32)
        mk = np.zeros((NCORE, E), dtype=np.float32)
        mk[:c, :] = 1.0
        m["maskrow"] = np.broadcast_to(mk.reshape(-1), (128, 64)).copy()
        in_maps.append(m)

    nc = build_nc(globals().get("DEV_STAGE", 5))
    res = run_bass_kernel_spmd(nc, in_maps, core_ids=list(range(NCORE)))
    out = np.empty((NCORE, S, H), dtype=np.float32)
    for c in range(NCORE):
        yf = res.results[c]["y_fm"]
        out[c] = yf.transpose(1, 0, 2).reshape(H, S).T
    return out


# version 06
# speedup vs baseline: 1.2109x; 1.0252x over previous
import dataclasses
from contextlib import ExitStack

import numpy as np

import concourse.mybir as mybir
from concourse.bacc import Bacc
from concourse.bass import IndirectOffsetOnAxis
from concourse.bass_utils import run_bass_kernel_spmd

F32 = mybir.dt.float32
I32 = mybir.dt.int32
AF = mybir.ActivationFunctionType
ALU = mybir.AluOpType
AX = mybir.AxisListType

S = 2048
H = 1024
HC = H // 128
ST = S // 128
NCORE = 8
E = 8
FF = 2048
F4 = 512
CAP = 4096
PAD = 768
RC2 = 256
NH = 16
HD = 64

ENGS = ("sync", "scalar", "vector", "tensor", "gpsimd")


class Rec:
    def __init__(self):
        self.q = {e: [] for e in ENGS}
        self.cnt = {}

    def c(self, sem):
        return self.cnt.get(sem, 0)

    def op(self, eng, fn, inc=None):
        if inc is not None and inc[0].startswith("d"):
            self.wait(eng, inc[0])
        ent = [fn, None, None]
        self.q[eng].append(ent)
        if inc is not None:
            self.inc(eng, inc[0], inc[1])
        return ent

    def inc(self, eng, sem, n=1):
        ent = self.q[eng][-1]
        assert ent[0] is not None and ent[1] is None
        ent[1] = (sem, n)
        self.cnt[sem] = self.cnt.get(sem, 0) + n

    def wait(self, eng, sem, val=None):
        if val is None:
            val = self.cnt.get(sem, 0)
        if val <= 0:
            return
        self.q[eng].append([None, None, (sem, val)])

    def edge(self, eng):
        self.inc(eng, "e_" + eng, 1)

    def join(self, eng, *others):
        for o in others:
            self.wait(eng, "e_" + o)

    def replay(self, eng, handle, sems):
        water = {}
        own = "e_" + eng
        own_cnt = 0
        for fn, inc, wt in self.q[eng]:
            if wt is not None:
                s, v = wt
                if water.get(s, 0) < v:
                    handle.wait_ge(sems[s], v)
                    water[s] = v
            if fn is not None:
                if water.get(own, 0) < own_cnt:
                    handle.wait_ge(sems[own], own_cnt)
                    water[own] = own_cnt
                r = fn(handle)
                if inc is not None:
                    r.then_inc(sems[inc[0]], inc[1])
                    if inc[0] == own:
                        own_cnt += inc[1]


FFN_SECTIONS = ("prep", "router", "exchange", "pos", "zero", "experts",
                "fallback", "combine")
LAST_REC = None


def build_nc(dev=5):
    nc = Bacc("TRN2")

    def dram_in(name, shape, dt=F32):
        return nc.dram_tensor(name, list(shape), dt, kind="ExternalInput").ap()

    x_in = dram_in("x_fm", [128, HC, S])
    rankv = dram_in("rankv", [1], I32)
    maskin = dram_in("maskrow", [128, 64])
    ident_in = dram_in("ident", [128, 128])
    wqkvT_a = dram_in("wqkvT_a", [H, 3 * H])
    wprojT_a = dram_in("wprojT_a", [H, H])
    bproj_a = dram_in("bproj_a", [128, HC])
    wqkvT_b = dram_in("wqkvT_b", [H, 3 * H])
    bqkv_b = dram_in("bqkv_b", [128, 3 * HC])
    woutT_b = dram_in("woutT_b", [H, H])
    bout_b = dram_in("bout_b", [128, HC])
    wr_in = dram_in("wr", [H, E])
    normw_in = dram_in("normw", [E, 128, HC])
    wg_in = dram_in("wg", [E, H, FF])
    wu_in = dram_in("wu", [E, H, FF])
    wd_in = dram_in("wd", [E, FF, H])
    fnorm_in = dram_in("fnorm", [128, HC])
    fwg_in = dram_in("fwg", [H, F4])
    fwu_in = dram_in("fwu", [H, F4])
    fwd_in = dram_in("fwd", [F4, H])

    y_fm = nc.dram_tensor("y_fm", [128, HC, S], F32, kind="ExternalOutput").ap()

    def scratch(name, shape, dt=F32):
        return nc.dram_tensor(name, list(shape), dt, kind="Internal").ap()

    r_spill = scratch("r_spill", [128, HC, S])
    v_spill = scratch("v_spill", [128, ST, H])
    q_spill = scratch("q_spill", [128, HC, S])
    a_spill = scratch("a_spill", [128, HC, S])
    xtm_spill = scratch("xtm_spill", [S, H])
    xr_spill = scratch("xr_spill", [S, H])
    pairs_d = scratch("pairs_d", [E * S, 2])
    sY_d = scratch("sY_d", [S, H])

    rec = Rec()
    global LAST_REC
    LAST_REC = rec
    es = ExitStack()
    with es:
        def sb(name, p, n, dt=F32):
            return es.enter_context(nc.sbuf_tensor(name, [p, n], dt))

        A = sb("A", 128, 16384)
        B = sb("B", 128, 16384)
        C = sb("C", 128, 8192)
        D = sb("D", 128, 8192)
        identt = sb("identt", 128, 128)
        smalls = sb("smalls", 128, 2048)
        zeroK = sb("zeroK", 128, 1024)
        wvec = sb("wvec", 128, 96)
        wrt = sb("wrt", 128, 64)
        iot8 = sb("iot8", 128, 8)
        cntt = sb("cntt", 128, 8)
        tblt = [sb("tbl0", 128, 64), sb("tbl1", 128, 64)]
        maskt = sb("maskt", 128, 64)
        stage16 = sb("stage16", 16, 128)
        onesr = sb("onesr", 1, 128)
        idxt = sb("idxt", 128, 64, I32)
        sidxc = sb("sidxc", 128, 16, I32)

        psb = [es.enter_context(nc.psum_tensor(f"ps{i}", [128, 512], F32))
               for i in range(8)]

        semnames = ["e_sync", "e_scalar", "e_vector", "e_tensor", "e_gpsimd",
                    "dW", "dA", "dG", "dS", "dWg", "dWu", "dWd", "dQ", "dV",
                    "prep", "rsem", "lsem", "rsem2", "lsem2"]
        sems = {n: es.enter_context(nc.semaphore(n)) for n in semnames}
        block = es.enter_context(nc.Block())

        def E_(eng, fn, inc=None):
            rec.op(eng, fn, inc=inc)

        def mm(out_ap, lhsT, rhs, start, stop, tp=None):
            rec.op("tensor", lambda e, o=out_ap, l=lhsT, r=rhs, st=start, sp=stop,
                   tp=tp: e.matmul(o, l, r, start=st, stop=sp, tile_position=tp))

        def tr(out_ap, in_ap):
            p = in_ap.shape[0]
            rec.op("tensor", lambda e, o=out_ap, i=in_ap, p=p:
                   e.transpose(o, i, identt[:p, :p]))

        def fmv(t, off=0):
            return t[:, off:off + HC * S].rearrange("p (c s) -> p c s", c=HC)

        A_fm = fmv(A)
        B_fm = fmv(B)

        def phase():
            engs = ("tensor", "vector", "scalar", "gpsimd", "sync")
            for a in engs:
                rec.join(a, *[o for o in engs if o != a])
                for dsem in ("dW", "dA", "dG", "dS", "dWg", "dWu", "dWd",
                             "dQ", "dV"):
                    rec.wait(a, dsem)

        _svar = {}

        def sv(name, w=16):
            if name not in _svar:
                base = sum(ww for _, ww in _svar.values())
                assert base + w <= 2048, "smalls overflow"
                _svar[name] = (base, w)
            o, ww = _svar[name]
            assert ww == w, name
            return smalls[:, o:o + w]

        def svc(name, t, w=16):
            return sv(name, w)[:, t:t + 1]

        E_("sync", lambda e: e.dma_start(out=identt[:, :], in_=ident_in), inc=("dW", 16))
        E_("sync", lambda e: e.dma_start(out=maskt[:, :], in_=maskin), inc=("dW", 16))
        E_("sync", lambda e: e.dma_start(out=A_fm, in_=x_in), inc=("dW", 16))
        E_("sync", lambda e: e.dma_start(
            out=wrt[:, :].rearrange("p (c e) -> p c e", c=HC),
            in_=wr_in.rearrange("(c p) e -> p c e", p=128)), inc=("dW", 16))
        E_("sync", lambda e: e.dma_start(out=idxt[:1, 62:63], in_=rankv[None, :]),
           inc=("dW", 16))
        E_("vector", lambda e: e.memset(zeroK[:, :], 0.0))
        E_("vector", lambda e: e.memset(smalls[:, :], 0.0))
        E_("vector", lambda e: e.memset(cntt[:, :], 0.0))
        E_("vector", lambda e: e.memset(idxt[:, 0:62], 0))
        E_("vector", lambda e: e.memset(wvec[:, :], 0.0))
        E_("vector", lambda e: e.memset(onesr[:1, :], 1.0))
        rec.edge("vector")
        ones_col = wvec[:, 95:96]
        eps5_col = wvec[:, 94:95]
        eps6_col = wvec[:, 93:94]
        E_("vector", lambda e: e.memset(ones_col, 1.0))
        E_("vector", lambda e: e.memset(eps5_col, 1e-5))
        E_("vector", lambda e: e.memset(eps6_col, 1e-6))
        rec.edge("vector")
        E_("gpsimd", lambda e: e.iota(sidxc[:, :], pattern=[[1, 16]], base=0,
                                      channel_multiplier=16))
        E_("gpsimd", lambda e: e.iota(iot8[:, :], pattern=[[1, 8]], base=0,
                                      channel_multiplier=0,
                                      allow_small_or_imprecise_dtypes=True))
        rec.edge("gpsimd")
        rec.join("vector", "gpsimd")
        E_("vector", lambda e: e.tensor_copy(sv("sidxf"), sidxc[:, :]))
        rec.edge("vector")

        def bcast_tm_var(var_ap, slab):
            rowC = C[0:1, 6144:8192]
            rec.join("tensor", "vector", "scalar")
            tr(psb[6][:16, 0:128], var_ap)
            rec.edge("tensor")
            rec.join("vector", "tensor")
            E_("vector", lambda e: e.tensor_copy(stage16[:, :], psb[6][:16, 0:128]))
            rec.edge("vector")
            rec.join("tensor", "vector")
            for r4 in range(4):
                for j in range(4):
                    c = r4 * 4 + j
                    mm(psb[7][0:1, j * 128:(j + 1) * 128],
                       identt[0:16, c:c + 1], stage16[:, :],
                       start=True, stop=True)
                rec.edge("tensor")
                rec.join("vector", "tensor")
                E_("vector", lambda e, r4=r4: e.tensor_copy(
                    rowC[:, r4 * 512:(r4 + 1) * 512], psb[7][0:1, 0:512]))
                rec.edge("vector")
                rec.join("tensor", "vector")
            for b4 in range(4):
                mm(psb[7][:, :], onesr[:1, :], rowC[:, b4 * 512:(b4 + 1) * 512],
                   start=True, stop=True)
                rec.edge("tensor")
                rec.join("vector", "tensor")
                E_("vector", lambda e, b4=b4: e.tensor_copy(
                    slab[:, b4 * 512:(b4 + 1) * 512], psb[7][:, :]))
                rec.edge("vector")
                rec.join("tensor", "vector")

        def layernorm(src_fm, dst_fm, eps, rms_only=False, apply=True):
            phase()
            x2 = C[:, 0:1024]
            a_slab = C[:, 2048:4096]
            b_slab = C[:, 4096:6144]
            for rnd in range(4):
                for hc in range(HC):
                    xb = x2[:, (hc % 2) * 512:(hc % 2) * 512 + 512]
                    rec.join("scalar", "tensor")
                    E_("scalar", lambda e, hc=hc, rnd=rnd, xb=xb: e.activation(
                        xb, src_fm[:, hc, rnd * 512:rnd * 512 + 512], AF.Square))
                    rec.edge("scalar")
                    rec.join("tensor", "scalar")
                    for i in range(4):
                        t = rnd * 4 + i
                        if not rms_only:
                            mm(psb[i][:, 0:1],
                               src_fm[:, hc, t * 128:(t + 1) * 128],
                               ones_col, start=(hc == 0), stop=(hc == HC - 1))
                        mm(psb[4 + i][:, 0:1], xb[:, i * 128:(i + 1) * 128],
                           ones_col, start=(hc == 0), stop=(hc == HC - 1))
                    rec.edge("tensor")
                rec.join("vector", "tensor")
                for i in range(4):
                    t = rnd * 4 + i
                    if not rms_only:
                        E_("vector", lambda e, i=i, t=t: e.tensor_copy(
                            svc("lsum", t), psb[i][:, 0:1]))
                    E_("vector", lambda e, i=i, t=t: e.tensor_copy(
                        svc("lsq", t), psb[4 + i][:, 0:1]))
                rec.edge("vector")
                rec.join("tensor", "vector")
            if rms_only:
                rec.join("scalar", "vector")
                E_("scalar", lambda e, eps=eps: e.activation(
                    sv("sd"), sv("lsq"), AF.Sqrt,
                    bias=eps5_col if eps == 1e-5 else eps6_col, scale=1.0 / H))
                rec.edge("scalar")
                rec.join("vector", "scalar")
                E_("vector", lambda e: e.reciprocal(sv("a"), sv("sd")))
                rec.edge("vector")
            else:
                E_("vector", lambda e: e.tensor_scalar_mul(sv("mean"), sv("lsum"),
                                                           1.0 / H))
                rec.edge("vector")
                E_("vector", lambda e: e.tensor_tensor(sv("m2"), sv("mean"),
                                                       sv("mean"), ALU.mult))
                rec.edge("vector")
                E_("vector", lambda e: e.scalar_tensor_tensor(
                    sv("var"), sv("lsq"), 1.0 / H, sv("m2"), ALU.mult, ALU.subtract))
                rec.edge("vector")
                rec.join("scalar", "vector")
                E_("scalar", lambda e, eps=eps: e.activation(
                    sv("sd"), sv("var"), AF.Sqrt,
                    bias=eps5_col if eps == 1e-5 else eps6_col))
                rec.edge("scalar")
                rec.join("vector", "scalar")
                E_("vector", lambda e: e.reciprocal(sv("a"), sv("sd")))
                rec.edge("vector")
                E_("vector", lambda e: e.scalar_tensor_tensor(
                    sv("b"), sv("mean"), -1.0, sv("a"), ALU.mult, ALU.mult))
                rec.edge("vector")
            if not apply:
                return
            bcast_tm_var(sv("a"), a_slab)
            if not rms_only:
                bcast_tm_var(sv("b"), b_slab)
            rec.join("vector", "tensor")
            for hc in range(HC):
                E_("vector", lambda e, hc=hc: e.tensor_tensor(
                    dst_fm[:, hc, :], src_fm[:, hc, :], a_slab, ALU.mult))
                rec.edge("vector")
                if not rms_only:
                    E_("vector", lambda e, hc=hc: e.tensor_tensor(
                        dst_fm[:, hc, :], dst_fm[:, hc, :], b_slab, ALU.add))
                    rec.edge("vector")

        def qkv_fm_phase(wsrc, col0, dst_fm_view, spill, bias_src=None, bias_cols=None):
            wq = C[:, 0:8192].rearrange("p (c f) -> p c f", c=HC)
            rec.join("sync", "tensor", "vector")
            for hc in range(HC):
                E_("sync", lambda e, hc=hc: e.dma_start(
                    out=wq[:, hc, :], in_=wsrc[hc * 128:(hc + 1) * 128, col0:col0 + H]),
                   inc=("dW", 16))
            rec.wait("tensor", "dW")
            stage = D[:, 0:4096]
            last_evac = 0
            for qfc in range(HC):
                if last_evac:
                    rec.wait("tensor", "e_vector", last_evac)
                for hc in range(HC):
                    for sc in range(4):
                        mm(psb[sc][:, :], wq[:, hc, qfc * 128:(qfc + 1) * 128],
                           B_fm[:, hc, sc * 512:(sc + 1) * 512],
                           start=(hc == 0), stop=(hc == HC - 1))
                rec.edge("tensor")
                rec.join("vector", "tensor")
                if spill is not None and qfc >= 2:
                    rec.wait("vector", "dA")
                sg = stage[:, (qfc % 2) * 2048:(qfc % 2) * 2048 + 2048]
                for sc in range(4):
                    dst = (sg[:, sc * 512:(sc + 1) * 512] if spill is not None
                           else dst_fm_view[:, qfc, sc * 512:(sc + 1) * 512])
                    if bias_src is not None:
                        E_("vector", lambda e, dst=dst, sc=sc, qfc=qfc:
                           e.tensor_scalar_add(dst, psb[sc][:, :],
                                               wvec[:, bias_cols + qfc:
                                                    bias_cols + qfc + 1]))
                    else:
                        E_("vector", lambda e, dst=dst, sc=sc: e.tensor_copy(
                            dst, psb[sc][:, :]))
                rec.edge("vector")
                last_evac = rec.c("e_vector")
                if spill is not None:
                    rec.join("scalar", "vector")
                    E_("scalar", lambda e, sg=sg, qfc=qfc: e.dma_start(
                        out=spill[:, qfc, :], in_=sg), inc=("dA", 16))

        def load_bias_cols(src_dram, cols, n=HC):
            E_("sync", lambda e: e.dma_start(
                out=wvec[:, cols:cols + n], in_=src_dram), inc=("dW", 16))

        def fm_to_tm_spill(src_fm, spill, psum_pair, tile_ap=None):
            b0, b1 = psum_pair
            if tile_ap is None:
                tile_ap = lambda tt: spill.rearrange("(p b) h -> p b h",
                                                     p=128)[:, tt, :]
            tmb = D[:, 4096:8192]
            for tt in range(ST):
                tb = tmb[:, (tt % 2) * 2048:(tt % 2) * 2048 + 1024]
                rec.join("tensor", "vector")
                for hc in range(HC):
                    bank = b0 if hc < 4 else b1
                    tr(psb[bank][:, (hc % 4) * 128:(hc % 4) * 128 + 128],
                       src_fm[:, hc, tt * 128:(tt + 1) * 128])
                rec.edge("tensor")
                rec.join("vector", "tensor")
                if tt >= 2:
                    rec.wait("vector", "dA")
                E_("vector", lambda e, tb=tb, b0=b0: e.tensor_copy(
                    tb[:, 0:512], psb[b0][:, :]))
                E_("vector", lambda e, tb=tb, b1=b1: e.tensor_copy(
                    tb[:, 512:1024], psb[b1][:, :]))
                rec.edge("vector")
                rec.join("scalar", "vector")
                E_("scalar", lambda e, tb=tb, tt=tt: e.dma_start(
                    out=tile_ap(tt), in_=tb), inc=("dA", 16))

        def attn_a():
            rec.join("scalar", "vector", "tensor")
            rec.wait("scalar", "dW")
            E_("scalar", lambda e: e.dma_start(out=r_spill, in_=A_fm), inc=("dA", 16))
            rspill_at = rec.c("dA")
            layernorm(A_fm, B_fm, 1e-5)

            phase()
            wv = C[:, 0:8192].rearrange("p (c f) -> p c f", c=HC)
            rec.join("sync", "vector", "tensor")
            for hc in range(HC):
                E_("sync", lambda e, hc=hc: e.dma_start(
                    out=wv[:, hc, :],
                    in_=wqkvT_a[hc * 128:(hc + 1) * 128, 2 * H:3 * H]), inc=("dW", 16))
            vbuf = D[:, 0:2048]
            rec.wait("tensor", "dW")
            evac_at = [0, 0]
            for st in range(ST):
                b0 = (st % 2) * 2
                if evac_at[st % 2]:
                    rec.wait("tensor", "e_vector", evac_at[st % 2])
                for hc in range(HC):
                    mm(psb[b0][:, :], B_fm[:, hc, st * 128:(st + 1) * 128],
                       wv[:, hc, 0:512], start=(hc == 0), stop=(hc == HC - 1))
                    mm(psb[b0 + 1][:, :], B_fm[:, hc, st * 128:(st + 1) * 128],
                       wv[:, hc, 512:1024], start=(hc == 0), stop=(hc == HC - 1))
                rec.edge("tensor")
                rec.join("vector", "tensor")
                if st >= 2:
                    rec.wait("vector", "dA")
                vb = vbuf[:, (st % 2) * 1024:(st % 2) * 1024 + 1024]
                E_("vector", lambda e, vb=vb, b0=b0: e.tensor_copy(vb[:, 0:512],
                                                                   psb[b0][:, :]))
                E_("vector", lambda e, vb=vb, b0=b0: e.tensor_copy(vb[:, 512:1024],
                                                                   psb[b0 + 1][:, :]))
                rec.edge("vector")
                evac_at[st % 2] = rec.c("e_vector")
                rec.join("scalar", "vector")
                E_("scalar", lambda e, vb=vb, st=st: e.dma_start(
                    out=v_spill[:, st, :], in_=vb), inc=("dA", 16))

            phase()
            qkv_fm_phase(wqkvT_a, 0, None, q_spill)
            phase()
            qkv_fm_phase(wqkvT_a, H, A_fm, None)
            phase()

            phase()
            scores = B[:, 0:8192]
            PT = B[:, 8192:16384].rearrange("p (k q) -> p k q", k=ST)
            qbuf = D[:, 0:8192]
            vstream = C[:, 0:4096]
            aout = C[:, 4096:8192]
            qload_at = [0, 0]
            rec.join("sync", "vector", "tensor")
            for qb in range(2):
                qbv = qbuf[:, (qb % 2) * 4096:(qb % 2) * 4096 + 4096]
                E_("sync", lambda e, qb=qb, qbv=qbv: e.dma_start(
                    out=qbv.rearrange("p (c s) -> p c s", c=HC),
                    in_=q_spill[:, :, qb * 512:(qb + 1) * 512]), inc=("dQ", 16))
                qload_at[qb % 2] = rec.c("dQ")
            sc_evac = [0, 0, 0, 0]
            pt_evac = [0, 0]
            pv_evac = [0, 0]
            for qb in range(4):
                qbv = qbuf[:, (qb % 2) * 4096:(qb % 2) * 4096 + 4096].rearrange(
                    "p (c s) -> p c s", c=HC)
                rec.wait("tensor", "dQ", qload_at[qb % 2])

                def scores_qt(qt, qbv=qbv):
                    for kc in range(4):
                        bank = kc
                        if sc_evac[bank]:
                            rec.wait("tensor", "e_vector", sc_evac[bank])
                        for hc in range(HC):
                            mm(psb[bank][:, :], qbv[:, hc, qt * 128:(qt + 1) * 128],
                               A_fm[:, hc, kc * 512:(kc + 1) * 512],
                               start=(hc == 0), stop=(hc == HC - 1))
                        rec.edge("tensor")
                        rec.join("vector", "tensor")
                        E_("vector", lambda e, qt=qt, kc=kc, bank=bank: e.tensor_copy(
                            scores[:, qt * 2048 + kc * 512:qt * 2048 + kc * 512 + 512],
                            psb[bank][:, :]))
                        rec.edge("vector")
                        sc_evac[bank] = rec.c("e_vector")

                def softmax_qt(qt):
                    sl = scores[:, qt * 2048:(qt + 1) * 2048]
                    E_("vector", lambda e, sl=sl, qt=qt: e.reduce_max(
                        svc("negmax", qt), sl, axis=AX.X, negate=True))
                    rec.edge("vector")
                    E_("vector", lambda e, qt=qt: e.tensor_scalar_mul(
                        svc("negmax", qt), svc("negmax", qt), 0.125))
                    rec.edge("vector")
                    rec.join("scalar", "vector")
                    E_("scalar", lambda e, sl=sl, qt=qt: e.activation(
                        sl, sl, AF.Exp, bias=svc("negmax", qt), scale=0.125,
                        accum_out=svc("ssum", qt)))
                    rec.edge("scalar")
                    rec.join("vector", "scalar")
                    E_("vector", lambda e, qt=qt: e.reciprocal(
                        svc("srec", qt), svc("ssum", qt)))
                    rec.edge("vector")
                    E_("vector", lambda e, sl=sl, qt=qt: e.tensor_scalar_mul(
                        sl, sl, svc("srec", qt)))
                    rec.edge("vector")

                def pt_qt(qt):
                    rec.join("tensor", "vector")
                    for kt4 in range(4):
                        bank = 4 + (kt4 % 2)
                        if pt_evac[kt4 % 2]:
                            rec.wait("tensor", "e_vector", pt_evac[kt4 % 2])
                        for j in range(4):
                            kt = kt4 * 4 + j
                            tr(psb[bank][:, j * 128:(j + 1) * 128],
                               scores[:, qt * 2048 + kt * 128:qt * 2048 + kt * 128 + 128])
                        rec.edge("tensor")
                        rec.join("vector", "tensor")
                        E_("vector", lambda e, kt4=kt4, qt=qt, bank=bank:
                           e.tensor_copy(
                               PT[:, kt4 * 4:kt4 * 4 + 4, qt * 128:(qt + 1) * 128],
                               psb[bank][:, :].rearrange("p (a q) -> p a q", a=4)))
                        rec.edge("vector")
                        pt_evac[kt4 % 2] = rec.c("e_vector")

                scores_qt(0)
                for qt in range(4):
                    if qt + 1 < 4:
                        scores_qt(qt + 1)
                    softmax_qt(qt)
                    pt_qt(qt)
                vload_at = [0, 0]
                rec.join("sync", "tensor")
                for vfc in range(2):
                    vsb = vstream[:, (vfc % 2) * 2048:(vfc % 2) * 2048 + 2048]
                    E_("sync", lambda e, vsb=vsb, vfc=vfc: e.dma_start(
                        out=vsb.rearrange("p (k f) -> p k f", k=ST),
                        in_=v_spill[:, :, vfc * 128:(vfc + 1) * 128]), inc=("dV", 16))
                    vload_at[vfc % 2] = rec.c("dV")
                rec.join("tensor", "vector")
                rec.wait("vector", "dA")
                for vfc in range(HC):
                    bank = 6 + (vfc % 2)
                    vsb = vstream[:, (vfc % 2) * 2048:(vfc % 2) * 2048 + 2048].rearrange(
                        "p (k f) -> p k f", k=ST)
                    rec.wait("tensor", "dV", vload_at[vfc % 2])
                    if pv_evac[vfc % 2]:
                        rec.wait("tensor", "e_vector", pv_evac[vfc % 2])
                    for kt in range(ST):
                        mm(psb[bank][:, :], vsb[:, kt, :], PT[:, kt, :],
                           start=(kt == 0), stop=(kt == ST - 1))
                    rec.edge("tensor")
                    mm_done = rec.c("e_tensor")
                    if vfc + 2 < HC:
                        vfc2 = vfc + 2
                        vsb2 = vstream[:, (vfc2 % 2) * 2048:(vfc2 % 2) * 2048 + 2048]
                        rec.wait("sync", "e_tensor", mm_done)
                        E_("sync", lambda e, vsb2=vsb2, vfc2=vfc2: e.dma_start(
                            out=vsb2.rearrange("p (k f) -> p k f", k=ST),
                            in_=v_spill[:, :, vfc2 * 128:(vfc2 + 1) * 128]),
                           inc=("dV", 16))
                        vload_at[vfc2 % 2] = rec.c("dV")
                    rec.join("vector", "tensor")
                    E_("vector", lambda e, vfc=vfc, bank=bank: e.tensor_copy(
                        aout[:, vfc * 512:(vfc + 1) * 512], psb[bank][:, :]))
                    rec.edge("vector")
                    pv_evac[vfc % 2] = rec.c("e_vector")
                rec.join("scalar", "vector")
                E_("scalar", lambda e, qb=qb: e.dma_start(
                    out=a_spill[:, :, qb * 512:(qb + 1) * 512],
                    in_=aout.rearrange("p (c s) -> p c s", c=HC)), inc=("dA", 16))
                if qb + 2 < 4:
                    qb2 = qb + 2
                    qbv2 = qbuf[:, (qb2 % 2) * 4096:(qb2 % 2) * 4096 + 4096]
                    rec.wait("sync", "e_tensor")
                    E_("sync", lambda e, qb2=qb2, qbv2=qbv2: e.dma_start(
                        out=qbv2.rearrange("p (c s) -> p c s", c=HC),
                        in_=q_spill[:, :, qb2 * 512:(qb2 + 1) * 512]), inc=("dQ", 16))
                    qload_at[qb2 % 2] = rec.c("dQ")

            phase()
            wp = D[:, 0:8192].rearrange("p (c f) -> p c f", c=HC)
            rec.join("sync", "tensor", "vector")
            for jc in range(HC):
                E_("sync", lambda e, jc=jc: e.dma_start(
                    out=wp[:, jc, :], in_=wprojT_a[jc * 128:(jc + 1) * 128, :]),
                   inc=("dW", 16))
            load_bias_cols(bproj_a, 0)
            rec.wait("vector", "dW")
            astream = C[:, 0:4096]
            rstream = C[:, 4096:8192]
            out2 = B[:, 0:2048].rearrange("p (c i) -> p c i", c=HC)
            tr_evac = [0, 0, 0, 0]
            pj_evac = [0, 0, 0, 0]
            rec.wait("sync", "dA", rspill_at)
            for c8 in range(HC):
                asb = astream[:, (c8 % 2) * 2048:(c8 % 2) * 2048 + 2048]
                rsb = rstream[:, (c8 % 2) * 2048:(c8 % 2) * 2048 + 2048]
                if c8 >= 2:
                    rec.wait("sync", "e_tensor")
                    rec.wait("sync", "e_vector")
                E_("sync", lambda e, asb=asb, c8=c8: e.dma_start(
                    out=asb, in_=a_spill[:, c8, :]), inc=("dQ", 16))
                E_("sync", lambda e, rsb=rsb, c8=c8: e.dma_start(
                    out=rsb.rearrange("p (c i) -> p c i", c=HC),
                    in_=r_spill[:, :, c8 * 256:(c8 + 1) * 256]), inc=("dV", 16))
                aload = rec.c("dQ")
                rload2 = rec.c("dV")
                rec.wait("tensor", "dQ", aload)
                rec.wait("vector", "dV", rload2)
                rec.wait("tensor", "dW")
                for m in range(2):
                    for jb in range(HC):
                        bank = (m * 8 + jb) % 4
                        if tr_evac[bank]:
                            rec.wait("tensor", "e_vector", tr_evac[bank])
                        tr(psb[bank][:, 0:128],
                           asb[:, m * 1024 + jb * 128:m * 1024 + jb * 128 + 128])
                        rec.edge("tensor")
                        rec.join("vector", "tensor")
                        E_("vector", lambda e, m=m, jb=jb, bank=bank: e.tensor_copy(
                            out2[:, jb, m:256:2], psb[bank][:, 0:128]))
                        rec.edge("vector")
                        tr_evac[bank] = rec.c("e_vector")
                rsv = rsb.rearrange("p (c i) -> p c i", c=HC)
                rec.join("tensor", "vector")
                for ec in range(HC):
                    bank = 4 + (ec % 4)
                    if pj_evac[ec % 4]:
                        rec.wait("tensor", "e_vector", pj_evac[ec % 4])
                    for jc in range(HC):
                        mm(psb[bank][:, 0:256], wp[:, jc, ec * 128:(ec + 1) * 128],
                           out2[:, jc, :], start=(jc == 0), stop=(jc == HC - 1))
                    rec.edge("tensor")
                    rec.join("vector", "tensor")
                    E_("vector", lambda e, ec=ec, bank=bank, rsv=rsv, c8=c8:
                       e.scalar_tensor_tensor(
                           A_fm[:, ec, c8 * 256:(c8 + 1) * 256], psb[bank][:, 0:256],
                           wvec[:, ec:ec + 1], rsv[:, ec, :], ALU.add, ALU.add))
                    rec.edge("vector")
                    pj_evac[ec % 4] = rec.c("e_vector")

        def ffn(ci):
            SEC = FFN_SECTIONS
            rec.join("scalar", "vector", "tensor")
            E_("scalar", lambda e: e.dma_start(out=r_spill, in_=A_fm), inc=("dA", 16))
            rspill_at = rec.c("dA")
            layernorm(A_fm, B_fm, 1e-5)
            if "router" not in SEC:
                return
            rec.join("tensor", "vector")
            rt_evac = [0] * 8
            for tt in range(ST):
                bank = tt % 8
                if rt_evac[bank]:
                    rec.wait("tensor", "e_vector", rt_evac[bank])
                for hc in range(HC):
                    mm(psb[bank][:, 0:8], B_fm[:, hc, tt * 128:(tt + 1) * 128],
                       wrt[:, hc * 8:hc * 8 + 8], start=(hc == 0), stop=(hc == HC - 1))
                rec.edge("tensor")
                rec.join("vector", "tensor")
                E_("vector", lambda e, bank=bank: e.tensor_copy(sv("z8", 8),
                                                                psb[bank][:, 0:8]))
                rec.edge("vector")
                rt_evac[bank] = rec.c("e_vector")
                E_("vector", lambda e: e.reduce_max(svc("nm1", 0, 8), sv("z8", 8),
                                                    axis=AX.X, negate=True))
                rec.edge("vector")
                E_("vector", lambda e: e.tensor_single_scalar(
                    sv("zs", 8), sv("z8", 8), svc("nm1", 0, 8), ALU.add))
                rec.edge("vector")
                E_("vector", lambda e: e.tensor_single_scalar(
                    sv("oh", 8), sv("zs", 8), 0.0, ALU.is_equal))
                rec.edge("vector")
                E_("vector", lambda e: e.scalar_tensor_tensor(
                    sv("ti", 8), sv("oh", 8), -1000.0, iot8[:, :], ALU.mult, ALU.add))
                rec.edge("vector")
                E_("vector", lambda e: e.tensor_reduce(
                    svc("e1r", 0, 8), sv("ti", 8), axis=AX.X, op=ALU.min))
                rec.edge("vector")
                E_("vector", lambda e, tt=tt: e.tensor_scalar_add(
                    svc("e1", tt), svc("e1r", 0, 8), 1000.0))
                rec.edge("vector")
                E_("vector", lambda e, tt=tt: e.tensor_single_scalar(
                    sv_oh1(tt), iot8[:, :], svc("e1", tt), ALU.is_equal))
                rec.edge("vector")
                rec.join("scalar", "vector")
                E_("scalar", lambda e: e.activation(
                    sv("ez", 8), sv("zs", 8), AF.Exp, accum_out=svc("esum", 0, 8)))
                rec.edge("scalar")
                rec.join("vector", "scalar")
                E_("vector", lambda e, tt=tt: e.reciprocal(svc("w1", tt),
                                                           svc("esum", 0, 8)))
                rec.edge("vector")
                E_("vector", lambda e, tt=tt: e.scalar_tensor_tensor(
                    sv("z2", 8), sv_oh1(tt), -1000.0, sv("zs", 8), ALU.mult, ALU.add))
                rec.edge("vector")
                E_("vector", lambda e: e.reduce_max(svc("nm2", 0, 8), sv("z2", 8),
                                                    axis=AX.X, negate=True))
                rec.edge("vector")
                E_("vector", lambda e: e.tensor_single_scalar(
                    sv("z2s", 8), sv("z2", 8), svc("nm2", 0, 8), ALU.add))
                rec.edge("vector")
                E_("vector", lambda e: e.tensor_single_scalar(
                    sv("oh2r", 8), sv("z2s", 8), 0.0, ALU.is_equal))
                rec.edge("vector")
                E_("vector", lambda e: e.scalar_tensor_tensor(
                    sv("ti", 8), sv("oh2r", 8), -1000.0, iot8[:, :], ALU.mult,
                    ALU.add))
                rec.edge("vector")
                E_("vector", lambda e: e.tensor_reduce(
                    svc("e2r", 0, 8), sv("ti", 8), axis=AX.X, op=ALU.min))
                rec.edge("vector")
                E_("vector", lambda e, tt=tt: e.tensor_scalar_add(
                    svc("e2", tt), svc("e2r", 0, 8), 1000.0))
                rec.edge("vector")
                E_("vector", lambda e, tt=tt: e.tensor_single_scalar(
                    sv_oh2(tt), iot8[:, :], svc("e2", tt), ALU.is_equal))
                rec.edge("vector")
                rec.join("scalar", "vector")
                E_("scalar", lambda e, tt=tt: e.activation(
                    svc("w2e", tt), svc("nm2", 0, 8), AF.Exp, scale=-1.0))
                rec.edge("scalar")
                rec.join("vector", "scalar")
                E_("vector", lambda e, tt=tt: e.tensor_tensor(
                    svc("w2", tt), svc("w2e", tt), svc("w1", tt), ALU.mult))
                rec.edge("vector")
                E_("vector", lambda e, tt=tt: e.tensor_tensor(
                    sv("y8", 8), sv_oh1(tt), sv_oh2(tt), ALU.add))
                rec.edge("vector")
                rec.join("tensor", "vector")
                bank2 = 4 + (tt % 2)
                tr(psb[bank2][:8, 0:128], sv("y8", 8))
                rec.edge("tensor")
                rec.join("vector", "tensor")
                E_("vector", lambda e, tt=tt, bank2=bank2: e.tensor_copy(
                    D[0:8, tt * 128:(tt + 1) * 128], psb[bank2][:8, 0:128]))
                rec.edge("vector")

            if "exchange" not in SEC:
                return
            phase()
            y_em = D[0:8, 0:2048]
            P_incl = D[0:8, 2048:4096]
            E_("vector", lambda e: e.tensor_tensor_scan(
                P_incl, y_em, y_em, 0.0, ALU.add, ALU.bypass))
            rec.edge("vector")
            E_("vector", lambda e: e.tensor_tensor(y_em, P_incl, y_em, ALU.subtract))
            rec.edge("vector")
            rec.wait("vector", "lsem")
            rec.wait("vector", "lsem2")
            E_("vector", lambda e: e.tensor_copy(cntt[0:8, 0:1],
                                                 P_incl[:, 2047:2048]))
            rec.edge("vector")
            rec.join("gpsimd", "vector")
            rec.wait("gpsimd", "dW")
            tb = tblt[ci]
            rs = "rsem" if ci == 0 else "rsem2"
            ls = "lsem" if ci == 0 else "lsem2"
            E_("gpsimd", lambda e, rs=rs, ls=ls: e.remote_dma_broadcast(
                dataclasses.replace(tb[:, 0:8], offset=rankreg[0] * 8),
                cntt[:, :], remote_sem=sems[rs], local_sem=sems[ls],
                rdests=[(0, k) for k in range(8)]), inc=("prep", 1))
            rec.wait("gpsimd", "prep")
            E_("gpsimd", lambda e: e.trigger_dma(count=1))
            rec.cnt[rs] = rec.c(rs) + 16
            rec.cnt[ls] = rec.c(ls) + 16

            if "zero" not in SEC:
                return
            rec.join("gpsimd", "vector")
            E_("gpsimd", lambda e: e.dma_start(
                out=pairs_d.rearrange("(p n) two -> p n two", p=128),
                in_=zeroK[:, 0:256].rearrange("p (n two) -> p n two", two=2)),
               inc=("dG", 16))
            for k in range(16):
                E_("gpsimd", lambda e, k=k: e.dma_start(
                    out=sY_d[k * 128:(k + 1) * 128, :], in_=zeroK[:, :]),
                   inc=("dG", 16))

            layernorm(B_fm, None, 1e-6, rms_only=True, apply=False)
            rs_slab = C[:, 2048:4096]
            bcast_tm_var(sv("a"), rs_slab)
            rec.join("vector", "tensor")
            rec.wait("vector", "dA", rspill_at)
            for hc in range(HC):
                E_("vector", lambda e, hc=hc: e.tensor_tensor(
                    A_fm[:, hc, :], B_fm[:, hc, :], rs_slab, ALU.mult))
            rec.edge("vector")

            phase()
            fm_to_tm_spill(B_fm, xtm_spill, (0, 1))
            phase()
            fm_to_tm_spill(A_fm, xr_spill, (2, 3))
            phase()

            phase()
            if "pos" not in SEC:
                return
            for tt in range(ST):
                bank = tt % 2
                rec.join("tensor", "vector")
                tr(psb[bank][:, 0:8], D[0:8, tt * 128:(tt + 1) * 128])
                rec.edge("tensor")
                rec.join("vector", "tensor")
                E_("vector", lambda e, bank=bank: e.tensor_copy(sv("ptm", 8),
                                                                psb[bank][:, 0:8]))
                rec.edge("vector")
                for k, ohf in ((1, sv_oh1), (2, sv_oh2)):
                    E_("vector", lambda e, ohf=ohf, tt=tt: e.tensor_tensor(
                        sv("t8", 8), sv("ptm", 8), ohf(tt), ALU.mult))
                    rec.edge("vector")
                    E_("vector", lambda e, k=k, tt=tt: e.reduce_sum(
                        svc("pos%d" % k, tt), sv("t8", 8), axis=AX.X))
                    rec.edge("vector")
                    ev = svc("e1", tt) if k == 1 else svc("e2", tt)
                    E_("vector", lambda e, ev=ev, k=k, tt=tt: e.scalar_tensor_tensor(
                        sv("slot", 8)[:, 0:1], ev, float(S), svc("pos%d" % k, tt),
                        ALU.mult, ALU.add))
                    rec.edge("vector")
                    E_("vector", lambda e: e.tensor_copy(idxt[:, 0:1],
                                                         sv("slot", 8)[:, 0:1]))
                    rec.edge("vector")
                    E_("vector", lambda e, tt=tt: e.tensor_copy(
                        sv("pv", 8)[:, 0:1], svc("sidxf", tt)))
                    E_("vector", lambda e, k=k, tt=tt: e.tensor_copy(
                        sv("pv", 8)[:, 1:2], svc("w%d" % k, tt)))
                    rec.edge("vector")
                    rec.join("gpsimd", "vector")
                    rec.wait("gpsimd", "dG")
                    E_("gpsimd", lambda e: e.indirect_dma_start(
                        pairs_d, IndirectOffsetOnAxis(idxt[:, 0:1], axis=0),
                        sv("pv", 8)[:, 0:2], None), inc=("dG", 16))
                    rec.wait("vector", "dG")

            if "experts" not in SEC:
                return
            phase()
            Xe = B[:, 0:6144].rearrange("p (c r) -> p c r", c=HC)
            Ytm = B[:, 8192:9216]
            gbuf = B[:, 9216:11264]
            wgv = lambda b: C[:, b * 4096:b * 4096 + 2048].rearrange(
                "p (c f) -> p c f", c=HC)
            wuv = lambda b: C[:, b * 4096 + 2048:b * 4096 + 4096].rearrange(
                "p (c f) -> p c f", c=HC)
            wdv = lambda b: D[:, b * 2048:b * 2048 + 2048].rearrange(
                "p (f h) -> p f h", f=2)
            scat_prev = 0
            for ex in range(E):
                rec.join("sync", "vector")
                E_("sync", lambda e, ex=ex: e.dma_start(
                    out=wvec[:, 8:16], in_=normw_in[ex]), inc=("dW", 16))
                nw_at = rec.c("dW")
                E_("sync", lambda e, ex=ex: e.dma_start(
                    out=sv("pairv", 16)[:, 0:12].rearrange("p (b two) -> p b two",
                                                           two=2),
                    in_=pairs_d[ex * S:ex * S + PAD, :].rearrange(
                        "(b p) two -> p b two", p=128)), inc=("dW", 16))
                pv_at = rec.c("dW")
                rec.wait("vector", "dW", pv_at)
                rec.wait("vector", "dG")
                for rt in range(PAD // 128):
                    E_("vector", lambda e, rt=rt: e.tensor_copy(
                        idxt[:, 8 + rt:9 + rt], sv("pairv", 16)[:, 2 * rt:2 * rt + 1]))
                rec.edge("vector")
                idx_at = rec.c("e_vector")
                for rt in range(PAD // 128):
                    gb = gbuf[:, (rt % 2) * 1024:(rt % 2) * 1024 + 1024]
                    rec.join("gpsimd", "tensor")
                    rec.wait("gpsimd", "e_vector", idx_at)
                    E_("gpsimd", lambda e, gb=gb, rt=rt: e.indirect_dma_start(
                        gb, None, xr_spill,
                        IndirectOffsetOnAxis(idxt[:, 8 + rt:9 + rt], axis=0)),
                       inc=("dG", 16))
                    g_at = rec.c("dG")
                    rec.wait("tensor", "dG", g_at)
                    rec.join("tensor", "vector")
                    for hc in range(HC):
                        bank = 4 + (hc % 2)
                        tr(psb[bank][:, (hc // 2 % 2) * 128:(hc // 2 % 2) * 128 + 128],
                           gb[:, hc * 128:(hc + 1) * 128])
                        if hc % 4 == 3:
                            rec.edge("tensor")
                            rec.join("vector", "tensor")
                            rec.wait("vector", "dW", nw_at)
                            for hh in range(hc - 3, hc + 1):
                                bank2 = 4 + (hh % 2)
                                E_("vector", lambda e, hh=hh, bank2=bank2, rt=rt, ex=ex:
                                   e.tensor_scalar_mul(
                                       Xe[:, hh, rt * 128:(rt + 1) * 128],
                                       psb[bank2][:, (hh // 2 % 2) * 128:
                                                  (hh // 2 % 2) * 128 + 128],
                                       wvec[:, 8 + hh:9 + hh]))
                            rec.edge("vector")
                            rec.join("tensor", "vector")
                for rc in range(3):
                    r0 = rc * RC2
                    nrt = 2

                    def load_fc(fc):
                        bb = fc % 2
                        rec.wait("sync", "e_tensor")
                        E_("sync", lambda e, ex=ex, fc=fc, bb=bb: e.dma_start(
                            out=wgv(bb), in_=wg_in[ex][:, fc * 256:(fc + 1) * 256]
                            .rearrange("(c p) f -> p c f", p=128)), inc=("dWg", 16))
                        E_("sync", lambda e, ex=ex, fc=fc, bb=bb: e.dma_start(
                            out=wuv(bb), in_=wu_in[ex][:, fc * 256:(fc + 1) * 256]
                            .rearrange("(c p) f -> p c f", p=128)), inc=("dWu", 16))
                        E_("sync", lambda e, ex=ex, fc=fc, bb=bb: e.dma_start(
                            out=wdv(bb), in_=wd_in[ex][fc * 256:(fc + 1) * 256, :]
                            .rearrange("(f p) h -> p f h", p=128)), inc=("dWd", 16))
                        wload_at[bb] = (rec.c("dWg"), rec.c("dWu"), rec.c("dWd"))

                    wload_at = [None, None]
                    load_fc(0)
                    load_fc(1)
                    gmark = [0] * 16
                    umark = [0] * 16
                    gumark = [0] * 16
                    wdmark = [0] * 16

                    def gbank(fgl):
                        return 4 + 2 * (fgl % 2)

                    def ubank(fgl):
                        return 5 + 2 * (fgl % 2)

                    def emit_wd(fgl, bb):
                        rec.wait("tensor", "e_vector", gumark[fgl])
                        for rl in range(nrt):
                            for hh in range(2):
                                mm(psb[rl * 2 + hh][:, :],
                                   sv_gu(fgl % 2)[:, rl * 128:(rl + 1) * 128],
                                   wdv(bb)[:, fgl % 2, hh * 512:(hh + 1) * 512],
                                   start=(fgl == 0), stop=(fgl == 15))
                        rec.edge("tensor")
                        wdmark[fgl] = rec.c("e_tensor")

                    for fgl in range(16):
                        fc, ft = fgl // 2, fgl % 2
                        bb = fc % 2
                        if ft == 0:
                            rec.wait("tensor", "dWg", wload_at[bb][0])
                            rec.wait("tensor", "dWu", wload_at[bb][1])
                            rec.wait("tensor", "dWd", wload_at[bb][2])
                        if fgl >= 2:
                            rec.wait("tensor", "e_scalar", gmark[fgl - 2])
                        for hc in range(HC):
                            mm(psb[gbank(fgl)][:, 0:RC2],
                               wgv(bb)[:, hc, ft * 128:(ft + 1) * 128],
                               Xe[:, hc, r0:r0 + RC2],
                               start=(hc == 0), stop=(hc == HC - 1))
                        rec.edge("tensor")
                        gmm = rec.c("e_tensor")
                        if fgl >= 2:
                            rec.wait("tensor", "e_vector", gumark[fgl - 2])
                        for hc in range(HC):
                            mm(psb[ubank(fgl)][:, 0:RC2],
                               wuv(bb)[:, hc, ft * 128:(ft + 1) * 128],
                               Xe[:, hc, r0:r0 + RC2],
                               start=(hc == 0), stop=(hc == HC - 1))
                        rec.edge("tensor")
                        umm = rec.c("e_tensor")
                        rec.wait("scalar", "e_tensor", gmm)
                        if fgl >= 2:
                            rec.wait("scalar", "e_vector", gumark[fgl - 2])
                        E_("scalar", lambda e, fgl=fgl: e.activation(
                            sv_gg(fgl % 2), psb[gbank(fgl)][:, 0:RC2], AF.Gelu))
                        rec.edge("scalar")
                        gmark[fgl] = rec.c("e_scalar")
                        rec.wait("vector", "e_tensor", umm)
                        rec.wait("vector", "e_scalar", gmark[fgl])
                        if fgl >= 2:
                            rec.wait("vector", "e_tensor", wdmark[fgl - 2])
                        E_("vector", lambda e, fgl=fgl: e.tensor_tensor(
                            sv_gu(fgl % 2), sv_gg(fgl % 2), psb[ubank(fgl)][:, 0:RC2],
                            ALU.mult))
                        rec.edge("vector")
                        gumark[fgl] = rec.c("e_vector")
                        if fgl >= 1:
                            emit_wd(fgl - 1, (fgl - 1) // 2 % 2)
                            if (fgl - 1) % 2 == 1 and (fgl - 1) // 2 + 2 < 8:
                                load_fc((fgl - 1) // 2 + 2)
                    emit_wd(15, 1)
                    for rl in range(2):
                        rt = rc * 2 + rl
                        rec.join("vector", "tensor")
                        rec.wait("vector", "dG")
                        for hh in range(2):
                            E_("vector", lambda e, rl=rl, hh=hh, rt=rt:
                               e.tensor_scalar_mul(
                                   Ytm[:, hh * 512:(hh + 1) * 512],
                                   psb[rl * 2 + hh][:, :],
                                   sv("pairv", 16)[:, 2 * rt + 1:2 * rt + 2]))
                        rec.edge("vector")
                        rec.join("gpsimd", "vector")
                        if scat_prev:
                            rec.wait("gpsimd", "dG", scat_prev)
                        E_("gpsimd", lambda e, rt=rt: e.indirect_dma_start(
                            sY_d, IndirectOffsetOnAxis(idxt[:, 8 + rt:9 + rt],
                                                       axis=0),
                            Ytm[:, :], None, compute_op=ALU.add), inc=("dG", 16))
                        rec.join("tensor", "vector")
                scat_prev = rec.c("dG")

            rec.wait("gpsimd", rs)
            rec.wait("gpsimd", ls)
            rec.wait("vector", rs)
            E_("vector", lambda e, tb=tb: e.tensor_tensor(sv("mtbl", 64), tb[:, :],
                                                          maskt[:, :], ALU.mult))
            rec.edge("vector")
            E_("vector", lambda e: e.reduce_sum(svc("offsE", 0, 8), sv("mtbl", 64),
                                                axis=AX.X))
            rec.edge("vector")
            rec.join("tensor", "vector")
            tr(psb[6][:1, 0:128], sv("offsE", 8)[:, 0:1])
            rec.edge("tensor")
            rec.join("vector", "tensor")
            E_("vector", lambda e: e.tensor_copy(stage16[0:1, 0:128],
                                                 psb[6][:1, 0:128]))
            rec.edge("vector")
            rec.join("tensor", "vector")
            mm(psb[7][:, 0:8], onesr[:1, :], stage16[0:1, 0:8], start=True, stop=True)
            rec.edge("tensor")
            rec.join("vector", "tensor")
            E_("vector", lambda e: e.tensor_copy(sv("offsB", 8), psb[7][:, 0:8]))
            rec.edge("vector")
            for tt in range(ST):
                for k, ohf in ((1, sv_oh1), (2, sv_oh2)):
                    E_("vector", lambda e, ohf=ohf, tt=tt: e.tensor_tensor(
                        sv("t8", 8), sv("offsB", 8), ohf(tt), ALU.mult))
                    rec.edge("vector")
                    E_("vector", lambda e, k=k: e.reduce_sum(
                        svc("od%d" % k, 0, 8), sv("t8", 8), axis=AX.X))
                    rec.edge("vector")
                    E_("vector", lambda e, k=k, tt=tt: e.tensor_tensor(
                        svc("gp%d" % k, 0), svc("pos%d" % k, tt),
                        svc("od%d" % k, 0, 8), ALU.add))
                    rec.edge("vector")
                    E_("vector", lambda e, k=k: e.tensor_single_scalar(
                        svc("keep%d" % k, 0), svc("gp%d" % k, 0), float(CAP),
                        ALU.is_lt))
                    rec.edge("vector")
                E_("vector", lambda e, tt=tt: e.tensor_tensor(
                    svc("kk", tt), svc("keep1", 0), svc("keep2", 0), ALU.mult))
                rec.edge("vector")
                E_("vector", lambda e, tt=tt: e.tensor_tensor(
                    svc("wsum", 0), svc("w1", tt), svc("w2", tt), ALU.add))
                rec.edge("vector")
                E_("vector", lambda e, tt=tt: e.tensor_tensor(
                    svc("t1", 0), svc("kk", tt), svc("wsum", 0), ALU.mult))
                rec.edge("vector")
                E_("vector", lambda e, tt=tt: e.tensor_scalar_add(
                    svc("Cc", tt), svc("t1", 0), 1.0))
                rec.edge("vector")
                E_("vector", lambda e, tt=tt: e.tensor_scalar(
                    svc("s1", tt), svc("kk", tt), -1.0, 1.0,
                    op0=ALU.mult, op1=ALU.add))
                rec.edge("vector")

            if "fallback" not in SEC:
                return
            phase()
            fwgv = C[:, 0:4096].rearrange("p (c f) -> p c f", c=HC)
            fwuv = C[:, 4096:8192].rearrange("p (c f) -> p c f", c=HC)
            fwdv = D[:, 0:4096].rearrange("p (f h) -> p f h", f=4)
            rec.join("sync", "tensor", "vector", "scalar")
            E_("sync", lambda e: e.dma_start(
                out=fwgv, in_=fwg_in.rearrange("(c p) f -> p c f", p=128)),
               inc=("dW", 16))
            E_("sync", lambda e: e.dma_start(
                out=fwuv, in_=fwu_in.rearrange("(c p) f -> p c f", p=128)),
               inc=("dW", 16))
            E_("sync", lambda e: e.dma_start(
                out=fwdv, in_=fwd_in.rearrange("(f p) h -> p f h", p=128)),
               inc=("dW", 16))
            E_("sync", lambda e: e.dma_start(
                out=wvec[:, 16:24], in_=fnorm_in), inc=("dW", 16))
            rec.wait("vector", "dW")
            for hc in range(HC):
                E_("vector", lambda e, hc=hc: e.tensor_scalar_mul(
                    fwgv[:, hc, :], fwgv[:, hc, :], wvec[:, 16 + hc:17 + hc]))
                E_("vector", lambda e, hc=hc: e.tensor_scalar_mul(
                    fwuv[:, hc, :], fwuv[:, hc, :], wvec[:, 16 + hc:17 + hc]))
            rec.edge("vector")
            GU4 = B[:, 0:2048].rearrange("p (f s) -> p f s", f=4)
            xs_buf = B[:, 8192:12288]
            rstr = B[:, 2048:4096]
            fb_sb = B[:, 12288:16384].rearrange("p (c s) -> p c s", c=HC)[:, :, 0:512]
            for qc in range(4):
                rec.join("tensor", "vector", "scalar")
                for ft4 in range(4):
                    bank = ft4 % 4
                    for hc in range(HC):
                        mm(psb[bank][:, :], fwgv[:, hc, ft4 * 128:(ft4 + 1) * 128],
                           A_fm[:, hc, qc * 512:(qc + 1) * 512],
                           start=(hc == 0), stop=(hc == HC - 1))
                    rec.edge("tensor")
                    rec.join("scalar", "tensor")
                    E_("scalar", lambda e, ft4=ft4, bank=bank: e.activation(
                        GU4[:, ft4, :], psb[bank][:, :], AF.Gelu))
                    rec.edge("scalar")
                rec.join("tensor", "scalar")
                for ft4 in range(4):
                    bank = ft4 % 4
                    for hc in range(HC):
                        mm(psb[bank][:, :], fwuv[:, hc, ft4 * 128:(ft4 + 1) * 128],
                           A_fm[:, hc, qc * 512:(qc + 1) * 512],
                           start=(hc == 0), stop=(hc == HC - 1))
                    rec.edge("tensor")
                    rec.join("vector", "tensor")
                    E_("vector", lambda e, ft4=ft4, bank=bank: e.tensor_tensor(
                        GU4[:, ft4, :], GU4[:, ft4, :], psb[bank][:, :], ALU.mult))
                    rec.edge("vector")
                rec.join("tensor", "vector")
                for half in range(2):
                    for hh in range(4):
                        hc = half * 4 + hh
                        bank = 4 + hh
                        for ft4 in range(4):
                            mm(psb[bank][:, :], fwdv[:, ft4, hc * 128:(hc + 1) * 128],
                               GU4[:, ft4, :], start=(ft4 == 0), stop=(ft4 == 3))
                    rec.edge("tensor")
                    rec.join("vector", "tensor")
                    for hh in range(4):
                        hc = half * 4 + hh
                        E_("vector", lambda e, hc=hc, hh=hh: e.tensor_copy(
                            fb_sb[:, hc, :], psb[4 + hh][:, :]))
                    rec.edge("vector")
                    rec.join("tensor", "vector")
                for j in range(4):
                    tt = qc * 4 + j
                    xb = xs_buf[:, (j % 2) * 2048:(j % 2) * 2048 + 1024]
                    syb = xs_buf[:, (j % 2) * 2048 + 1024:(j % 2) * 2048 + 2048]
                    rec.join("scalar", "vector", "gpsimd")
                    rec.wait("scalar", "dG")
                    E_("scalar", lambda e, xb=xb, tt=tt: e.dma_start(
                        out=xb, in_=xtm_spill.rearrange("(p b) h -> p b h",
                                                        p=128)[:, tt, :]),
                       inc=("dA", 16))
                    E_("scalar", lambda e, syb=syb, tt=tt: e.dma_start(
                        out=syb, in_=sY_d.rearrange("(p b) h -> p b h",
                                                    p=128)[:, tt, :]),
                       inc=("dA", 16))
                    xld = rec.c("dA")
                    rec.join("tensor", "vector")
                    for hc in range(HC):
                        bank = hc % 2
                        tr(psb[bank][:, (hc // 2 % 2) * 128:(hc // 2 % 2) * 128 + 128],
                           fb_sb[:, hc, j * 128:(j + 1) * 128])
                        if hc % 4 == 3:
                            rec.edge("tensor")
                            rec.join("vector", "tensor")
                            for hh in range(hc - 3, hc + 1):
                                E_("vector", lambda e, hh=hh: e.tensor_copy(
                                    sv_fbtm()[:, hh * 128:(hh + 1) * 128],
                                    psb[hh % 2][:, (hh // 2 % 2) * 128:
                                                (hh // 2 % 2) * 128 + 128]))
                            rec.edge("vector")
                            rec.join("tensor", "vector")
                    rec.wait("vector", "dA", xld)
                    rec.wait("vector", "dG")
                    E_("vector", lambda e, xb=xb, tt=tt: e.tensor_scalar_mul(
                        sv_cmb(), xb, svc("Cc", tt)))
                    rec.edge("vector")
                    E_("vector", lambda e, syb=syb, tt=tt: e.tensor_scalar_mul(
                        sv_cm2(), syb, svc("kk", tt)))
                    rec.edge("vector")
                    E_("vector", lambda e: e.tensor_tensor(
                        sv_cmb(), sv_cmb(), sv_cm2(), ALU.add))
                    rec.edge("vector")
                    E_("vector", lambda e, tt=tt: e.tensor_scalar_mul(
                        sv_cm2(), sv_fbtm(), svc("s1", tt)))
                    rec.edge("vector")
                    E_("vector", lambda e: e.tensor_tensor(
                        sv_cmb(), sv_cmb(), sv_cm2(), ALU.add))
                    rec.edge("vector")
                    rec.join("sync", "vector")
                    rsb = rstr[:, (j % 2) * 1024:(j % 2) * 1024 + 1024]
                    E_("sync", lambda e, rsb=rsb, tt=tt: e.dma_start(
                        out=rsb.rearrange("p (c s) -> p c s", c=HC),
                        in_=r_spill[:, :, tt * 128:(tt + 1) * 128]), inc=("dV", 16))
                    rld = rec.c("dV")
                    rec.join("tensor", "vector")
                    for hc in range(HC):
                        bank = 2 + (hc % 2)
                        tr(psb[bank][:, (hc // 2 % 2) * 128:(hc // 2 % 2) * 128 + 128],
                           sv_cmb()[:, hc * 128:(hc + 1) * 128])
                        if hc % 4 == 3:
                            rec.edge("tensor")
                            rec.join("vector", "tensor")
                            rec.wait("vector", "dV", rld)
                            for hh in range(hc - 3, hc + 1):
                                E_("vector", lambda e, hh=hh, rsb=rsb, tt=tt:
                                   e.tensor_tensor(
                                       A_fm[:, hh, tt * 128:(tt + 1) * 128],
                                       psb[2 + hh % 2][:, (hh // 2 % 2) * 128:
                                                       (hh // 2 % 2) * 128 + 128],
                                       rsb.rearrange("p (c s) -> p c s",
                                                     c=HC)[:, hh, :], ALU.add))
                            rec.edge("vector")
                            rec.join("tensor", "vector")

        def attn_b():
            rec.join("scalar", "vector", "tensor")
            E_("scalar", lambda e: e.dma_start(out=r_spill, in_=A_fm), inc=("dA", 16))
            rspill_at = rec.c("dA")
            layernorm(A_fm, B_fm, 1e-5)
            rec.join("sync", "vector")
            load_bias_cols(bqkv_b, 32, 24)
            rec.wait("vector", "dA", rspill_at)
            phase()
            qkv_fm_phase(wqkvT_b, 2 * H, A_fm, None, bias_src=True, bias_cols=48)
            phase()
            fm_to_tm_spill(A_fm, None, (0, 1), tile_ap=lambda tt: v_spill[:, tt, :])
            phase()
            qkv_fm_phase(wqkvT_b, 0, None, q_spill, bias_src=True, bias_cols=32)
            phase()
            qkv_fm_phase(wqkvT_b, H, A_fm, None, bias_src=True, bias_cols=40)
            phase()

            PTa = B[:, 0:8192].rearrange("p (k q) -> p k q", k=ST)
            PTb = B[:, 8192:16384].rearrange("p (k q) -> p k q", k=ST)
            qbuf = D[:, 0:8192]
            vstream = C[:, 0:4096]
            sco = [C[:, 4096:6144], C[:, 6144:8192]]
            qload_at = [0, 0]
            for qb in range(2):
                qbv = qbuf[:, (qb % 2) * 4096:(qb % 2) * 4096 + 4096]
                rec.join("sync", "tensor")
                E_("sync", lambda e, qb=qb, qbv=qbv: e.dma_start(
                    out=qbv.rearrange("p (c s) -> p c s", c=HC),
                    in_=q_spill[:, :, qb * 512:(qb + 1) * 512]), inc=("dQ", 16))
                qload_at[qb % 2] = rec.c("dQ")
            sc_evac = [0, 0, 0, 0]
            pt_evac = [0] * 8
            pt_done = [0, 0]
            pv_evac = [0, 0]
            ao_dma = [0]
            for qb in range(4):
                qbv = qbuf[:, (qb % 2) * 4096:(qb % 2) * 4096 + 4096].rearrange(
                    "p (c s) -> p c s", c=HC)
                rec.wait("tensor", "dQ", qload_at[qb % 2])
                vload_at = [0, 0]
                rec.join("sync", "tensor")
                for hc in range(2):
                    vsb = vstream[:, (hc % 2) * 2048:(hc % 2) * 2048 + 2048]
                    E_("sync", lambda e, vsb=vsb, hc=hc: e.dma_start(
                        out=vsb.rearrange("p (k f) -> p k f", k=ST),
                        in_=v_spill[:, :, hc * 128:(hc + 1) * 128]), inc=("dV", 16))
                    vload_at[hc % 2] = rec.c("dV")
                for hc in range(HC):
                    def scoresb_qt(qt, hc=hc, qbv=qbv):
                        for h2 in range(2):
                            p0 = h2 * 64
                            for kc in range(4):
                                bank = h2 * 2 + kc % 2
                                if sc_evac[bank]:
                                    rec.wait("tensor", "e_scalar", sc_evac[bank])
                                mm2 = lambda e, hc=hc, qt=qt, kc=kc, p0=p0, bank=bank, \
                                    qbv=qbv: \
                                    e.matmul(psb[bank][:, :],
                                             qbv[p0:p0 + 64, hc, qt * 128:(qt + 1) * 128],
                                             A_fm[p0:p0 + 64, hc, kc * 512:(kc + 1) * 512],
                                             start=True, stop=True,
                                             tile_position=(p0, 0))
                                rec.op("tensor", mm2)
                                rec.edge("tensor")
                                rec.join("scalar", "tensor")
                                rec.wait("scalar", "dA", ao_dma[0])
                                if pt_done[h2]:
                                    rec.wait("scalar", "e_tensor", pt_done[h2])
                                E_("scalar", lambda e, h2=h2, qt=qt, kc=kc, bank=bank:
                                   e.copy(sco[h2][:, kc * 512:(kc + 1) * 512],
                                          psb[bank][:, :]))
                                rec.edge("scalar")
                                sc_evac[bank] = rec.c("e_scalar")

                    def smpt_qt(qt, hc=hc):
                        tsmul_at = [0, 0]
                        for h2 in range(2):
                            sl = sco[h2]
                            rec.join("vector", "scalar")
                            E_("vector", lambda e, sl=sl, h2=h2: e.reduce_max(
                                svc("negmax", h2), sl, axis=AX.X, negate=True))
                            rec.edge("vector")
                            E_("vector", lambda e, h2=h2: e.tensor_scalar_mul(
                                svc("negmax", h2), svc("negmax", h2), 0.125))
                            rec.edge("vector")
                            rec.join("scalar", "vector")
                            E_("scalar", lambda e, sl=sl, h2=h2: e.activation(
                                sl, sl, AF.Exp, bias=svc("negmax", h2), scale=0.125,
                                accum_out=svc("ssum", h2)))
                            rec.edge("scalar")
                            rec.join("vector", "scalar")
                            E_("vector", lambda e, h2=h2: e.reciprocal(
                                svc("srec", h2), svc("ssum", h2)))
                            rec.edge("vector")
                            E_("vector", lambda e, sl=sl, h2=h2: e.tensor_scalar_mul(
                                sl, sl, svc("srec", h2)))
                            rec.edge("vector")
                            tsmul_at[h2] = rec.c("e_vector")
                        for h2 in range(2):
                            sl = sco[h2]
                            rec.wait("tensor", "e_vector", tsmul_at[h2])
                            PT = PTa if h2 == 0 else PTb
                            for kt4 in range(4):
                                bank = 5 + (kt4 % 2)
                                if pt_evac[bank]:
                                    rec.wait("tensor", "e_vector", pt_evac[bank])
                                for j in range(4):
                                    kt = kt4 * 4 + j
                                    tr(psb[bank][:, j * 128:(j + 1) * 128],
                                       sl[:, kt * 128:(kt + 1) * 128])
                                rec.edge("tensor")
                                rec.join("vector", "tensor")
                                E_("vector", lambda e, PT=PT, kt4=kt4, qt=qt, bank=bank:
                                   e.tensor_copy(
                                       PT[:, kt4 * 4:kt4 * 4 + 4,
                                          qt * 128:(qt + 1) * 128],
                                       psb[bank][:, :].rearrange(
                                           "p (a q) -> p a q", a=4)))
                                rec.edge("vector")
                                pt_evac[bank] = rec.c("e_vector")
                            pt_done[h2] = rec.c("e_tensor")

                    for qt in range(4):
                        scoresb_qt(qt)
                        smpt_qt(qt)
                    rec.wait("tensor", "dV", vload_at[hc % 2])
                    rec.join("tensor", "vector")
                    if pv_evac[0]:
                        rec.wait("tensor", "e_vector", pv_evac[0])
                    vsb = vstream[:, (hc % 2) * 2048:(hc % 2) * 2048 + 2048].rearrange(
                        "p (k f) -> p k f", k=ST)
                    for kt in range(ST):
                        for h2 in range(2):
                            mm2 = lambda e, kt=kt, h2=h2, vsb=vsb, hc=hc: \
                                e.matmul((psb[4] if h2 == 0 else psb[7])[
                                             h2 * 64:h2 * 64 + 64, :],
                                         vsb[:, kt, h2 * 64:h2 * 64 + 64],
                                         (PTa if h2 == 0 else PTb)[:, kt, :],
                                         start=(kt == 0), stop=(kt == ST - 1),
                                         tile_position=(0, h2 * 64))
                            rec.op("tensor", mm2)
                    rec.edge("tensor")
                    rec.join("vector", "tensor")
                    if ao_dma[0]:
                        rec.wait("vector", "dA", ao_dma[0])
                    E_("vector", lambda e, hc=hc: e.tensor_copy(
                        sco[0][0:64, 0:512], psb[4][0:64, :]))
                    E_("vector", lambda e, hc=hc: e.tensor_copy(
                        sco[0][64:128, 0:512], psb[7][64:128, :]))
                    rec.edge("vector")
                    pv_evac[0] = rec.c("e_vector")
                    rec.join("scalar", "vector")
                    E_("scalar", lambda e, hc=hc, qb=qb: e.dma_start(
                        out=a_spill[:, hc, qb * 512:(qb + 1) * 512],
                        in_=sco[0][:, 0:512]), inc=("dA", 16))
                    ao_dma[0] = rec.c("dA")
                    if hc + 2 < HC:
                        hc2 = hc + 2
                        rec.wait("sync", "e_tensor")
                        vsb2 = vstream[:, (hc2 % 2) * 2048:(hc2 % 2) * 2048 + 2048]
                        E_("sync", lambda e, vsb2=vsb2, hc2=hc2: e.dma_start(
                            out=vsb2.rearrange("p (k f) -> p k f", k=ST),
                            in_=v_spill[:, :, hc2 * 128:(hc2 + 1) * 128]),
                           inc=("dV", 16))
                        vload_at[hc2 % 2] = rec.c("dV")
                if qb + 2 < 4:
                    qb2 = qb + 2
                    qbv2 = qbuf[:, (qb2 % 2) * 4096:(qb2 % 2) * 4096 + 4096]
                    rec.wait("sync", "e_tensor")
                    E_("sync", lambda e, qb2=qb2, qbv2=qbv2: e.dma_start(
                        out=qbv2.rearrange("p (c s) -> p c s", c=HC),
                        in_=q_spill[:, :, qb2 * 512:(qb2 + 1) * 512]), inc=("dQ", 16))
                    qload_at[qb2 % 2] = rec.c("dQ")

            phase()
            wp = D[:, 0:8192].rearrange("p (c f) -> p c f", c=HC)
            for jc in range(HC):
                E_("sync", lambda e, jc=jc: e.dma_start(
                    out=wp[:, jc, :], in_=woutT_b[jc * 128:(jc + 1) * 128, :]),
                   inc=("dW", 16))
            load_bias_cols(bout_b, 0)
            rec.wait("vector", "dW")
            astream = C[:, 0:4096]
            rstream = C[:, 4096:8192]
            pj_evac = [0, 0, 0, 0]
            for blk in range(HC):
                asb = astream[:, (blk % 2) * 2048:(blk % 2) * 2048 + 2048]
                rsb = rstream[:, (blk % 2) * 2048:(blk % 2) * 2048 + 2048]
                if blk >= 2:
                    rec.wait("sync", "e_tensor")
                    rec.wait("sync", "e_vector")
                E_("sync", lambda e, asb=asb, blk=blk: e.dma_start(
                    out=asb.rearrange("p (c i) -> p c i", c=HC),
                    in_=a_spill[:, :, blk * 256:(blk + 1) * 256]), inc=("dQ", 16))
                E_("sync", lambda e, rsb=rsb, blk=blk: e.dma_start(
                    out=rsb.rearrange("p (c i) -> p c i", c=HC),
                    in_=r_spill[:, :, blk * 256:(blk + 1) * 256]), inc=("dV", 16))
                aload = rec.c("dQ")
                rload2 = rec.c("dV")
                rec.wait("tensor", "dQ", aload)
                rec.wait("vector", "dV", rload2)
                asv = asb.rearrange("p (c i) -> p c i", c=HC)
                rsv = rsb.rearrange("p (c i) -> p c i", c=HC)
                for ec in range(HC):
                    bank = 4 + (ec % 4)
                    if pj_evac[ec % 4]:
                        rec.wait("tensor", "e_vector", pj_evac[ec % 4])
                    for jc in range(HC):
                        mm(psb[bank][:, 0:256], wp[:, jc, ec * 128:(ec + 1) * 128],
                           asv[:, jc, :], start=(jc == 0), stop=(jc == HC - 1))
                    rec.edge("tensor")
                    rec.join("vector", "tensor")
                    E_("vector", lambda e, ec=ec, bank=bank, rsv=rsv, blk=blk:
                       e.scalar_tensor_tensor(
                           A_fm[:, ec, blk * 256:(blk + 1) * 256], psb[bank][:, 0:256],
                           wvec[:, ec:ec + 1], rsv[:, ec, :], ALU.add, ALU.add))
                    rec.edge("vector")
                    pj_evac[ec % 4] = rec.c("e_vector")

        def sv_gg(p=0):
            return D[:, 4096 + p * 512:4096 + p * 512 + RC2]

        def sv_gu(p=0):
            return D[:, 4352 + p * 512:4352 + p * 512 + RC2]

        def sv_fbtm():
            return D[:, 5120:6144]

        def sv_cmb():
            return D[:, 6144:7168]

        def sv_cm2():
            return D[:, 7168:8192]

        def sv_oh1(tt):
            return sv("oh1_%d" % tt, 8)

        def sv_oh2(tt):
            return sv("oh2_%d" % tt, 8)

        rankreg = []
        rec.wait("gpsimd", "dW")

        def _loadrank(e):
            rankreg.append(e.value_load(idxt[:1, 62:63], min_val=0, max_val=7))
            return e.nop()
        E_("gpsimd", _loadrank)

        if dev >= 2:
            attn_a()
        if dev >= 3:
            ffn(0)
        if dev >= 4:
            attn_b()
        if dev >= 5:
            ffn(1)

        rec.wait("sync", "dW")
        rec.wait("sync", "dA")
        rec.join("sync", "vector", "tensor", "scalar", "gpsimd")
        E_("sync", lambda e: e.dma_start(out=y_fm, in_=A_fm), inc=("dS", 16))
        rec.wait("gpsimd", "dS")
        rec.wait("sync", "dS")

        for ename in ENGS:
            def mk(ename):
                def _(h):
                    rec.replay(ename, h, sems)
                return _
            getattr(block, ename)(mk(ename))

    nc.compile()
    return nc


def kernel(L0_attn_a_x, L0_attn_a_w_qkv, L0_attn_a_w_proj, L0_attn_a_b_proj,
           L0_attn_b_W_qkv, L0_attn_b_b_qkv, L0_attn_b_W_out, L0_attn_b_b_out,
           L0_ffn_Wr, L0_ffn_norm_w, L0_ffn_Wg, L0_ffn_Wu, L0_ffn_Wd,
           L0_ffn_f_norm, L0_ffn_fWg, L0_ffn_fWu, L0_ffn_fWd):
    x = np.ascontiguousarray(np.asarray(L0_attn_a_x, dtype=np.float32))
    shared = {
        "ident": np.eye(128, dtype=np.float32),
        "wqkvT_a": np.ascontiguousarray(np.asarray(L0_attn_a_w_qkv, np.float32).T),
        "wprojT_a": np.ascontiguousarray(np.asarray(L0_attn_a_w_proj, np.float32).T),
        "bproj_a": np.ascontiguousarray(np.asarray(L0_attn_a_b_proj, np.float32).reshape(HC, 128).T),
        "wqkvT_b": np.ascontiguousarray(np.asarray(L0_attn_b_W_qkv, np.float32).T),
        "bqkv_b": np.ascontiguousarray(np.asarray(L0_attn_b_b_qkv, np.float32).reshape(3 * HC, 128).T),
        "woutT_b": np.ascontiguousarray(np.asarray(L0_attn_b_W_out, np.float32).T),
        "bout_b": np.ascontiguousarray(np.asarray(L0_attn_b_b_out, np.float32).reshape(HC, 128).T),
        "wr": np.ascontiguousarray(np.asarray(L0_ffn_Wr, np.float32)),
        "normw": np.ascontiguousarray(np.asarray(L0_ffn_norm_w, np.float32).reshape(E, HC, 128).transpose(0, 2, 1)),
        "wg": np.ascontiguousarray(np.asarray(L0_ffn_Wg, np.float32)),
        "wu": np.ascontiguousarray(np.asarray(L0_ffn_Wu, np.float32)),
        "wd": np.ascontiguousarray(np.asarray(L0_ffn_Wd, np.float32)),
        "fnorm": np.ascontiguousarray(np.asarray(L0_ffn_f_norm, np.float32).reshape(HC, 128).T),
        "fwg": np.ascontiguousarray(np.asarray(L0_ffn_fWg, np.float32)),
        "fwu": np.ascontiguousarray(np.asarray(L0_ffn_fWu, np.float32)),
        "fwd": np.ascontiguousarray(np.asarray(L0_ffn_fWd, np.float32)),
    }
    in_maps = []
    for c in range(NCORE):
        m = dict(shared)
        xt = np.ascontiguousarray(x[c].T).reshape(HC, 128, S).transpose(1, 0, 2)
        m["x_fm"] = np.ascontiguousarray(xt)
        m["rankv"] = np.array([c], dtype=np.int32)
        mk = np.zeros((NCORE, E), dtype=np.float32)
        mk[:c, :] = 1.0
        m["maskrow"] = np.broadcast_to(mk.reshape(-1), (128, 64)).copy()
        in_maps.append(m)

    nc = build_nc(globals().get("DEV_STAGE", 5))
    res = run_bass_kernel_spmd(nc, in_maps, core_ids=list(range(NCORE)))
    out = np.empty((NCORE, S, H), dtype=np.float32)
    for c in range(NCORE):
        yf = res.results[c]["y_fm"]
        out[c] = yf.transpose(1, 0, 2).reshape(H, S).T
    return out
